# version 03
# baseline (speedup 1.0000x reference)
import contextlib
import math

import numpy as np
import ml_dtypes

import concourse.bass as bass
import concourse.mybir as mybir
from concourse.bass import AP
from concourse.bass_utils import run_bass_kernel_spmd

F32 = mybir.dt.float32
BF16 = mybir.dt.bfloat16
I32 = mybir.dt.int32
AF = mybir.ActivationFunctionType
ALU = mybir.AluOpType
BF = ml_dtypes.bfloat16

NT = 4096
D = 1024
DC = 8
TB = 512
NTB = NT // TB
CHUNK = 64
HD = 128
EA, EF, FH = 8, 16, 512
MAXP = 64
EPS = 1e-5
INV2PI = 1.0 / (2.0 * math.pi)
TWOPI = 2.0 * math.pi

ENGINES = ("tensor", "vector", "scalar", "gpsimd", "sync")


class Sem:
    __slots__ = ("h", "n", "name")

    def __init__(self, h, name):
        self.h = h
        self.n = 0
        self.name = name


class Tok:
    __slots__ = ("sem", "val")

    def __init__(self, sem, val):
        self.sem = sem
        self.val = val


class Res:
    __slots__ = ("w", "readers")

    def __init__(self):
        self.w = None
        self.readers = []

    def wr_deps(self):
        return ([self.w] if self.w else []) + self.readers

    def rd_deps(self):
        return [self.w] if self.w else []

    def set_w(self, tok):
        self.w = tok
        self.readers = []

    def add_r(self, tok):
        if tok is not None:
            self.readers.append(tok)


class KB:

    def __init__(self, nc):
        self.nc = nc
        self.ops = {e: [] for e in ENGINES}
        self.stack = contextlib.ExitStack()
        self.n_inst = 0

    def sem(self, name):
        return Sem(self.stack.enter_context(self.nc.semaphore(name)), name)

    def op(self, eng, fn, waits=(), sem=None, inc=1):
        ws = [w for w in waits if w is not None]
        tok = None
        if sem is not None:
            sem.n += inc
            tok = Tok(sem, sem.n)
        self.ops[eng].append((fn, ws, sem, inc))
        self.n_inst += 1
        return tok

    def finalize(self):
        with self.nc.Block() as block:
            for ename in ENGINES:
                lst = self.ops[ename]
                if not lst:
                    continue

                def make(lst):
                    def emit(eng):
                        maxw = {}
                        for fn, ws, sem, inc in lst:
                            for tok in ws:
                                if maxw.get(id(tok.sem), -1) < tok.val:
                                    eng.wait_ge(tok.sem.h, tok.val)
                                    maxw[id(tok.sem)] = tok.val
                            inst = fn(eng)
                            if sem is not None and inst is not None:
                                inst.then_inc(sem.h, inc)
                    return emit

                getattr(block, ename)(make(lst))
        self.stack.close()


def build_program(phase, dbg=False):
    nc = bass.Bass("TRN2", target_bir_lowering=False)
    k = KB(nc)
    t = {}

    def dram(name, shape, dtype, kind):
        t[name] = nc.dram_tensor(name, shape, dtype, kind=kind).ap()
        return t[name]

    def sbuf(name, shape, dtype):
        t[name] = k.stack.enter_context(nc.sbuf_tensor(name, list(shape), dtype))
        return t[name]

    do_a = phase in ("A", "ALL")
    do_b = phase in ("B", "ALL")
    do_c = phase in ("C", "ALL")
    inner = "Internal"

    if do_a:
        dram("x", [NT, D], F32, "ExternalInput")
        for nm in ("Wk", "Wv", "Wq", "Wkp", "Wqp", "Wo"):
            dram(nm, [D, D], F32, "ExternalInput")
        for nm in ("bk", "bv", "bq", "bkp", "bqp", "phase_scale", "ln_g", "ln_b", "bo"):
            dram(nm, [D], F32, "ExternalInput")
        dram("wcacheA", [6, DC, 128, D], BF16, inner)
    dram("x1cache", [NTB, DC, 128, TB], BF16,
         "ExternalOutput" if phase == "A" else ("ExternalInput" if phase in ("B",) else inner))
    if do_b:
        dram("attn_gate_w", [D, EA], F32, "ExternalInput")
        dram("q_w", [EA, D, HD], F32, "ExternalInput")
        dram("q_b", [EA, HD], F32, "ExternalInput")
        dram("k_w", [D, HD], F32, "ExternalInput")
        dram("k_b", [HD], F32, "ExternalInput")
        dram("v_w", [D, HD], F32, "ExternalInput")
        dram("v_b", [HD], F32, "ExternalInput")
        dram("o_w", [EA, HD, D], F32, "ExternalInput")
        dram("o_b", [EA, D], F32, "ExternalInput")
        dram("rel_table", [2 * MAXP + 1, HD], F32, "ExternalInput")
        dram("ln1_w", [D], F32, "ExternalInput")
        dram("ln1_b", [D], F32, "ExternalInput")
        dram("wq_c", [EA, DC, 128, HD], BF16, inner)
        dram("Rimg", [2, 128, 640], BF16, inner)
        dram("rowbounceB", [D], F32, inner)
        dram("owc", [EA, 128, D], BF16, inner)
        dram("rows1d", [2, NT], BF16, inner)
        dram("gatc", [8, NT], BF16, inner)
    if do_b or do_c:
        kind_s = inner if phase == "ALL" else ("ExternalOutput" if phase == "B" else "ExternalInput")
        dram("scache", [NTB, DC, 128, TB], BF16, kind_s)
        dram("x2cache", [NTB, DC, 128, TB], BF16, kind_s)
    if do_c:
        dram("ff_gate_w", [D, EF], F32, "ExternalInput")
        dram("ff_w1", [EF, D, FH], F32, "ExternalInput")
        dram("ff_b1", [EF, FH], F32, "ExternalInput")
        dram("ff_w2", [EF, FH, D], F32, "ExternalInput")
        dram("ff_b2", [EF, D], F32, "ExternalInput")
        dram("ln2_w", [D], F32, "ExternalInput")
        dram("ln2_b", [D], F32, "ExternalInput")
        dram("w1_c", [EF, DC, 128, FH], BF16, inner)
        dram("w2_c", [EF, 4, 128, D], BF16, inner)
        dram("rowbounceC", [D], F32, inner)
        dram("gatfd", [EF, TB], BF16, inner)
        dram("out", [NT, D], F32, "ExternalOutput")

    sT, sV, sS, sG = k.sem("sT"), k.sem("sV"), k.sem("sS"), k.sem("sG")
    sDs, sDa, sDg = k.sem("sDs"), k.sem("sDa"), k.sem("sDg")
    esem = dict(tensor=sT, vector=sV, scalar=sS, gpsimd=sG)
    qsem = dict(sync=sDs, scalar=sDa, gpsimd=sDg)

    def dma(queue, out, in_, waits=()):
        return k.op(queue, lambda e, o=out, i=in_: e.dma_start(out=o, in_=i),
                    waits=waits, sem=qsem[queue], inc=16)

    def xpose_dma(out, in_, waits=()):
        return k.op("sync", lambda e, o=out, i=in_: e.dma_start_transpose(o, i),
                    waits=waits, sem=sDs, inc=16)

    def mm(out, lhsT, rhs, start, stop, waits=(), tok=True):
        return k.op("tensor",
                    lambda e, o=out, l=lhsT, r=rhs, st=start, sp=stop:
                    e.matmul(o, l, r, start=st, stop=sp),
                    waits=waits, sem=sT if tok else None)

    def act(out, in_, func, bias=None, scale=None, accum=None, waits=()):
        kw = {}
        if bias is not None:
            kw["bias"] = bias
        if scale is not None:
            kw["scale"] = scale
        if accum is not None:
            kw["accum_out"] = accum
        return k.op("scalar",
                    lambda e, o=out, i=in_, f=func, kw=kw: e.activation(out=o, in_=i, func=f, **kw),
                    waits=waits, sem=sS)

    def vop(name, eng="vector", waits=(), **kw):
        return k.op(eng, lambda e, kw=kw, n=name: getattr(e, n)(**kw),
                    waits=waits, sem=esem[eng])

    poolBF = k.stack.enter_context(nc.sbuf_tensor("poolBF", [128, 81920], BF16))
    poolF = k.stack.enter_context(nc.sbuf_tensor("poolF", [128, 8448], F32))
    poolI = k.stack.enter_context(nc.sbuf_tensor("poolI", [128, 512], I32))
    _pool_off = {"bf": 0, "f32": 0, "i32": 0}

    def _pool_reset():
        _pool_off["bf"] = _pool_off["f32"] = _pool_off["i32"] = 0

    def pool_sbuf(name, shape, dtype):
        import functools, operator
        n = functools.reduce(operator.mul, shape[1:], 1)
        if dtype == BF16:
            key, pool, esz = "bf", poolBF, 2
        elif dtype == F32:
            key, pool, esz = "f32", poolF, 4
        else:
            key, pool, esz = "i32", poolI, 4
        off = _pool_off[key]
        off = (off + 15) & ~15
        assert off + n <= pool.shape[1], f"pool {key} overflow at {name}: {off}+{n}"
        _pool_off[key] = off + n
        ap = pool[:, off:off + n]
        if shape[0] != 128:
            ap = pool[0:shape[0], off:off + n]
        if len(shape) == 3:
            ap = ap.rearrange("p (a b) -> p a b", a=shape[1], b=shape[2])
        elif len(shape) == 4:
            ap = ap.rearrange("p (a b c) -> p a b c", a=shape[1], b=shape[2], c=shape[3])
        t[name] = ap
        return ap

    def phase_barrier():
        toks = [Tok(x, x.n) for x in (sT, sV, sS, sG, sDs, sDa, sDg) if x.n > 0]
        for eng in ENGINES:
            k.op(eng, lambda e: None, waits=toks)

    ident = sbuf("ident", (128, 128), F32)
    ones1x128 = sbuf("ones1x128", (1, 128), BF16)
    ones1x512 = sbuf("ones1x512", (1, 512), BF16)
    onescol = sbuf("onescol", (128, 1), BF16)

    cst = []
    cst.append(k.op("gpsimd", lambda e: e.memset(ones1x128[:], 1.0), sem=sG))
    cst.append(k.op("gpsimd", lambda e: e.memset(ones1x512[:], 1.0), sem=sG))
    cst.append(k.op("gpsimd", lambda e: e.memset(onescol[:], 1.0), sem=sG))
    cvec = sbuf("cvec", (128, 2), F32)
    cst.append(k.op("gpsimd", lambda e: e.memset(cvec[:, 0:1], math.pi / 2.0), sem=sG))
    cst.append(k.op("gpsimd", lambda e: e.memset(cvec[:, 1:2], EPS), sem=sG))
    HALFPI_AP = cvec[:, 0:1]
    EPS_AP1 = cvec[0:1, 1:2]
    tio = k.op("gpsimd", lambda e: e.iota(ident[:], [[1, 128]], base=0, channel_multiplier=-1,
                                          allow_small_or_imprecise_dtypes=True), sem=sG)
    tid = vop("tensor_scalar", waits=[tio], out=ident[:], in0=ident[:],
              scalar1=0.0, scalar2=None, op0=ALU.is_equal)
    cst.append(tid)
    CONST = cst

    PS = k.stack.enter_context(nc.psum_tensor("PS", [128, 4096], F32))
    bankR = [Res() for _ in range(8)]

    rowsA = sbuf("rowsA", (1, 3 * TB), F32)
    rowbf = sbuf("rowbf", (1, 2 * TB), BF16)
    rowsR = Res()
    rowbfR = Res()

    def bank(i):
        return PS[:, i * 512:(i + 1) * 512]

    def brow(i):
        return PS[0:1, i * 512:(i + 1) * 512]

    final_toks = []

    def dbg_dump(name, ap, waits, res_list):
        if not dbg:
            return
        o = dram("dbg_" + name, list(ap.shape), ap.dtype, "ExternalOutput")
        tk = dma("gpsimd", o, ap, waits=waits)
        for r in res_list:
            r.add_r(tk)
        final_toks.append(tk)

    gsc = sbuf("gsc", (128, 10), F32)
    gscR = Res()
    geq = sbuf("geq", (128, 2 * 16), F32)
    geqR = Res()

    def gates_chain(lg_ap, out_ap, ne, waits, out_res):
        d0 = gscR.wr_deps() + geqR.wr_deps()
        tm1 = vop("tensor_reduce", waits=list(waits) + d0, out=gsc[:, 0:1], in_=lg_ap,
                  axis=mybir.AxisListType.X, op=ALU.max)
        te1 = vop("tensor_scalar", waits=[tm1] + list(waits), out=geq[:, 0:ne], in0=lg_ap,
                  scalar1=gsc[:, 0:1], scalar2=None, op0=ALU.is_equal)
        tlm = vop("scalar_tensor_tensor", waits=[te1], out=geq[:, 16:16 + ne],
                  in0=geq[:, 0:ne], scalar=-1e30, in1=lg_ap, op0=ALU.mult, op1=ALU.add)
        tm2 = vop("tensor_reduce", waits=[tlm], out=gsc[:, 1:2], in_=geq[:, 16:16 + ne],
                  axis=mybir.AxisListType.X, op=ALU.max)
        te2 = vop("tensor_scalar", waits=[tm2], out=geq[:, 16:16 + ne], in0=lg_ap,
                  scalar1=gsc[:, 1:2], scalar2=None, op0=ALU.is_equal)
        td = vop("tensor_tensor", waits=[tm2], out=gsc[:, 2:3], in0=gsc[:, 1:2],
                 in1=gsc[:, 0:1], op=ALU.subtract)
        tex = act(gsc[:, 3:4], gsc[:, 2:3], AF.Exp, waits=[td])
        tp1 = vop("tensor_scalar", waits=[tex], out=gsc[:, 4:5], in0=gsc[:, 3:4],
                  scalar1=1.0, scalar2=None, op0=ALU.add)
        tg1 = vop("reciprocal", waits=[tp1], out=gsc[:, 5:6], in_=gsc[:, 4:5])
        tg2 = vop("tensor_scalar", waits=[tg1], out=gsc[:, 6:7], in0=gsc[:, 5:6],
                  scalar1=-1.0, scalar2=1.0, op0=ALU.mult, op1=ALU.add)
        ta = vop("tensor_scalar", waits=[tg1, te1] + out_res.wr_deps(), out=out_ap,
                 in0=geq[:, 0:ne], scalar1=gsc[:, 5:6], scalar2=None, op0=ALU.mult)
        tb_ = vop("scalar_tensor_tensor", waits=[tg2, te2, ta], out=out_ap,
                  in0=geq[:, 16:16 + ne], scalar=gsc[:, 6:7], in1=out_ap,
                  op0=ALU.mult, op1=ALU.add)
        gscR.set_w(tb_)
        geqR.set_w(tb_)
        out_res.set_w(tb_)
        return tb_

    def chunkload(dst, src, waits=()):
        toks = []
        for c in range(DC):
            toks.append(dma("sync", dst[:, c:c + 1], src[c * 128:(c + 1) * 128], waits=waits))
        return toks

    def ln_rows(sum_bank, sq_bank, waits, need_mu):
        deps = list(waits) + rowsR.wr_deps()
        tmu = vop("tensor_scalar", waits=deps, out=rowsA[:, 0:TB], in0=brow(sum_bank),
                  scalar1=1.0 / D, scalar2=None, op0=ALU.mult)
        tm2 = vop("tensor_tensor", waits=[tmu], out=rowsA[:, 2 * TB:3 * TB],
                  in0=rowsA[:, 0:TB], in1=rowsA[:, 0:TB], op=ALU.mult)
        tvar = vop("scalar_tensor_tensor", waits=[tm2], out=rowsA[:, TB:2 * TB],
                   in0=brow(sq_bank), scalar=1.0 / D, in1=rowsA[:, 2 * TB:3 * TB],
                   op0=ALU.mult, op1=ALU.subtract)
        bankR[sum_bank].add_r(tmu)
        bankR[sq_bank].add_r(tvar)
        tln = act(rowsA[:, TB:2 * TB], rowsA[:, TB:2 * TB], AF.Ln, bias=EPS_AP1, waits=[tvar] + CONST)
        tex = act(rowsA[:, TB:2 * TB], rowsA[:, TB:2 * TB], AF.Exp, scale=-0.5, waits=[tln])
        deps2 = rowbfR.wr_deps()
        trb = vop("tensor_copy", waits=[tex] + deps2, out=rowbf[:, 0:TB],
                  in_=rowsA[:, TB:2 * TB])
        tmr = None
        if need_mu:
            tmm = vop("tensor_tensor", waits=[tex, tmu], out=rowsA[:, 2 * TB:3 * TB],
                      in0=rowsA[:, 0:TB], in1=rowsA[:, TB:2 * TB], op=ALU.mult)
            tmr = vop("tensor_copy", waits=[tmm] + deps2, out=rowbf[:, TB:2 * TB],
                      in_=rowsA[:, 2 * TB:3 * TB])
        rowsR.set_w(tmr if tmr else trb)
        rowbfR.set_w(tmr if tmr else trb)
        return trb, tmr

    def ln_rows_f32(sum_bank, sq_bank, waits):
        return ln_rows(sum_bank, sq_bank, waits, True)

    if do_a:
        sbuf = pool_sbuf
        _pool_reset()
        scanmask = sbuf("scanmask", (128, TB), BF16)
        tmsk = k.op("gpsimd", lambda e: e.memset(scanmask[:], 1.0), sem=sG)
        for cs in range(0, TB, CHUNK):
            tmsk = k.op("gpsimd", lambda e, c=cs: e.memset(scanmask[:, c:c + 1], 0.0),
                        waits=[tmsk], sem=sG)
        CONST = CONST + [tmsk]
        biasA = sbuf("biasA", (128, 8 * DC), F32)
        BK, BV32, BQ, BKP, BQP, PSC, PS2PI, LNG = range(8)

        def bcol(idx, c):
            return biasA[:, idx * DC + c: idx * DC + c + 1]

        wstage = sbuf("wstage", (128, D), F32)
        wconv = sbuf("wconv", (128, D), BF16)
        lnbch = sbuf("lnbch", (128, DC), BF16)
        lnb_f = sbuf("lnb_f", (128, DC), F32)
        btrow = sbuf("btrow", (1, D), BF16)

        tb_bk = chunkload(biasA[:, BK * DC:], t["bk"])
        tb_bq = chunkload(biasA[:, BQ * DC:], t["bq"])
        tb_bkp = chunkload(biasA[:, BKP * DC:], t["bkp"])
        tb_bqp = chunkload(biasA[:, BQP * DC:], t["bqp"])
        tb_ps = chunkload(biasA[:, PSC * DC:], t["phase_scale"])
        tb_lng = chunkload(biasA[:, LNG * DC:], t["ln_g"])
        tb_bv = chunkload(biasA[:, BV32 * DC:], t["bv"])
        tv1 = vop("tensor_scalar", waits=tb_bv, out=biasA[:, BV32 * DC:(BV32 + 1) * DC],
                  in0=biasA[:, BV32 * DC:(BV32 + 1) * DC], scalar1=1.0 / 32.0, scalar2=None, op0=ALU.mult)
        tv2 = vop("tensor_scalar", waits=tb_ps, out=biasA[:, PS2PI * DC:(PS2PI + 1) * DC],
                  in0=biasA[:, PSC * DC:(PSC + 1) * DC], scalar1=INV2PI, scalar2=None, op0=ALU.mult)
        tb_lnb = chunkload(lnb_f, t["ln_b"])
        t_lnb = vop("tensor_copy", waits=tb_lnb, out=lnbch[:], in_=lnb_f[:])

        biasA_rdy = {BK: tb_bk, BQ: tb_bq, BKP: tb_bkp, BQP: tb_bqp,
                     PSC: tb_ps, LNG: tb_lng, BV32: [tv1], PS2PI: [tv2]}

        wnames = ["Wk", "Wv", "Wq", "Wkp", "Wqp", "Wo"]
        prev_store = None
        prev_conv = None
        bt_mms = []
        for wi, wn in enumerate(wnames):
            for ci in range(DC):
                tl = dma("scalar", wstage[:], t[wn][ci * 128:(ci + 1) * 128, :],
                         waits=[prev_conv] if prev_conv else [])
                waits_c = [tl] + ([prev_store] if prev_store else [])
                tc = vop("tensor_copy", waits=waits_c, out=wconv[:], in_=wstage[:])
                prev_conv = tc
                src_tok = tc
                if wn == "Wo":
                    m1 = mm(brow(4), lnbch[:, ci:ci + 1], wconv[:, 0:512],
                            start=(ci == 0), stop=(ci == DC - 1),
                            waits=[tc, t_lnb] + bankR[4].wr_deps() if ci == 0 else [tc, t_lnb])
                    m2 = mm(brow(5), lnbch[:, ci:ci + 1], wconv[:, 512:1024],
                            start=(ci == 0), stop=(ci == DC - 1),
                            waits=[tc] + (bankR[5].wr_deps() if ci == 0 else []))
                    bt_mms += [m1, m2]
                    src_tok = vop("tensor_scalar", waits=[m1, m2] + biasA_rdy[LNG],
                                  out=wconv[:], in0=wconv[:], scalar1=bcol(LNG, ci),
                                  scalar2=None, op0=ALU.mult)
                    prev_conv = src_tok
                prev_store = dma("gpsimd", t["wcacheA"][wi, ci], wconv[:], waits=[src_tok])
        bankR[4].set_w(bt_mms[-2])
        bankR[5].set_w(bt_mms[-1])
        tbt = None
        for hb in range(2):
            tb_bo = dma("sync", rowsA[:, 0:TB], t["bo"][hb * 512:(hb + 1) * 512].unsqueeze(0),
                        waits=rowsR.wr_deps())
            rowsR.set_w(tb_bo)
            tbt1 = vop("tensor_tensor", waits=[bankR[4 + hb].w, tb_bo],
                       out=rowsA[:, TB:2 * TB], in0=brow(4 + hb), in1=rowsA[:, 0:TB],
                       op=ALU.add)
            rowsR.set_w(tbt1)
            bankR[4 + hb].add_r(tbt1)
            tbt = vop("tensor_copy", waits=[tbt1], out=btrow[:, hb * 512:(hb + 1) * 512],
                      in_=rowsA[:, TB:2 * TB])
            rowsR.add_r(tbt)
        wprep_done = prev_store

        wbuf = [sbuf("wbuf0", (128, DC, D), BF16), sbuf("wbuf1", (128, DC, D), BF16)]
        wbufR = [Res(), Res()]
        xinf = sbuf("xinf", (128, 2, D), F32)
        xinfR = [Res(), Res()]
        xinb = sbuf("xinb", (128, 2, D), BF16)
        xinbR = [Res(), Res()]
        xT = sbuf("xT", (128, 2, DC, TB), BF16)
        xTR = [Res(), Res()]
        KEY = sbuf("KEY", (128, DC, TB), BF16)
        VAL = sbuf("VAL", (128, DC, TB), BF16)
        QRY = sbuf("QRY", (128, DC, TB), BF16)
        PH = sbuf("PH", (128, DC, TB), F32)
        UI = sbuf("UI", (128, TB), I32)
        ABSR = sbuf("ABSR", (128, TB), F32)
        SIN = sbuf("SIN", (128, DC, TB), BF16)
        COS = sbuf("COS", (128, DC, TB), BF16)
        BR = sbuf("BR", (128, DC, TB), BF16)
        BI = sbuf("BI", (128, DC, TB), BF16)
        MR = sbuf("MR", (128, DC, TB), BF16)
        MI = sbuf("MI", (128, DC, TB), BF16)
        RET = sbuf("RET", (128, DC, TB), BF16)
        RRS = sbuf("RRS", (128, DC, TB), BF16)
        STL = sbuf("STL", (128, DC, TB), BF16)
        tens = {"KEY": KEY, "VAL": VAL, "QRY": QRY, "PH": PH, "SIN": SIN, "COS": COS,
                "BR": BR, "BI": BI, "MR": MR, "MI": MI, "RET": RET, "RRS": RRS,
                "STL": STL}
        bufR = {nm: [Res() for _ in range(DC)] for nm in tens}
        alias = {"TKP": "PH", "TQP": "PH", "SINK": "SIN", "COSK": "COS",
                 "SINQ": "SIN", "COSQ": "COS", "T1": "BR", "SQA": "BI", "X1M": "RET"}
        for a, b in alias.items():
            tens[a] = tens[b]
            bufR[a] = bufR[b]
        t.update({nm: tens[nm] for nm in tens})
        T1 = BR
        SQA = BI
        X1M = RET
        uiR = Res()
        absrR = Res()

        def load_w(widx, slot, extra=()):
            toks = []
            deps = wbufR[slot].wr_deps() + list(extra)
            for ci in range(DC):
                toks.append(dma("scalar" if ci % 2 == 0 else "gpsimd",
                               wbuf[slot][:, ci, :], t["wcacheA"][widx, ci], waits=deps))
            wbufR[slot].set_w(toks[-1])
            for tk in toks[:-1]:
                wbufR[slot].add_r(tk)
            return toks

        for tb in range(NTB):
            sl = tb % 2
            xload = []
            for tt in range(4):
                ts_ = tt % 2
                tld = dma("sync", xinf[:, ts_, :], t["x"][(tb * 4 + tt) * 128:(tb * 4 + tt + 1) * 128, :],
                          waits=xinfR[ts_].wr_deps())
                xinfR[ts_].set_w(tld)
                tcv = vop("tensor_copy", waits=[tld] + xinbR[ts_].wr_deps(),
                          out=xinb[:, ts_, :], in_=xinf[:, ts_, :])
                xinfR[ts_].add_r(tcv)
                xinbR[ts_].set_w(tcv)
                ttr = xpose_dma(xT[:, sl, :, tt * 128:(tt + 1) * 128], xinb[:, ts_, :],
                                waits=[tcv] + (xTR[sl].wr_deps() if tt == 0 else []))
                xinbR[ts_].add_r(ttr)
                xload.append(ttr)
            xTR[sl].set_w(xload[-1])
            for tk in xload[:-1]:
                xTR[sl].add_r(tk)
            xTc = xT[:, sl]

            for pi, (pn, dstn, bidx, scl) in enumerate(
                    (("k", "KEY", BK, None), ("v", "VAL", BV32, 1.0 / 32.0), ("q", "QRY", BQ, None))):
                slot = pi % 2
                wld = load_w(pi, slot, extra=[wprep_done] if tb == 0 else [])
                dst = t[dstn]
                for dc in range(DC):
                    bk_i = dc % 2
                    w0 = wld + [xTR[sl].w] + bankR[bk_i].wr_deps() + biasA_rdy[bidx]
                    mt = None
                    for ci in range(DC):
                        mt = mm(bank(bk_i), wbuf[slot][:, ci, dc * 128:(dc + 1) * 128],
                                xTc[:, ci, :], start=(ci == 0), stop=(ci == DC - 1),
                                waits=w0 if ci == 0 else [], tok=(ci == DC - 1))
                    bankR[bk_i].set_w(mt)
                    xTR[sl].add_r(mt)
                    wbufR[slot].add_r(mt)
                    te = act(dst[:, dc, :], bank(bk_i), AF.Identity, bias=bcol(bidx, dc),
                             scale=scl, waits=[mt] + bufR[dstn][dc].wr_deps())
                    bankR[bk_i].add_r(te)
                    bufR[dstn][dc].set_w(te)

            def phase_chain(srcn, bidx, seq):
                slot = seq % 2
                wld = load_w(3 if srcn == "KEY" else 4, slot)
                src = tens[srcn]
                for dc in range(DC):
                    bk_i = 2 + dc % 2
                    w0 = (wld + [bufR[srcn][c].w for c in range(DC)]
                          + bankR[bk_i].wr_deps() + biasA_rdy[bidx] + bufR["PH"][dc].wr_deps())
                    mt = None
                    for ci in range(DC):
                        mt = mm(bank(bk_i), wbuf[slot][:, ci, dc * 128:(dc + 1) * 128],
                                src[:, ci, :], start=(ci == 0), stop=(ci == DC - 1),
                                waits=w0 if ci == 0 else [], tok=(ci == DC - 1))
                    bankR[bk_i].set_w(mt)
                    wbufR[slot].add_r(mt)
                    for c in range(DC):
                        bufR[srcn][c].add_r(mt)
                    tt_ = act(PH[:, dc, :], bank(bk_i), AF.Tanh, bias=bcol(bidx, dc),
                              waits=[mt] + bufR["PH"][dc].wr_deps())
                    bankR[bk_i].add_r(tt_)
                    bufR["PH"][dc].set_w(tt_)
                for dc in range(DC):
                    tg1 = vop("tensor_scalar", eng="gpsimd",
                              waits=[bufR["PH"][dc].w] + biasA_rdy[PSC],
                              out=PH[:, dc, :], in0=PH[:, dc, :],
                              scalar1=bcol(PSC, dc), scalar2=0.0, op0=ALU.mult, op1=ALU.add)
                    bufR["PH"][dc].set_w(tg1)
                    tg2 = vop("tensor_scalar", waits=[tg1] + uiR.wr_deps(),
                              out=UI[:], in0=PH[:, dc, :],
                              scalar1=INV2PI, scalar2=None, op0=ALU.mult)
                    uiR.set_w(tg2)
                    bufR["PH"][dc].add_r(tg2)
                    tg3 = vop("scalar_tensor_tensor", waits=[tg2],
                              out=PH[:, dc, :], in0=UI[:], scalar=-TWOPI,
                              in1=PH[:, dc, :], op0=ALU.mult, op1=ALU.add)
                    uiR.add_r(tg3)
                    bufR["PH"][dc].set_w(tg3)
                    tsin = act(SIN[:, dc, :], PH[:, dc, :], AF.Sin,
                               waits=[tg3] + bufR["SIN"][dc].wr_deps())
                    bufR["SIN"][dc].set_w(tsin)
                    bufR["PH"][dc].add_r(tsin)
                    tabs = vop("scalar_tensor_tensor", waits=[tg3] + absrR.wr_deps(),
                               out=ABSR[:], in0=PH[:, dc, :], scalar=-1.0,
                               in1=PH[:, dc, :], op0=ALU.mult, op1=ALU.max)
                    absrR.set_w(tabs)
                    bufR["PH"][dc].add_r(tabs)
                    tcos = act(COS[:, dc, :], ABSR[:], AF.Sin, scale=-1.0,
                               bias=HALFPI_AP, waits=[tabs] + CONST + bufR["COS"][dc].wr_deps())
                    absrR.add_r(tcos)
                    bufR["COS"][dc].set_w(tcos)

            phase_chain("KEY", BKP, 3)
            if dbg and tb == 0:
                dbg_dump("SINK", SIN[:], [bufR["SIN"][c].w for c in range(DC)], bufR["SIN"])
                dbg_dump("COSK", COS[:], [bufR["COS"][c].w for c in range(DC)], bufR["COS"])
                dbg_dump("RK", PH[:], [bufR["PH"][c].w for c in range(DC)], bufR["PH"])
            for dc in range(DC):
                tbr = vop("tensor_tensor",
                          waits=[bufR["COS"][dc].w, bufR["VAL"][dc].w] + bufR["BR"][dc].wr_deps(),
                          out=BR[:, dc, :], in0=VAL[:, dc, :], in1=COS[:, dc, :], op=ALU.mult)
                bufR["BR"][dc].set_w(tbr)
                bufR["COS"][dc].add_r(tbr)
                bufR["VAL"][dc].add_r(tbr)
                tbi = vop("tensor_tensor",
                          waits=[bufR["SIN"][dc].w, bufR["VAL"][dc].w] + bufR["BI"][dc].wr_deps(),
                          out=BI[:, dc, :], in0=VAL[:, dc, :], in1=SIN[:, dc, :], op=ALU.mult)
                bufR["BI"][dc].set_w(tbi)
                bufR["SIN"][dc].add_r(tbi)
                bufR["VAL"][dc].add_r(tbi)
                tmr = vop("tensor_tensor_scan", waits=[tbr] + CONST + bufR["MR"][dc].wr_deps(),
                          out=MR[:, dc, :], data0=scanmask[:], data1=BR[:, dc, :],
                          initial=0.0, op0=ALU.mult, op1=ALU.add)
                bufR["MR"][dc].set_w(tmr)
                bufR["BR"][dc].add_r(tmr)
                tmi = vop("tensor_tensor_scan", waits=[tbi] + CONST + bufR["MI"][dc].wr_deps(),
                          out=MI[:, dc, :], data0=scanmask[:], data1=BI[:, dc, :],
                          initial=0.0, op0=ALU.mult, op1=ALU.add)
                bufR["MI"][dc].set_w(tmi)
                bufR["BI"][dc].add_r(tmi)

            if dbg and tb == 0:
                dbg_dump("KEY", KEY[:], [bufR["KEY"][c].w for c in range(DC)], bufR["KEY"])
                dbg_dump("VAL", VAL[:], [bufR["VAL"][c].w for c in range(DC)], bufR["VAL"])
                dbg_dump("MRD", MR[:], [bufR["MR"][c].w for c in range(DC)], bufR["MR"])
            phase_chain("QRY", BQP, 4)
            for dc in range(DC):
                tt1 = vop("tensor_tensor",
                          waits=[bufR["MR"][dc].w, bufR["COS"][dc].w] + bufR["BR"][dc].wr_deps(),
                          out=T1[:, dc, :], in0=MR[:, dc, :], in1=COS[:, dc, :], op=ALU.mult)
                bufR["BR"][dc].set_w(tt1)
                bufR["MR"][dc].add_r(tt1)
                bufR["COS"][dc].add_r(tt1)
                tt2 = vop("tensor_tensor", eng="gpsimd",
                          waits=[bufR["MI"][dc].w, bufR["SIN"][dc].w],
                          out=MI[:, dc, :], in0=MI[:, dc, :], in1=SIN[:, dc, :], op=ALU.mult)
                bufR["MI"][dc].set_w(tt2)
                bufR["SIN"][dc].add_r(tt2)
                tre = vop("tensor_tensor", waits=[tt1, tt2] + bufR["RET"][dc].wr_deps(),
                          out=RET[:, dc, :], in0=T1[:, dc, :], in1=MI[:, dc, :], op=ALU.add)
                bufR["RET"][dc].set_w(tre)
                bufR["BR"][dc].add_r(tre)
                bufR["MI"][dc].add_r(tre)
                tsq = vop("tensor_tensor", eng="gpsimd", waits=[tre] + bufR["BI"][dc].wr_deps(),
                          out=SQA[:, dc, :], in0=RET[:, dc, :], in1=RET[:, dc, :], op=ALU.mult)
                bufR["BI"][dc].set_w(tsq)
                bufR["RET"][dc].add_r(tsq)

            if dbg and tb == 0:
                dbg_dump("RET", RET[:], [bufR["RET"][c].w for c in range(DC)], bufR["RET"])
            m1 = m2 = None
            for dc in range(DC):
                m1 = mm(brow(4), onescol[:], RET[:, dc, :], start=(dc == 0), stop=(dc == DC - 1),
                        waits=[bufR["RET"][dc].w] + CONST + (bankR[4].wr_deps() if dc == 0 else []),
                        tok=(dc == DC - 1))
                m2 = mm(brow(5), onescol[:], SQA[:, dc, :], start=(dc == 0), stop=(dc == DC - 1),
                        waits=[bufR["SQA"][dc].w] + (bankR[5].wr_deps() if dc == 0 else []),
                        tok=(dc == DC - 1))
            bankR[4].set_w(m1)
            bankR[5].set_w(m2)
            for dc in range(DC):
                bufR["RET"][dc].add_r(m1)
                bufR["SQA"][dc].add_r(m2)
            trb, tmr_ = ln_rows(4, 5, [m1, m2], True)
            tbc = mm(bank(6), ones1x128[:], rowbf[:, 0:TB], start=True, stop=True,
                     waits=[trb] + CONST + bankR[6].wr_deps())
            bankR[6].set_w(tbc)
            rowbfR.add_r(tbc)
            tbc7 = mm(bank(7), ones1x128[:], rowbf[:, TB:2 * TB], start=True, stop=True,
                      waits=[tmr_] + CONST + bankR[7].wr_deps())
            bankR[7].set_w(tbc7)
            rowbfR.add_r(tbc7)
            for dc in range(DC):
                trr = vop("tensor_tensor", waits=[tbc, bufR["RET"][dc].w] + bufR["RRS"][dc].wr_deps(),
                          out=RRS[:, dc, :], in0=RET[:, dc, :], in1=bank(6), op=ALU.mult)
                bufR["RET"][dc].add_r(trr)
                bankR[6].add_r(trr)
                trr = vop("tensor_tensor", waits=[tbc7, trr],
                          out=RRS[:, dc, :], in0=RRS[:, dc, :], in1=bank(7), op=ALU.subtract)
                bufR["RRS"][dc].set_w(trr)
                bankR[7].add_r(trr)

            slot = 5 % 2
            wld = load_w(5, slot, extra=[tbt])
            for dc in range(DC):
                bk_i = dc % 2
                w0 = wld + [bufR["RRS"][c].w for c in range(DC)] + bankR[bk_i].wr_deps()
                for ci in range(DC):
                    mm(bank(bk_i), wbuf[slot][:, ci, dc * 128:(dc + 1) * 128],
                       RRS[:, ci, :], start=(ci == 0), stop=False,
                       waits=w0 if ci == 0 else [], tok=False)
                tbm = mm(bank(bk_i), btrow[:, dc * 128:(dc + 1) * 128], ones1x512[:],
                         start=False, stop=True, waits=[tbt] + CONST)
                bankR[bk_i].set_w(tbm)
                wbufR[slot].add_r(tbm)
                for c in range(DC):
                    bufR["RRS"][c].add_r(tbm)
                tst = vop("scalar_tensor_tensor", waits=[tbm, xTR[sl].w] + bufR["STL"][dc].wr_deps(),
                          out=STL[:, dc, :], in0=xTc[:, dc, :], scalar=2.0,
                          in1=bank(bk_i), op0=ALU.mult, op1=ALU.add)
                bankR[bk_i].add_r(tst)
                xTR[sl].add_r(tst)
                bufR["STL"][dc].set_w(tst)

            if dbg and tb == 0:
                dbg_dump("STL", STL[:], [bufR["STL"][c].w for c in range(DC)], bufR["STL"])
            for dc in range(DC):
                tsq = vop("tensor_tensor", eng="gpsimd",
                          waits=[bufR["STL"][dc].w] + bufR["SQA"][dc].wr_deps(),
                          out=SQA[:, dc, :], in0=STL[:, dc, :], in1=STL[:, dc, :], op=ALU.mult)
                bufR["SQA"][dc].set_w(tsq)
                bufR["STL"][dc].add_r(tsq)
            m1 = m2 = None
            for dc in range(DC):
                m1 = mm(brow(4), onescol[:], STL[:, dc, :], start=(dc == 0), stop=(dc == DC - 1),
                        waits=[bufR["STL"][dc].w] + (bankR[4].wr_deps() if dc == 0 else []),
                        tok=(dc == DC - 1))
                m2 = mm(brow(5), onescol[:], SQA[:, dc, :], start=(dc == 0), stop=(dc == DC - 1),
                        waits=[bufR["SQA"][dc].w] + (bankR[5].wr_deps() if dc == 0 else []),
                        tok=(dc == DC - 1))
            bankR[4].set_w(m1)
            bankR[5].set_w(m2)
            for dc in range(DC):
                bufR["STL"][dc].add_r(m1)
                bufR["SQA"][dc].add_r(m2)
            trb0, _ = ln_rows(4, 5, [m1, m2], False)
            tbc0 = mm(bank(6), ones1x128[:], rowbf[:, 0:TB], start=True, stop=True,
                      waits=[trb0] + bankR[6].wr_deps())
            bankR[6].set_w(tbc0)
            rowbfR.add_r(tbc0)
            for dc in range(DC):
                tx1 = vop("tensor_tensor", waits=[tbc0, bufR["STL"][dc].w] + bufR["X1M"][dc].wr_deps(),
                          out=X1M[:, dc, :], in0=STL[:, dc, :], in1=bank(6), op=ALU.mult)
                bufR["X1M"][dc].set_w(tx1)
                bufR["STL"][dc].add_r(tx1)
                bankR[6].add_r(tx1)
                td = dma("gpsimd", t["x1cache"][tb, dc], X1M[:, dc, :], waits=[tx1])
                bufR["X1M"][dc].add_r(td)
                final_toks.append(td)



    if do_b:
        if do_a:
            phase_barrier()
        sbuf = pool_sbuf
        _pool_reset()
        ln1w = sbuf("ln1w", (128, DC), F32)
        ln1b_bf = sbuf("ln1b_bf", (128, DC), BF16)
        ln1b_f = sbuf("ln1b_f", (128, DC), F32)
        tkw1 = chunkload(ln1w, t["ln1_w"])
        tkb1 = chunkload(ln1b_f, t["ln1_b"])
        tlb1 = vop("tensor_copy", waits=tkb1, out=ln1b_bf[:], in_=ln1b_f[:])

        gws = sbuf("gws", (128, DC, EA), BF16)
        kws = sbuf("kws", (128, DC, HD), BF16)
        vws = sbuf("vws", (128, DC, HD), BF16)
        obs = sbuf("obs", (8, D), BF16)
        relT = sbuf("relT", (128, 132), BF16)
        bgate = sbuf("bgate", (1, EA), BF16)
        vbias_row = sbuf("vbias_row", (1, HD), BF16)
        kqbias = sbuf("kqbias", (128, 2 + EA), F32)
        wstB = sbuf("wstB", (128, 512), F32)
        wcvB = sbuf("wcvB", (128, 512), BF16)
        rowB = sbuf("rowB", (1, 2 * HD), F32)
        wstR = Res()
        wcvR = Res()
        rowBR = Res()

        prevB = None
        bias_mm = []
        for ci in range(DC):
            tl = dma("scalar", wstB[:, 0:EA], t["attn_gate_w"][ci * 128:(ci + 1) * 128, :],
                     waits=wstR.wr_deps())
            wstR.set_w(tl)
            tc = vop("tensor_copy", waits=[tl] + wcvR.wr_deps(), out=wcvB[:, 0:EA], in_=wstB[:, 0:EA])
            wstR.add_r(tc)
            wcvR.set_w(tc)
            m1 = mm(PS[0:1, 3584:3584 + EA], ln1b_bf[:, ci:ci + 1], wcvB[:, 0:EA],
                    start=(ci == 0), stop=(ci == DC - 1), waits=[tc, tlb1] + (bankR[7].wr_deps() if ci == 0 else []))
            bias_mm.append(m1)
            tg = vop("tensor_scalar", waits=[m1] + tkw1, out=gws[:, ci, :], in0=wcvB[:, 0:EA],
                     scalar1=ln1w[:, ci:ci + 1], scalar2=None, op0=ALU.mult)
            wcvR.add_r(tg)
            prevB = tg
        bankR[7].set_w(bias_mm[-1])
        tbg = vop("tensor_copy", waits=[bias_mm[-1]], out=bgate[:], in_=PS[0:1, 3584:3584 + EA])
        bankR[7].add_r(tbg)

        kvbias_toks = []
        for wi, (wn, bn, dstw) in enumerate((("k_w", "k_b", kws), ("v_w", "v_b", vws))):
            bias_mm = []
            for ci in range(DC):
                tl = dma("scalar", wstB[:, 0:HD], t[wn][ci * 128:(ci + 1) * 128, :],
                         waits=wstR.wr_deps())
                wstR.set_w(tl)
                tc = vop("tensor_copy", waits=[tl] + wcvR.wr_deps(), out=wcvB[:, 0:HD], in_=wstB[:, 0:HD])
                wstR.add_r(tc)
                wcvR.set_w(tc)
                m1 = mm(PS[0:1, 3584:3584 + HD], ln1b_bf[:, ci:ci + 1], wcvB[:, 0:HD],
                        start=(ci == 0), stop=(ci == DC - 1), waits=[tc, tlb1] + (bankR[7].wr_deps() if ci == 0 else []))
                bias_mm.append(m1)
                tg = vop("tensor_scalar", waits=[m1] + tkw1, out=dstw[:, ci, :], in0=wcvB[:, 0:HD],
                         scalar1=ln1w[:, ci:ci + 1], scalar2=None, op0=ALU.mult)
                wcvR.add_r(tg)
            bankR[7].set_w(bias_mm[-1])
            tbl = dma("sync", rowB[:, 0:HD], t[bn].unsqueeze(0), waits=rowBR.wr_deps())
            tba = vop("tensor_tensor", waits=[tbl, bias_mm[-1]], out=rowB[:, HD:2 * HD],
                      in0=PS[0:1, 3584:3584 + HD], in1=rowB[:, 0:HD], op=ALU.add)
            rowBR.set_w(tba)
            bankR[7].add_r(tba)
            if wn == "v_w":
                tvb = vop("tensor_copy", waits=[tba], out=vbias_row[:], in_=rowB[:, HD:2 * HD])
                rowBR.add_r(tvb)
                kvbias_toks.append(tvb)
            else:
                tst = dma("gpsimd", t["rowbounceB"][0:HD], rowB[0:1, HD:2 * HD], waits=[tba])
                rowBR.add_r(tst)
                tld = dma("gpsimd", kqbias[:, 0:1], t["rowbounceB"][0:HD], waits=[tst])
                kvbias_toks.append(tld)

        for e in range(EA):
            bias_mm = []
            for ci in range(DC):
                tl = dma("scalar", wstB[:, 0:HD], t["q_w"][e, ci * 128:(ci + 1) * 128, :],
                         waits=wstR.wr_deps())
                wstR.set_w(tl)
                tc = vop("tensor_copy", waits=[tl] + wcvR.wr_deps(), out=wcvB[:, 0:HD], in_=wstB[:, 0:HD])
                wstR.add_r(tc)
                wcvR.set_w(tc)
                m1 = mm(PS[0:1, 3584:3584 + HD], ln1b_bf[:, ci:ci + 1], wcvB[:, 0:HD],
                        start=(ci == 0), stop=(ci == DC - 1), waits=[tc, tlb1] + (bankR[7].wr_deps() if ci == 0 else []))
                bias_mm.append(m1)
                tg = vop("tensor_scalar", waits=[m1] + tkw1, out=wcvB[:, 0:HD], in0=wcvB[:, 0:HD],
                         scalar1=ln1w[:, ci:ci + 1], scalar2=HD ** -0.5, op0=ALU.mult, op1=ALU.mult)
                wcvR.set_w(tg)
                tsv = dma("gpsimd", t["wq_c"][e, ci], wcvB[:, 0:HD], waits=[tg])
                wcvR.add_r(tsv)
            bankR[7].set_w(bias_mm[-1])
            tbl = dma("sync", rowB[:, 0:HD], t["q_b"][e].unsqueeze(0), waits=rowBR.wr_deps())
            tba = vop("scalar_tensor_tensor", waits=[tbl, bias_mm[-1]], out=rowB[:, HD:2 * HD],
                      in0=PS[0:1, 3584:3584 + HD], scalar=1.0, in1=rowB[:, 0:HD],
                      op0=ALU.mult, op1=ALU.add)
            rowBR.set_w(tba)
            bankR[7].add_r(tba)
            tsc = vop("tensor_scalar", waits=[tba], out=rowB[:, HD:2 * HD], in0=rowB[:, HD:2 * HD],
                      scalar1=HD ** -0.5, scalar2=None, op0=ALU.mult)
            rowBR.set_w(tsc)
            tst = dma("gpsimd", t["rowbounceB"][0:HD], rowB[0:1, HD:2 * HD], waits=[tsc])
            rowBR.add_r(tst)
            tld = dma("gpsimd", kqbias[:, 2 + e:3 + e], t["rowbounceB"][0:HD], waits=[tst])
            kvbias_toks.append(tld)

        ow_toks = []
        for e in range(EA):
            for hb in range(2):
                tl = dma("scalar", wstB[:], t["o_w"][e][:, hb * 512:(hb + 1) * 512],
                         waits=wstR.wr_deps())
                wstR.set_w(tl)
                tc = vop("tensor_copy", waits=[tl] + wcvR.wr_deps(), out=wcvB[:, 0:512], in_=wstB[:])
                wstR.add_r(tc)
                wcvR.set_w(tc)
                tsv = dma("gpsimd", t["owc"][e][:, hb * 512:(hb + 1) * 512], wcvB[:, 0:512],
                          waits=[tc])
                wcvR.add_r(tsv)
                ow_toks.append(tsv)
        obstage = sbuf("obstage", (8, D), F32)
        tob_l = dma("scalar", obstage[:], t["o_b"][:, :])
        tob = vop("tensor_copy", waits=[tob_l], out=obs[:], in_=obstage[:])

        trl = dma("scalar", wstB[:, 0:HD], t["rel_table"][0:128, :], waits=wstR.wr_deps())
        wstR.set_w(trl)
        ttr = k.op("tensor", lambda e: e.transpose(PS[:, 3584:3584 + 128], wstB[:, 0:HD], ident[:]),
                   waits=[trl, tid] + bankR[7].wr_deps(), sem=sT)
        bankR[7].set_w(ttr)
        wstR.add_r(ttr)
        tre = vop("tensor_copy", waits=[ttr], out=relT[:, 0:128], in_=PS[:, 3584:3584 + 128])
        bankR[7].add_r(tre)
        trl2 = dma("scalar", rowB[:, 0:HD], t["rel_table"][128].unsqueeze(0), waits=rowBR.wr_deps())
        rowBR.set_w(trl2)
        ttr2 = k.op("tensor", lambda e: e.transpose(PS[:, 3584:3584 + 1], rowB[:, 0:HD], ident[0:1, 0:1]),
                    waits=[trl2, tid, tre] + bankR[7].wr_deps(), sem=sT)
        bankR[7].set_w(ttr2)
        rowBR.add_r(ttr2)
        tre2 = vop("tensor_copy", waits=[ttr2], out=relT[:, 128:129], in_=PS[:, 3584:3584 + 1])
        bankR[7].add_r(tre2)
        relT_rdy = [tre, tre2]
        prepB_done = [tbg, tob] + kvbias_toks + ow_toks + relT_rdy

        x1b = sbuf("x1b", (128, DC, TB), BF16)
        x1bR = [Res() for _ in range(DC)]
        XN = sbuf("XN", (128, DC, TB), BF16)
        xnR = [Res() for _ in range(DC)]
        SQB = XN
        sqbR = xnR
        kF = sbuf("kF", (128, NT), BF16)
        kFR = [Res() for _ in range(NTB)]
        vK = sbuf("vK", (128, 32, HD), BF16)
        vKR = [Res() for _ in range(32)]
        LGF = sbuf("LGF", (16, TB), F32)
        lgfR = Res()
        LGA = sbuf("LGA", (128, 4, 16), F32)
        lgaR = [Res() for _ in range(4)]
        GTS = sbuf("GTS", (128, 32, EA), F32)
        gtsR = [Res() for _ in range(32)]
        gstg = sbuf("gstg", (8, 128), BF16)
        gstgR = Res()
        gATb = sbuf("gATb", (8, TB), BF16)
        gATbR = Res()
        rowtb = sbuf("rowtb", (1, 2 * TB), BF16)
        rowtbR = Res()

        rows1_toks = {}

        def load_x1(tb, extra=()):
            toks = []
            for dc in range(DC):
                tk_ = dma("gpsimd", x1b[:, dc, :], t["x1cache"][tb, dc],
                          waits=x1bR[dc].wr_deps() + list(extra))
                x1bR[dc].set_w(tk_)
                toks.append(tk_)
            return toks

        def ln1_and_xn(tb, phaseA_done):
            tx1 = load_x1(tb, extra=phaseA_done)
            for dc in range(DC):
                tsq = vop("tensor_tensor", eng="gpsimd", waits=[tx1[dc]] + sqbR[dc].wr_deps(),
                          out=SQB[:, dc, :], in0=x1b[:, dc, :], in1=x1b[:, dc, :], op=ALU.mult)
                sqbR[dc].set_w(tsq)
            m1 = m2 = None
            for dc in range(DC):
                m1 = mm(brow(4), onescol[:], x1b[:, dc, :], start=(dc == 0), stop=(dc == DC - 1),
                        waits=[tx1[dc]] + CONST + (bankR[4].wr_deps() if dc == 0 else []),
                        tok=(dc == DC - 1))
                m2 = mm(brow(5), onescol[:], SQB[:, dc, :], start=(dc == 0), stop=(dc == DC - 1),
                        waits=[sqbR[dc].w] + (bankR[5].wr_deps() if dc == 0 else []),
                        tok=(dc == DC - 1))
            bankR[4].set_w(m1)
            bankR[5].set_w(m2)
            for dc in range(DC):
                x1bR[dc].add_r(m1)
                sqbR[dc].add_r(m2)
            trb, tmr_ = ln_rows(4, 5, [m1, m2], True)
            tc1 = dma("gpsimd", t["rows1d"][0, tb * TB:(tb + 1) * TB], rowbf[0:1, 0:TB],
                      waits=[trb, tmr_])
            tc2 = dma("gpsimd", t["rows1d"][1, tb * TB:(tb + 1) * TB], rowbf[0:1, TB:2 * TB],
                      waits=[tmr_])
            rowbfR.add_r(tc1)
            rowbfR.add_r(tc2)
            rows1_toks[tb] = [tc1, tc2]
            return make_xn(tb, rowbf[:, 0:TB], rowbf[:, TB:2 * TB], [trb, tmr_], rowbfR)

        def make_xn(tb, rstd_ap, mu_ap, row_toks, row_res):
            tbc = mm(bank(6), ones1x128[:], rstd_ap, start=True, stop=True,
                     waits=list(row_toks) + CONST + bankR[6].wr_deps())
            bankR[6].set_w(tbc)
            row_res.add_r(tbc)
            tbc7 = mm(bank(7), ones1x128[:], mu_ap, start=True, stop=True,
                      waits=list(row_toks) + CONST + bankR[7].wr_deps())
            bankR[7].set_w(tbc7)
            row_res.add_r(tbc7)
            toks = []
            for dc in range(DC):
                t1_ = vop("tensor_tensor", waits=[tbc, x1bR[dc].w] + xnR[dc].wr_deps(),
                          out=XN[:, dc, :], in0=x1b[:, dc, :], in1=bank(6), op=ALU.mult)
                x1bR[dc].add_r(t1_)
                bankR[6].add_r(t1_)
                t2_ = vop("tensor_tensor", waits=[tbc7, t1_],
                          out=XN[:, dc, :], in0=XN[:, dc, :], in1=bank(7), op=ALU.subtract)
                bankR[7].add_r(t2_)
                xnR[dc].set_w(t2_)
                toks.append(t2_)
            return toks


        rows1_toks = {}

        def load_x1(tb, extra=()):
            toks = []
            for dc in range(DC):
                tk_ = dma("gpsimd", x1b[:, dc, :], t["x1cache"][tb, dc],
                          waits=x1bR[dc].wr_deps() + list(extra))
                x1bR[dc].set_w(tk_)
                toks.append(tk_)
            return toks

        def ln1_and_xn(tb, phaseA_done):
            tx1 = load_x1(tb, extra=phaseA_done)
            for dc in range(DC):
                tsq = vop("tensor_tensor", eng="gpsimd", waits=[tx1[dc]] + sqbR[dc].wr_deps(),
                          out=SQB[:, dc, :], in0=x1b[:, dc, :], in1=x1b[:, dc, :], op=ALU.mult)
                sqbR[dc].set_w(tsq)
            m1 = m2 = None
            for dc in range(DC):
                m1 = mm(brow(4), onescol[:], x1b[:, dc, :], start=(dc == 0), stop=(dc == DC - 1),
                        waits=[tx1[dc]] + CONST + (bankR[4].wr_deps() if dc == 0 else []),
                        tok=(dc == DC - 1))
                m2 = mm(brow(5), onescol[:], SQB[:, dc, :], start=(dc == 0), stop=(dc == DC - 1),
                        waits=[sqbR[dc].w] + (bankR[5].wr_deps() if dc == 0 else []),
                        tok=(dc == DC - 1))
            bankR[4].set_w(m1)
            bankR[5].set_w(m2)
            for dc in range(DC):
                x1bR[dc].add_r(m1)
                sqbR[dc].add_r(m2)
            trb, tmr_ = ln_rows(4, 5, [m1, m2], True)
            tc1 = dma("gpsimd", t["rows1d"][0, tb * TB:(tb + 1) * TB], rowbf[0:1, 0:TB],
                      waits=[trb, tmr_])
            tc2 = dma("gpsimd", t["rows1d"][1, tb * TB:(tb + 1) * TB], rowbf[0:1, TB:2 * TB],
                      waits=[tmr_])
            rowbfR.add_r(tc1)
            rowbfR.add_r(tc2)
            rows1_toks[tb] = [tc1, tc2]
            return make_xn(tb, rowbf[:, 0:TB], rowbf[:, TB:2 * TB], [trb, tmr_], rowbfR)

        def make_xn(tb, rstd_ap, mu_ap, row_toks, row_res):
            tbc = mm(bank(6), ones1x128[:], rstd_ap, start=True, stop=True,
                     waits=list(row_toks) + CONST + bankR[6].wr_deps())
            bankR[6].set_w(tbc)
            row_res.add_r(tbc)
            tbc7 = mm(bank(7), ones1x128[:], mu_ap, start=True, stop=True,
                      waits=list(row_toks) + CONST + bankR[7].wr_deps())
            bankR[7].set_w(tbc7)
            row_res.add_r(tbc7)
            toks = []
            for dc in range(DC):
                t1_ = vop("tensor_tensor", waits=[tbc, x1bR[dc].w] + xnR[dc].wr_deps(),
                          out=XN[:, dc, :], in0=x1b[:, dc, :], in1=bank(6), op=ALU.mult)
                x1bR[dc].add_r(t1_)
                bankR[6].add_r(t1_)
                t2_ = vop("tensor_tensor", waits=[tbc7, t1_],
                          out=XN[:, dc, :], in0=XN[:, dc, :], in1=bank(7), op=ALU.subtract)
                bankR[7].add_r(t2_)
                xnR[dc].set_w(t2_)
                toks.append(t2_)
            return toks


        def gates_chain(lg_ap, out_ap, ne, waits, out_res):
            d0 = gscR.wr_deps() + geqR.wr_deps()
            tm1 = vop("tensor_reduce", waits=list(waits) + d0, out=gsc[:, 0:1], in_=lg_ap,
                      axis=mybir.AxisListType.X, op=ALU.max)
            te1 = vop("tensor_scalar", waits=[tm1] + list(waits), out=geq[:, 0:ne], in0=lg_ap,
                      scalar1=gsc[:, 0:1], scalar2=None, op0=ALU.is_equal)
            tlm = vop("scalar_tensor_tensor", waits=[te1], out=geq[:, 16:16 + ne],
                      in0=geq[:, 0:ne], scalar=-1e30, in1=lg_ap, op0=ALU.mult, op1=ALU.add)
            tm2 = vop("tensor_reduce", waits=[tlm], out=gsc[:, 1:2], in_=geq[:, 16:16 + ne],
                      axis=mybir.AxisListType.X, op=ALU.max)
            te2 = vop("tensor_scalar", waits=[tm2], out=geq[:, 16:16 + ne], in0=lg_ap,
                      scalar1=gsc[:, 1:2], scalar2=None, op0=ALU.is_equal)
            td = vop("tensor_tensor", waits=[tm2], out=gsc[:, 2:3], in0=gsc[:, 1:2],
                     in1=gsc[:, 0:1], op=ALU.subtract)
            tex = act(gsc[:, 3:4], gsc[:, 2:3], AF.Exp, waits=[td])
            tp1 = vop("tensor_scalar", waits=[tex], out=gsc[:, 4:5], in0=gsc[:, 3:4],
                      scalar1=1.0, scalar2=None, op0=ALU.add)
            tg1 = vop("reciprocal", waits=[tp1], out=gsc[:, 5:6], in_=gsc[:, 4:5])
            tg2 = vop("tensor_scalar", waits=[tg1], out=gsc[:, 6:7], in0=gsc[:, 5:6],
                      scalar1=-1.0, scalar2=1.0, op0=ALU.mult, op1=ALU.add)
            ta = vop("tensor_scalar", waits=[tg1, te1] + out_res.wr_deps(), out=out_ap,
                     in0=geq[:, 0:ne], scalar1=gsc[:, 5:6], scalar2=None, op0=ALU.mult)
            tb_ = vop("scalar_tensor_tensor", waits=[tg2, te2, ta], out=out_ap,
                      in0=geq[:, 16:16 + ne], scalar=gsc[:, 6:7], in1=out_ap,
                      op0=ALU.mult, op1=ALU.add)
            gscR.set_w(tb_)
            geqR.set_w(tb_)
            out_res.set_w(tb_)
            return tb_

        gat_toks = []
        for tb in range(NTB):
            xn_toks = ln1_and_xn(tb, final_toks[-1:] if do_a else [])
            mt = None
            for ci in range(DC):
                mt = mm(bank(0), kws[:, ci, :], XN[:, ci, :], start=(ci == 0), stop=(ci == DC - 1),
                        waits=(xn_toks + prepB_done + bankR[0].wr_deps()) if ci == 0 else [],
                        tok=(ci == DC - 1))
            bankR[0].set_w(mt)
            for dc in range(DC):
                xnR[dc].add_r(mt)
            te = act(kF[:, tb * TB:(tb + 1) * TB], bank(0), AF.Identity, bias=kqbias[:, 0:1],
                     waits=[mt] + kFR[tb].wr_deps())
            bankR[0].add_r(te)
            kFR[tb].set_w(te)
            for tt in range(4):
                mt = None
                for ci in range(DC):
                    mt = mm(PS[:, 512:512 + HD], XN[:, ci, tt * 128:(tt + 1) * 128], vws[:, ci, :],
                            start=(ci == 0), stop=False,
                            waits=(xn_toks + bankR[1].wr_deps()) if ci == 0 else [], tok=False)
                mb = mm(PS[:, 512:512 + HD], ones1x128[:], vbias_row[:],
                        start=False, stop=True, waits=CONST + prepB_done)
                bankR[1].set_w(mb)
                for dc in range(DC):
                    xnR[dc].add_r(mb)
                tev = vop("tensor_copy", waits=[mb] + vKR[tb * 4 + tt].wr_deps(),
                          out=vK[:, tb * 4 + tt, :], in_=PS[:, 512:512 + HD])
                bankR[1].add_r(tev)
                vKR[tb * 4 + tt].set_w(tev)
            mt = None
            for ci in range(DC):
                mt = mm(PS[0:8, 1024:1024 + TB], gws[:, ci, :], XN[:, ci, :],
                        start=(ci == 0), stop=False,
                        waits=(xn_toks + bankR[2].wr_deps()) if ci == 0 else [], tok=False)
            mb = mm(PS[0:8, 1024:1024 + TB], bgate[:], ones1x512[:], start=False, stop=True,
                    waits=CONST + prepB_done)
            bankR[2].set_w(mb)
            for dc in range(DC):
                xnR[dc].add_r(mb)
            tlg = vop("tensor_copy", waits=[mb] + lgfR.wr_deps(), out=LGF[0:8, 0:TB],
                      in_=PS[0:8, 1024:1024 + TB])
            bankR[2].add_r(tlg)
            lgfR.set_w(tlg)
            for tt in range(4):
                tpt = k.op("tensor",
                           lambda e, o=PS[:, 1536:1536 + 8], i=LGF[0:8, tt * 128:(tt + 1) * 128]:
                           e.transpose(o, i, ident[0:8, 0:8]),
                           waits=[tlg, tid] + bankR[3].wr_deps(), sem=sT)
                bankR[3].set_w(tpt)
                lgfR.add_r(tpt)
                tla = vop("tensor_copy", waits=[tpt] + lgaR[tt].wr_deps(),
                          out=LGA[:, tt, 0:8], in_=PS[:, 1536:1536 + 8])
                bankR[3].add_r(tla)
                lgaR[tt].set_w(tla)
                tg = gates_chain(LGA[:, tt, 0:8], GTS[:, tb * 4 + tt, :], EA,
                                 [tla], gtsR[tb * 4 + tt])
                tgt = k.op("tensor",
                           lambda e, o=PS[0:8, 1536 + 16:1536 + 16 + 128], i=GTS[:, tb * 4 + tt, :]:
                           e.transpose(o, i, ident[:]),
                           waits=[tg, tid] + bankR[3].wr_deps(), sem=sT)
                bankR[3].set_w(tgt)
                gtsR[tb * 4 + tt].add_r(tgt)
                tge = vop("tensor_copy", waits=[tgt] + gstgR.wr_deps(),
                          out=gstg[:], in_=PS[0:8, 1536 + 16:1536 + 16 + 128])
                bankR[3].add_r(tge)
                gstgR.set_w(tge)
                tgd = dma("gpsimd", t["gatc"][:, (tb * 4 + tt) * 128:(tb * 4 + tt + 1) * 128],
                          gstg[:], waits=[tge])
                gstgR.add_r(tgd)
                gat_toks.append(tgd)


        QE = sbuf("QE", (128, EA, TB), BF16)
        qeR = [Res() for _ in range(EA)]
        _wq0 = sbuf("wqbuf0", (128, DC, HD), BF16)
        _wqr0 = Res()
        wqbuf = [_wq0, _wq0]
        wqR = [_wqr0, _wqr0]
        RELS = sbuf("RELS", (128, 132), F32)
        relsR = Res()
        RB = sbuf("RB", (128, 640), BF16)
        rbR = Res()
        BAND = sbuf("BAND", (128, 2, 384), BF16)
        bandR = [Res(), Res()]
        P1 = sbuf("P1", (128, 2, NT), BF16)
        p1R = [Res(), Res()]
        PT = sbuf("PT", (128, 32, EA, 128), BF16)
        ptR = Res()
        ZP = sbuf("ZP", (128, 8), F32)
        zpR = Res()
        ZS = sbuf("ZS", (128, 2), F32)
        zsR = Res()
        FV = sbuf("FV", (128, EA), F32)
        fvR = Res()
        frows = sbuf("frows", (1, 128), BF16)
        frowsR = Res()
        FBt = sbuf("FBt", (128, EA, 128), BF16)
        fbR = [Res() for _ in range(EA)]
        CXP = sbuf("CXP", (128, EA, TB), BF16)
        cxpR = [Res() for _ in range(EA)]
        owdc = sbuf("owdc", (128, 2, EA, 128), BF16)
        owdcR = [Res(), Res()]
        SOUT = sbuf("SOUT", (128, 2, TB), BF16)
        soutR = [Res(), Res()]
        X2O = SOUT
        x2oR = soutR
        rimgR = [Res(), Res()]

        sstore_toks = []
        regA = (0, 1)
        regB = (2, 3)

        for tb in range(NTB):
            tx1 = load_x1(tb)
            cxp_w_toks = []
            trl1 = dma("gpsimd", rowtb[0:1, 0:TB].bitcast(BF16) if False else rowtb[0:1, 0:TB],
                       t["rows1d"][0, tb * TB:(tb + 1) * TB],
                       waits=rowtbR.wr_deps() + rows1_toks[tb])
            trl2 = dma("gpsimd", rowtb[0:1, TB:2 * TB], t["rows1d"][1, tb * TB:(tb + 1) * TB],
                       waits=rows1_toks[tb])
            rowtbR.set_w(trl2)
            rowtbR.add_r(trl1)
            tgl = dma("gpsimd", gATb[:], t["gatc"][:, tb * TB:(tb + 1) * TB],
                      waits=gATbR.wr_deps() + gat_toks)
            gATbR.set_w(tgl)
            xn_toks = make_xn(tb, rowtb[:, 0:TB], rowtb[:, TB:2 * TB], [trl1, trl2], rowtbR)
            for e in range(EA):
                slot = e % 2
                wq_toks = []
                deps = wqR[slot].wr_deps()
                for ci in range(DC):
                    wq_toks.append(dma("scalar", wqbuf[slot][:, ci, :], t["wq_c"][e, ci],
                                       waits=deps + (kvbias_toks if tb == 0 else [])))
                wqR[slot].set_w(wq_toks[-1])
                for tk_ in wq_toks[:-1]:
                    wqR[slot].add_r(tk_)
                bk_i = 2 + e % 2
                mt = None
                for ci in range(DC):
                    mt = mm(bank(bk_i), wqbuf[slot][:, ci, :], XN[:, ci, :],
                            start=(ci == 0), stop=(ci == DC - 1),
                            waits=(wq_toks + xn_toks + bankR[bk_i].wr_deps()) if ci == 0 else [],
                            tok=(ci == DC - 1))
                bankR[bk_i].set_w(mt)
                wqR[slot].add_r(mt)
                for dc in range(DC):
                    xnR[dc].add_r(mt)
                te = act(QE[:, e, :], bank(bk_i), AF.Identity, bias=kqbias[:, 2 + e:3 + e],
                         waits=[mt] + qeR[e].wr_deps())
                bankR[bk_i].add_r(te)
                qeR[e].set_w(te)

            for qtl in range(4):
                j = tb * 4 + qtl
                j0 = max(0, 128 * (j - 1))
                j1 = min(NT, 128 * (j + 2))
                wwid = j1 - j0
                moff = 256 if j == 0 else 128
                xbar_toks = []
                for e in range(EA):
                    trm = mm(PS[:, 2048:2048 + 132], QE[:, e, qtl * 128:(qtl + 1) * 128],
                             relT[:, 0:132], start=True, stop=True,
                             waits=[qeR[e].w] + relT_rdy + bankR[4].wr_deps())
                    bankR[4].set_w(trm)
                    qeR[e].add_r(trm)
                    tre_ = vop("tensor_copy", waits=[trm] + relsR.wr_deps(),
                               out=RELS[:], in_=PS[:, 2048:2048 + 132])
                    bankR[4].add_r(tre_)
                    relsR.set_w(tre_)
                    tr1 = vop("tensor_scalar", eng="gpsimd", waits=[tre_] + rbR.wr_deps(),
                              out=RB[:, 0:192], in0=RB[:, 0:192], scalar1=0.0,
                              scalar2=RELS[:, 0:1], op0=ALU.mult, op1=ALU.add)
                    tr2 = vop("tensor_copy", eng="gpsimd", waits=[tre_, tr1],
                              out=RB[:, 192:321], in_=RELS[:, 0:129])
                    tr3 = vop("tensor_scalar", eng="gpsimd", waits=[tre_, tr2],
                              out=RB[:, 321:640], in0=RB[:, 321:640], scalar1=0.0,
                              scalar2=RELS[:, 128:129], op0=ALU.mult, op1=ALU.add)
                    relsR.add_r(tr1)
                    relsR.add_r(tr2)
                    relsR.add_r(tr3)
                    rbR.set_w(tr3)
                    par = e % 2
                    trs = dma("scalar", t["Rimg"][par], RB[:], waits=[tr3] + rimgR[par].wr_deps())
                    rbR.add_r(trs)
                    rimgR[par].set_w(trs)
                    src = AP(tensor=t["Rimg"].tensor, offset=par * 128 * 640 + moff,
                             ap=[[639, 128], [1, wwid]])
                    tgt_ = dma("gpsimd", BAND[:, par, 0:wwid], src,
                               waits=[trs] + bandR[par].wr_deps())
                    rimgR[par].add_r(tgt_)
                    bandR[par].set_w(tgt_)
                    zidx = 0
                    exp_toks = []
                    zp_deps = zpR.wr_deps()
                    for u in range(4):
                        reg = regA if u % 2 == 0 else regB
                        psoff = (u % 2) * 1024
                        rdep = bankR[reg[0]].wr_deps() + bankR[reg[1]].wr_deps()
                        mts = []
                        for half in range(2):
                            mt = mm(PS[:, psoff + half * 512: psoff + (half + 1) * 512],
                                    QE[:, e, qtl * 128:(qtl + 1) * 128],
                                    kF[:, u * 1024 + half * 512: u * 1024 + (half + 1) * 512],
                                    start=True, stop=True,
                                    waits=([qeR[e].w] + [kFR[i].w for i in range(NTB)] + rdep)
                                    if half == 0 else [], tok=True)
                            mts.append(mt)
                        qeR[e].add_r(mts[-1])
                        bankR[reg[0]].set_w(mts[0])
                        bankR[reg[1]].set_w(mts[1])
                        us, ue = 1024 * u, 1024 * (u + 1)
                        ba, bb = max(us, j0), min(ue, j1)
                        tband = None
                        if ba < bb:
                            tband = vop("tensor_tensor",
                                        waits=[tgt_, mts[0] if ba < us + 512 else mts[1],
                                               mts[1] if bb > us + 512 else mts[0]],
                                        out=PS[:, psoff + ba - us: psoff + bb - us],
                                        in0=PS[:, psoff + ba - us: psoff + bb - us],
                                        in1=BAND[:, par, ba - j0: bb - j0], op=ALU.add)
                            bandR[par].add_r(tband)
                            bankR[reg[0]].set_w(tband)
                            bankR[reg[1]].set_w(tband)
                        segs = []
                        if us < j0:
                            segs.append((us, min(ue, j0), RELS[:, 0:1]))
                        if ba < bb:
                            segs.append((ba, bb, None))
                        if ue > j1:
                            segs.append((max(us, j1), ue, RELS[:, 128:129]))
                        for (a, b, bias_ap) in segs:
                            dep = [mts[0]] if a < us + 512 else []
                            if b > us + 512:
                                dep.append(mts[1])
                            if tband is not None:
                                dep.append(tband)
                            tex = act(P1[:, e % 2, a:b], PS[:, psoff + a - us: psoff + b - us],
                                      AF.Exp, bias=bias_ap,
                                      accum=ZP[:, zidx:zidx + 1],
                                      waits=dep + [tre_] + zp_deps
                                      + (p1R[e % 2].wr_deps() if (u == 0 and zidx == 0) else []))
                            relsR.add_r(tex)
                            bankR[reg[0]].add_r(tex)
                            bankR[reg[1]].add_r(tex)
                            exp_toks.append(tex)
                            zidx += 1
                    p1R[e % 2].set_w(exp_toks[-1])
                    for tk_ in exp_toks[:-1]:
                        p1R[e % 2].add_r(tk_)
                    tz = vop("tensor_reduce", waits=exp_toks + zsR.wr_deps(),
                             out=ZS[:, 0:1], in_=ZP[:, 0:zidx], axis=mybir.AxisListType.X,
                             op=ALU.add)
                    zpR.set_w(tz)
                    tiv = vop("reciprocal", waits=[tz], out=ZS[:, 1:2], in_=ZS[:, 0:1])
                    zsR.set_w(tiv)
                    tf = vop("tensor_tensor", waits=[tiv, gtsR[j].w] + (fvR.wr_deps() if e == 0 else []),
                             out=FV[:, e:e + 1], in0=GTS[:, j, e:e + 1], in1=ZS[:, 1:2],
                             op=ALU.mult)
                    zsR.add_r(tf)
                    gtsR[j].add_r(tf)
                    if e == EA - 1:
                        fvR.set_w(tf)
                    txb = xpose_dma(PT[:, :, e, :], P1[:, e % 2, :],
                                    waits=[p1R[e % 2].w] + ([ptR.w] + ptR.readers if e == 0 else []))
                    p1R[e % 2].add_r(txb)
                    xbar_toks.append(txb)
                ptR.set_w(xbar_toks[-1])
                for tk_ in xbar_toks[:-1]:
                    ptR.add_r(tk_)
                for e in range(EA):
                    tft = k.op("tensor",
                               lambda e_, i=FV[:, e:e + 1]:
                               e_.transpose(PS[0:1, 2048:2048 + 128], i, ident[:]),
                               waits=[fvR.w, tid] + bankR[4].wr_deps(), sem=sT)
                    bankR[4].set_w(tft)
                    fvR.add_r(tft)
                    tfr = vop("tensor_copy", waits=[tft] + frowsR.wr_deps(),
                              out=frows[0:1, :], in_=PS[0:1, 2048:2048 + 128])
                    bankR[4].add_r(tfr)
                    frowsR.set_w(tfr)
                    tfb = mm(PS[:, 2560:2560 + 128], ones1x128[:], frows[0:1, :],
                             start=True, stop=True, waits=[tfr] + CONST + bankR[5].wr_deps())
                    bankR[5].set_w(tfb)
                    frowsR.add_r(tfb)
                    tfc = vop("tensor_copy", waits=[tfb] + fbR[e].wr_deps(),
                              out=FBt[:, e, :], in_=PS[:, 2560:2560 + 128])
                    bankR[5].add_r(tfc)
                    fbR[e].set_w(tfc)
                m6 = m7 = None
                for kk in range(32):
                    w0 = (xbar_toks + [vKR[i].w for i in range(32)]
                          + bankR[6].wr_deps() + bankR[7].wr_deps()) if kk == 0 else []
                    m6 = mm(PS[:, 3072:3584], vK[:, kk, :], PT[:, kk, 0:4, :],
                            start=(kk == 0), stop=(kk == 31), waits=w0, tok=(kk == 31))
                    m7 = mm(PS[:, 3584:4096], vK[:, kk, :], PT[:, kk, 4:8, :],
                            start=(kk == 0), stop=(kk == 31), waits=[], tok=(kk == 31))
                bankR[6].set_w(m6)
                bankR[7].set_w(m7)
                ptR.add_r(m7)
                for i in range(32):
                    vKR[i].add_r(m7)
                for e in range(EA):
                    bnk = 6 if e < 4 else 7
                    coff = 3072 + (e % 4) * 128 if e < 4 else 3584 + (e % 4) * 128
                    tcx = vop("tensor_tensor",
                              waits=[m6 if e < 4 else m7, fbR[e].w]
                              + (cxpR[e].wr_deps() if qtl == 0 else []),
                              out=CXP[:, e, qtl * 128:(qtl + 1) * 128],
                              in0=PS[:, coff:coff + 128], in1=FBt[:, e, :], op=ALU.mult)
                    bankR[bnk].add_r(tcx)
                    fbR[e].add_r(tcx)
                    cxp_w_toks.append(tcx)
                    if qtl == 3:
                        cxpR[e].set_w(tcx)
                    else:
                        cxpR[e].add_r(tcx)

            for dc in range(DC):
                sl_o = dc % 2
                owt = []
                for e in range(EA):
                    owt.append(dma("scalar", owdc[:, sl_o, e, :], t["owc"][e][:, dc * 128:(dc + 1) * 128],
                                   waits=owdcR[sl_o].wr_deps() + (ow_toks if tb == 0 else [])))
                owdcR[sl_o].set_w(owt[-1])
                for tk_ in owt[:-1]:
                    owdcR[sl_o].add_r(tk_)
                bk_i = dc % 2
                w0 = ([cxpR[e].w for e in range(EA)] + cxp_w_toks
                      + bankR[bk_i].wr_deps() + [gATbR.w] + owt + [tob])
                mt = None
                for e in range(EA):
                    mt = mm(bank(bk_i), owdc[:, sl_o, e, :], CXP[:, e, :],
                            start=(e == 0), stop=False, waits=w0 if e == 0 else [], tok=False)
                mb = mm(bank(bk_i), obs[0:8, dc * 128:(dc + 1) * 128],
                        gATb[:], start=False, stop=True)
                bankR[bk_i].set_w(mb)
                gATbR.add_r(mb)
                owdcR[sl_o].add_r(mb)
                for e in range(EA):
                    cxpR[e].add_r(mb)
                ts1 = vop("scalar_tensor_tensor", waits=[mb, x1bR[dc].w] + soutR[dc % 2].wr_deps(),
                          out=SOUT[:, dc % 2, :], in0=x1b[:, dc, :], scalar=2.0,
                          in1=bank(bk_i), op0=ALU.mult, op1=ALU.add)
                soutR[dc % 2].set_w(ts1)
                bankR[bk_i].add_r(ts1)
                x1bR[dc].add_r(ts1)
                td1 = dma("gpsimd", t["scache"][tb, dc], SOUT[:, dc % 2, :], waits=[ts1])
                soutR[dc % 2].add_r(td1)
                sstore_toks.append(td1)
                ts2 = vop("tensor_tensor", waits=[mb, x1bR[dc].w] + x2oR[dc % 2].wr_deps(),
                          out=X2O[:, dc % 2, :], in0=x1b[:, dc, :], in1=bank(bk_i), op=ALU.add)
                x2oR[dc % 2].set_w(ts2)
                bankR[bk_i].add_r(ts2)
                x1bR[dc].add_r(ts2)
                td2 = dma("gpsimd", t["x2cache"][tb, dc], X2O[:, dc % 2, :], waits=[ts2])
                x2oR[dc % 2].add_r(td2)
                sstore_toks.append(td2)

            if dbg and tb == 0:
                dbg_dump("XN", XN[:], [xnR[c].w for c in range(DC)], xnR)
                dbg_dump("QE", QE[:], [qeR[e].w for e in range(EA)], qeR)
                dbg_dump("CXP", CXP[:], [cxpR[e].w for e in range(EA)], cxpR)
        if dbg:
            dbg_dump("KFD", kF[:], [kFR[i].w for i in range(NTB)], kFR)
            dbg_dump("VKD", vK[:], [vKR[i].w for i in range(32)], vKR)
            dbg_dump("GTSD", GTS[:], [gtsR[i].w for i in range(32)], gtsR)
        final_toks += sstore_toks


    if do_c:
        if do_b:
            phase_barrier()
        sbuf = pool_sbuf
        _pool_reset()
        ln2w = sbuf("ln2w", (128, DC), F32)
        ln2b_bf = sbuf("ln2b_bf", (128, DC), BF16)
        ln2b_f = sbuf("ln2b_f", (128, DC), F32)
        tkw2 = chunkload(ln2w, t["ln2_w"])
        tkb2 = chunkload(ln2b_f, t["ln2_b"])
        tlb2 = vop("tensor_copy", waits=tkb2, out=ln2b_bf[:], in_=ln2b_f[:])
        ones1x128f = sbuf("ones1x128f", (1, 128), F32)
        tof = k.op("gpsimd", lambda e: e.memset(ones1x128f[:], 1.0), sem=sG)
        CONSTF = CONST + [tof]

        fgws = sbuf("fgws", (128, DC, EF), BF16)
        bgate2 = sbuf("bgate2", (1, EF), BF16)
        b1c = sbuf("b1c", (128, EF * 4), F32)
        fb2s = sbuf("fb2s", (16, D), BF16)
        wstC = sbuf("wstC", (128, 1024), F32)
        wcvC = sbuf("wcvC", (128, 1024), BF16)
        rowC = sbuf("rowC", (1, 2 * FH), F32)
        wstCR = Res()
        wcvCR = Res()
        rowCR = Res()

        bias_mm = []
        for ci in range(DC):
            tl = dma("scalar", wstC[:, 0:EF], t["ff_gate_w"][ci * 128:(ci + 1) * 128, :],
                     waits=wstCR.wr_deps())
            wstCR.set_w(tl)
            tc = vop("tensor_copy", waits=[tl] + wcvCR.wr_deps(), out=wcvC[:, 0:EF], in_=wstC[:, 0:EF])
            wstCR.add_r(tc)
            wcvCR.set_w(tc)
            m1 = mm(PS[0:1, 3584:3584 + EF], ln2b_bf[:, ci:ci + 1], wcvC[:, 0:EF],
                    start=(ci == 0), stop=(ci == DC - 1),
                    waits=[tc, tlb2] + (bankR[7].wr_deps() if ci == 0 else []))
            bias_mm.append(m1)
            tg = vop("tensor_scalar", waits=[m1] + tkw2, out=fgws[:, ci, :], in0=wcvC[:, 0:EF],
                     scalar1=ln2w[:, ci:ci + 1], scalar2=None, op0=ALU.mult)
            wcvCR.add_r(tg)
        bankR[7].set_w(bias_mm[-1])
        tbg2 = vop("tensor_copy", waits=[bias_mm[-1]], out=bgate2[:], in_=PS[0:1, 3584:3584 + EF])
        bankR[7].add_r(tbg2)

        prepC_toks = [tbg2]
        for f in range(EF):
            bias_mm = []
            for ci in range(DC):
                tl = dma("scalar", wstC[:, 0:FH], t["ff_w1"][f, ci * 128:(ci + 1) * 128, :],
                         waits=wstCR.wr_deps())
                wstCR.set_w(tl)
                tc = vop("tensor_copy", waits=[tl] + wcvCR.wr_deps(), out=wcvC[:, 0:FH], in_=wstC[:, 0:FH])
                wstCR.add_r(tc)
                wcvCR.set_w(tc)
                m1 = mm(PS[0:1, 3584:3584 + FH], ln2b_bf[:, ci:ci + 1], wcvC[:, 0:FH],
                        start=(ci == 0), stop=(ci == DC - 1),
                        waits=[tc, tlb2] + (bankR[7].wr_deps() if ci == 0 else []))
                bias_mm.append(m1)
                tg = vop("tensor_scalar", waits=[m1] + tkw2, out=wcvC[:, 0:FH], in0=wcvC[:, 0:FH],
                         scalar1=ln2w[:, ci:ci + 1], scalar2=None, op0=ALU.mult)
                wcvCR.set_w(tg)
                tsv = dma("gpsimd", t["w1_c"][f, ci], wcvC[:, 0:FH], waits=[tg])
                wcvCR.add_r(tsv)
                prepC_toks.append(tsv)
            bankR[7].set_w(bias_mm[-1])
            tbl = dma("sync", rowC[:, 0:FH], t["ff_b1"][f].unsqueeze(0), waits=rowCR.wr_deps())
            tba = vop("tensor_tensor", waits=[tbl, bias_mm[-1]], out=rowC[:, FH:2 * FH],
                      in0=PS[0:1, 3584:3584 + FH], in1=rowC[:, 0:FH], op=ALU.add)
            rowCR.set_w(tba)
            bankR[7].add_r(tba)
            tst = dma("gpsimd", t["rowbounceC"][0:FH], rowC[0:1, FH:2 * FH], waits=[tba])
            rowCR.add_r(tst)
            for hb in range(4):
                tld = dma("gpsimd", b1c[:, f * 4 + hb: f * 4 + hb + 1],
                          t["rowbounceC"][hb * 128:(hb + 1) * 128], waits=[tst])
                prepC_toks.append(tld)
            for hc in range(4):
                tl = dma("scalar", wstC[:], t["ff_w2"][f, hc * 128:(hc + 1) * 128, :],
                         waits=wstCR.wr_deps())
                wstCR.set_w(tl)
                tc = vop("tensor_copy", waits=[tl] + wcvCR.wr_deps(), out=wcvC[:], in_=wstC[:])
                wstCR.add_r(tc)
                wcvCR.set_w(tc)
                tsv = dma("gpsimd", t["w2_c"][f, hc], wcvC[:], waits=[tc])
                wcvCR.add_r(tsv)
                prepC_toks.append(tsv)
        tfl = dma("scalar", wstC[0:16, :], t["ff_b2"][:, :], waits=wstCR.wr_deps())
        wstCR.set_w(tfl)
        tfb2 = vop("tensor_copy", waits=[tfl], out=fb2s[:], in_=wstC[0:16, :])
        wstCR.add_r(tfb2)
        prepC_toks.append(tfb2)

        X2B = sbuf("X2B", (128, DC, TB), BF16)
        x2bR = [Res() for _ in range(DC)]
        SB = sbuf("SB", (128, DC, TB), BF16)
        sbR = [Res() for _ in range(DC)]
        XN2 = sbuf("XN2", (128, DC, TB), BF16)
        xn2R = [Res() for _ in range(DC)]
        SQC = XN2
        sqcR = xn2R
        LGF2 = sbuf("LGF2", (16, TB), F32)
        lgf2R = Res()
        LGA2 = sbuf("LGA2", (128, 4, 16), F32)
        lga2R = [Res() for _ in range(4)]
        GTSF = sbuf("GTSF", (128, 4, 16), F32)
        gtsfR = [Res() for _ in range(4)]
        gATf = sbuf("gATf", (16, TB), BF16)
        gatfR = Res()
        gstg2 = sbuf("gstg2", (16, 128), BF16)
        gstg2R = Res()
        grow = sbuf("grow", (1, TB), BF16)
        growR = Res()
        wm1 = [sbuf("wm10", (128, DC, FH), BF16), sbuf("wm11", (128, DC, FH), BF16)]
        wm1R = [Res(), Res()]
        wm2 = [sbuf("wm20", (128, 4, D), BF16), sbuf("wm21", (128, 4, D), BF16)]
        wm2R = [Res(), Res()]
        HST = sbuf("HST", (128, 2, TB), BF16)
        hstR = [Res(), Res()]
        HP = sbuf("HP", (128, 8, 4, TB), BF16)
        hpR = [[Res() for _ in range(4)] for _ in range(8)]
        OFF = sbuf("OFF", (128, DC, TB), F32)
        offR = [Res() for _ in range(DC)]
        OSTG = sbuf("OSTG", (128, 1, D), F32)
        ostgR = [Res()]

        def load_cache(name, dst, dstR, tb, extra=()):
            toks = []
            for dc in range(DC):
                tk_ = dma("gpsimd", dst[:, dc, :], t[name][tb, dc],
                          waits=dstR[dc].wr_deps() + list(extra))
                dstR[dc].set_w(tk_)
                toks.append(tk_)
            return toks

        out_toks = []
        for tb in range(NTB):
            bwait = [final_toks[-1]] if (do_b and tb == 0) else []
            tx2 = load_cache("x2cache", X2B, x2bR, tb, extra=bwait)
            tsb = load_cache("scache", SB, sbR, tb, extra=bwait)
            for dc in range(DC):
                tsq = vop("tensor_tensor", eng="gpsimd", waits=[tx2[dc]] + sqcR[dc].wr_deps(),
                          out=SQC[:, dc, :], in0=X2B[:, dc, :], in1=X2B[:, dc, :], op=ALU.mult)
                sqcR[dc].set_w(tsq)
            m1 = m2 = None
            for dc in range(DC):
                m1 = mm(brow(4), onescol[:], X2B[:, dc, :], start=(dc == 0), stop=(dc == DC - 1),
                        waits=[tx2[dc]] + CONST + (bankR[4].wr_deps() if dc == 0 else []),
                        tok=(dc == DC - 1))
                m2 = mm(brow(5), onescol[:], SQC[:, dc, :], start=(dc == 0), stop=(dc == DC - 1),
                        waits=[sqcR[dc].w] + (bankR[5].wr_deps() if dc == 0 else []),
                        tok=(dc == DC - 1))
            bankR[4].set_w(m1)
            bankR[5].set_w(m2)
            for dc in range(DC):
                x2bR[dc].add_r(m1)
                sqcR[dc].add_r(m2)
            trb, tmr_ = ln_rows(4, 5, [m1, m2], True)
            tbc = mm(bank(6), ones1x128[:], rowbf[:, 0:TB], start=True, stop=True,
                     waits=[trb] + CONST + bankR[6].wr_deps())
            bankR[6].set_w(tbc)
            rowbfR.add_r(tbc)
            tbc7 = mm(bank(7), ones1x128[:], rowbf[:, TB:2 * TB], start=True, stop=True,
                      waits=[tmr_] + CONST + bankR[7].wr_deps())
            bankR[7].set_w(tbc7)
            rowbfR.add_r(tbc7)
            xn2_toks = []
            for dc in range(DC):
                t1_ = vop("tensor_tensor", waits=[tbc, x2bR[dc].w, sqcR[dc].readers[-1] if sqcR[dc].readers else None, m2],
                          out=XN2[:, dc, :], in0=X2B[:, dc, :], in1=bank(6), op=ALU.mult)
                x2bR[dc].add_r(t1_)
                bankR[6].add_r(t1_)
                t2_ = vop("tensor_tensor", waits=[tbc7, t1_],
                          out=XN2[:, dc, :], in0=XN2[:, dc, :], in1=bank(7), op=ALU.subtract)
                bankR[7].add_r(t2_)
                xn2R[dc].set_w(t2_)
                xn2_toks.append(t2_)
            mt = None
            for ci in range(DC):
                mt = mm(PS[0:16, 1024:1024 + TB], fgws[:, ci, :], XN2[:, ci, :],
                        start=(ci == 0), stop=False,
                        waits=(xn2_toks + prepC_toks + bankR[2].wr_deps()) if ci == 0 else [],
                        tok=False)
            mb = mm(PS[0:16, 1024:1024 + TB], bgate2[:], ones1x512[:], start=False, stop=True,
                    waits=CONSTF)
            bankR[2].set_w(mb)
            for dc in range(DC):
                xn2R[dc].add_r(mb)
            tlg = vop("tensor_copy", waits=[mb] + lgf2R.wr_deps(), out=LGF2[:],
                      in_=PS[0:16, 1024:1024 + TB])
            bankR[2].add_r(tlg)
            lgf2R.set_w(tlg)
            for tt in range(4):
                tpt = k.op("tensor",
                           lambda e, o=PS[:, 1536:1536 + 16], i=LGF2[0:16, tt * 128:(tt + 1) * 128]:
                           e.transpose(o, i, ident[0:16, 0:16]),
                           waits=[tlg, tid] + bankR[3].wr_deps(), sem=sT)
                bankR[3].set_w(tpt)
                lgf2R.add_r(tpt)
                tla = vop("tensor_copy", waits=[tpt] + lga2R[tt].wr_deps(),
                          out=LGA2[:, tt, :], in_=PS[:, 1536:1536 + 16])
                bankR[3].add_r(tla)
                lga2R[tt].set_w(tla)
                tg = gates_chain(LGA2[:, tt, :], GTSF[:, tt, :], EF, [tla], gtsfR[tt])
                tgt = k.op("tensor",
                           lambda e, o=PS[0:16, 1536 + 16:1536 + 16 + 128], i=GTSF[:, tt, :]:
                           e.transpose(o, i, ident[:]),
                           waits=[tg, tid] + bankR[3].wr_deps(), sem=sT)
                bankR[3].set_w(tgt)
                gtsfR[tt].add_r(tgt)
                tge = vop("tensor_copy", waits=[tgt] + (gatfR.wr_deps() if tt == 0 else []),
                          out=gATf[0:16, tt * 128:(tt + 1) * 128],
                          in_=PS[0:16, 1536 + 16:1536 + 16 + 128])
                bankR[3].add_r(tge)
                if tt == 3:
                    gatfR.set_w(tge)
            tgd = dma("gpsimd", t["gatfd"], gATf[:], waits=[gatfR.w])
            gatfR.add_r(tgd)

            for ph in range(2):
                hp_toks = []
                for f8 in range(8):
                    f = ph * 8 + f8
                    slot = f % 2
                    w1toks = []
                    deps = wm1R[slot].wr_deps()
                    for ci in range(DC):
                        w1toks.append(dma("scalar", wm1[slot][:, ci, :], t["w1_c"][f, ci],
                                          waits=deps + (prepC_toks if tb == 0 else [])))
                    wm1R[slot].set_w(w1toks[-1])
                    for tk_ in w1toks[:-1]:
                        wm1R[slot].add_r(tk_)
                    tgr = dma("gpsimd", grow[:], t["gatfd"][f], waits=[tgd] + growR.wr_deps())
                    growR.set_w(tgr)
                    tgb = mm(bank(5), ones1x128[:], grow[:], start=True, stop=True,
                             waits=[tgr] + CONST + bankR[5].wr_deps())
                    bankR[5].set_w(tgb)
                    growR.add_r(tgb)
                    for hb in range(4):
                        bk_i = 2 + hb % 2
                        mt = None
                        for ci in range(DC):
                            mt = mm(bank(bk_i), wm1[slot][:, ci, hb * 128:(hb + 1) * 128],
                                    XN2[:, ci, :], start=(ci == 0), stop=(ci == DC - 1),
                                    waits=(w1toks + xn2_toks + bankR[bk_i].wr_deps()) if ci == 0 else [],
                                    tok=(ci == DC - 1))
                        bankR[bk_i].set_w(mt)
                        wm1R[slot].add_r(mt)
                        for dc in range(DC):
                            xn2R[dc].add_r(mt)
                        th = act(HST[:, hb % 2, :], bank(bk_i), AF.Relu,
                                 bias=b1c[:, f * 4 + hb:f * 4 + hb + 1],
                                 waits=[mt] + hstR[hb % 2].wr_deps())
                        bankR[bk_i].add_r(th)
                        hstR[hb % 2].set_w(th)
                        tp = vop("tensor_tensor", waits=[th, tgb] + hpR[f8][hb].wr_deps(),
                                 out=HP[:, f8, hb, :], in0=HST[:, hb % 2, :], in1=bank(5),
                                 op=ALU.mult)
                        hstR[hb % 2].add_r(tp)
                        bankR[5].add_r(tp)
                        hpR[f8][hb].set_w(tp)
                        hp_toks.append(tp)
                lastm = [None] * 8
                for f8 in range(8):
                    f = ph * 8 + f8
                    slot = f % 2
                    w2toks = []
                    deps = wm2R[slot].wr_deps()
                    for hc in range(4):
                        w2toks.append(dma("scalar", wm2[slot][:, hc, :], t["w2_c"][f, hc],
                                          waits=deps + (prepC_toks if tb == 0 else [])))
                    wm2R[slot].set_w(w2toks[-1])
                    for tk_ in w2toks[:-1]:
                        wm2R[slot].add_r(tk_)
                    for dc in range(DC):
                        first = (f8 == 0)
                        w0 = []
                        if first:
                            w0 = hp_toks + bankR[dc].wr_deps()
                        if dc == 0:
                            w0 = w0 + w2toks
                        mt = None
                        for hc in range(4):
                            last_mm = (f8 == 7 and hc == 3 and ph == 1 and False)
                            mt = mm(bank(dc), wm2[slot][:, hc, dc * 128:(dc + 1) * 128],
                                    HP[:, f8, hc, :],
                                    start=(f8 == 0 and hc == 0), stop=False,
                                    waits=w0 if hc == 0 else [], tok=(hc == 3))
                        lastm[dc] = mt
                        wm2R[slot].add_r(mt)
                        for hc in range(4):
                            hpR[f8][hc].add_r(mt)
                for dc in range(DC):
                    if ph == 1:
                        mfin = mm(bank(dc), fb2s[0:16, dc * 128:(dc + 1) * 128], gATf[:],
                                  start=False, stop=True, waits=[lastm[dc], gatfR.w, tfb2])
                        gatfR.add_r(mfin)
                    else:
                        mfin = lastm[dc]
                    bankR[dc].set_w(mfin)
                    tse = vop("tensor_tensor", waits=[mfin, sbR[dc].w, tsb[dc]],
                              out=SB[:, dc, :], in0=SB[:, dc, :], in1=bank(dc), op=ALU.add)
                    bankR[dc].add_r(tse)
                    sbR[dc].set_w(tse)

            for dc in range(DC):
                tsq = vop("tensor_tensor", eng="gpsimd", waits=[sbR[dc].w] + sqcR[dc].wr_deps(),
                          out=SQC[:, dc, :], in0=SB[:, dc, :], in1=SB[:, dc, :], op=ALU.mult)
                sqcR[dc].set_w(tsq)
                sbR[dc].add_r(tsq)
            m1 = m2 = None
            for dc in range(DC):
                m1 = mm(brow(4), onescol[:], SB[:, dc, :], start=(dc == 0), stop=(dc == DC - 1),
                        waits=[sbR[dc].w] + (bankR[4].wr_deps() if dc == 0 else []),
                        tok=(dc == DC - 1))
                m2 = mm(brow(5), onescol[:], SQC[:, dc, :], start=(dc == 0), stop=(dc == DC - 1),
                        waits=[sqcR[dc].w] + (bankR[5].wr_deps() if dc == 0 else []),
                        tok=(dc == DC - 1))
            bankR[4].set_w(m1)
            bankR[5].set_w(m2)
            for dc in range(DC):
                sbR[dc].add_r(m1)
                sqcR[dc].add_r(m2)
            trb, tmr_ = ln_rows(4, 5, [m1, m2], True)
            tbc = mm(bank(6), ones1x128f[:], rowsA[:, TB:2 * TB], start=True, stop=True,
                     waits=[trb, tmr_] + CONSTF + bankR[6].wr_deps())
            bankR[6].set_w(tbc)
            rowsR.add_r(tbc)
            tbc7 = mm(bank(7), ones1x128f[:], rowsA[:, 2 * TB:3 * TB], start=True, stop=True,
                      waits=[tmr_] + CONSTF + bankR[7].wr_deps())
            bankR[7].set_w(tbc7)
            rowsR.add_r(tbc7)
            off_toks = []
            for dc in range(DC):
                t1_ = vop("tensor_tensor", waits=[tbc, sbR[dc].w] + offR[dc].wr_deps(),
                          out=OFF[:, dc, :], in0=SB[:, dc, :], in1=bank(6), op=ALU.mult)
                sbR[dc].add_r(t1_)
                bankR[6].add_r(t1_)
                t2_ = vop("tensor_tensor", waits=[tbc7, t1_],
                          out=OFF[:, dc, :], in0=OFF[:, dc, :], in1=bank(7), op=ALU.subtract)
                bankR[7].add_r(t2_)
                offR[dc].set_w(t2_)
                off_toks.append(t2_)
            for tt in range(4):
                sl = 0
                ev_toks = []
                for dc in range(DC):
                    bnk = dc % 2
                    tptr = k.op("tensor",
                                lambda e, o=PS[:, bnk * 512:bnk * 512 + 128],
                                i=OFF[:, dc, tt * 128:(tt + 1) * 128]:
                                e.transpose(o, i, ident[:]),
                                waits=[off_toks[dc], tid] + bankR[bnk].wr_deps(), sem=sT)
                    bankR[bnk].set_w(tptr)
                    offR[dc].add_r(tptr)
                    tev = vop("tensor_copy", waits=[tptr] + (ostgR[sl].wr_deps() if dc == 0 else []),
                              out=OSTG[:, sl, dc * 128:(dc + 1) * 128],
                              in_=PS[:, bnk * 512:bnk * 512 + 128])
                    bankR[bnk].add_r(tev)
                    ev_toks.append(tev)
                ostgR[sl].set_w(ev_toks[-1])
                for tk_ in ev_toks[:-1]:
                    ostgR[sl].add_r(tk_)
                tdo = dma("sync", t["out"][(tb * 4 + tt) * 128:(tb * 4 + tt + 1) * 128, :],
                          OSTG[:, sl, :], waits=ev_toks)
                ostgR[sl].add_r(tdo)
                out_toks.append(tdo)
        final_toks += out_toks

    k.ops["gpsimd"].append((lambda e: None, [tk for tk in final_toks], None, 0))

    k.finalize()
    return nc, t


def _ins_a(inputs, core):
    return {
        "x": np.ascontiguousarray(inputs["L0_attn_x"][core]),
        "Wk": inputs["L0_attn_Wk"], "bk": inputs["L0_attn_bk"],
        "Wv": inputs["L0_attn_Wv"], "bv": inputs["L0_attn_bv"],
        "Wq": inputs["L0_attn_Wq"], "bq": inputs["L0_attn_bq"],
        "Wkp": inputs["L0_attn_Wkp"], "bkp": inputs["L0_attn_bkp"],
        "Wqp": inputs["L0_attn_Wqp"], "bqp": inputs["L0_attn_bqp"],
        "phase_scale": inputs["L0_attn_phase_scale"],
        "ln_g": inputs["L0_attn_ln_g"], "ln_b": inputs["L0_attn_ln_b"],
        "Wo": inputs["L0_attn_Wo"], "bo": inputs["L0_attn_bo"],
    }


def _ins_b(inputs):
    return {
        "attn_gate_w": inputs["L0_ffn_attn_gate_w"],
        "q_w": inputs["L0_ffn_q_w"], "q_b": inputs["L0_ffn_q_b"],
        "k_w": inputs["L0_ffn_k_w"], "k_b": inputs["L0_ffn_k_b"],
        "v_w": inputs["L0_ffn_v_w"], "v_b": inputs["L0_ffn_v_b"],
        "o_w": inputs["L0_ffn_o_w"], "o_b": inputs["L0_ffn_o_b"],
        "rel_table": inputs["L0_ffn_rel_table"],
        "ln1_w": inputs["L0_ffn_ln1_w"], "ln1_b": inputs["L0_ffn_ln1_b"],
    }


def _ins_c(inputs):
    return {
        "ff_gate_w": inputs["L0_ffn_ff_gate_w"],
        "ff_w1": inputs["L0_ffn_ff_w1"], "ff_b1": inputs["L0_ffn_ff_b1"],
        "ff_w2": inputs["L0_ffn_ff_w2"], "ff_b2": inputs["L0_ffn_ff_b2"],
        "ln2_w": inputs["L0_ffn_ln2_w"], "ln2_b": inputs["L0_ffn_ln2_b"],
    }


def kernel(**inputs):
    cores = list(range(8))
    nc, _ = build_program("ALL")
    im = []
    for c in cores:
        m = _ins_a(inputs, c)
        m.update(_ins_b(inputs))
        m.update(_ins_c(inputs))
        im.append(m)
    res = run_bass_kernel_spmd(nc, im, core_ids=cores)
    return np.stack([res.results[c]["out"] for c in cores])


# version 07
# speedup vs baseline: 1.1745x; 1.1745x over previous
import contextlib
import math

import numpy as np
import ml_dtypes

import concourse.bass as bass
import concourse.mybir as mybir
from concourse.bass import AP
from concourse.bass_utils import run_bass_kernel_spmd

F32 = mybir.dt.float32
BF16 = mybir.dt.bfloat16
I32 = mybir.dt.int32
AF = mybir.ActivationFunctionType
ALU = mybir.AluOpType
BF = ml_dtypes.bfloat16

NT = 4096
D = 1024
DC = 8
TB = 512
NTB = NT // TB
CHUNK = 64
HD = 128
EA, EF, FH = 8, 16, 512
MAXP = 64
EPS = 1e-5
INV2PI = 1.0 / (2.0 * math.pi)
TWOPI = 2.0 * math.pi

ENGINES = ("tensor", "vector", "scalar", "gpsimd", "sync")


class Sem:
    __slots__ = ("h", "n", "name")

    def __init__(self, h, name):
        self.h = h
        self.n = 0
        self.name = name


class Tok:
    __slots__ = ("sem", "val")

    def __init__(self, sem, val):
        self.sem = sem
        self.val = val


class Res:
    __slots__ = ("w", "readers")

    def __init__(self):
        self.w = None
        self.readers = []

    def wr_deps(self):
        return ([self.w] if self.w else []) + self.readers

    def rd_deps(self):
        return [self.w] if self.w else []

    def set_w(self, tok):
        self.w = tok
        self.readers = []

    def add_r(self, tok):
        if tok is not None:
            self.readers.append(tok)


class KB:

    def __init__(self, nc):
        self.nc = nc
        self.ops = {e: [] for e in ENGINES}
        self.stack = contextlib.ExitStack()
        self.n_inst = 0

    def sem(self, name):
        return Sem(self.stack.enter_context(self.nc.semaphore(name)), name)

    def op(self, eng, fn, waits=(), sem=None, inc=1):
        ws = [w for w in waits if w is not None]
        tok = None
        if sem is not None:
            sem.n += inc
            tok = Tok(sem, sem.n)
        self.ops[eng].append((fn, ws, sem, inc))
        self.n_inst += 1
        return tok

    def finalize(self):
        with self.nc.Block() as block:
            for ename in ENGINES:
                lst = self.ops[ename]
                if not lst:
                    continue

                def make(lst, ename=ename):
                    def emit(eng):
                        maxw = {}
                        for idx, (fn, ws, sem, inc) in enumerate(lst):
                            for tok in ws:
                                if maxw.get(id(tok.sem), -1) < tok.val:
                                    eng.wait_ge(tok.sem.h, tok.val)
                                    maxw[id(tok.sem)] = tok.val
                            try:
                                inst = fn(eng)
                            except Exception as ex:
                                raise RuntimeError(f"emit failed: engine={ename} opidx={idx}: {ex}") from ex
                            if sem is not None and inst is not None:
                                inst.then_inc(sem.h, inc)
                    return emit

                getattr(block, ename)(make(lst))
        self.stack.close()


def build_program(phase, dbg=False):
    nc = bass.Bass("TRN2", target_bir_lowering=False)
    k = KB(nc)
    t = {}

    def dram(name, shape, dtype, kind):
        t[name] = nc.dram_tensor(name, shape, dtype, kind=kind).ap()
        return t[name]

    def sbuf(name, shape, dtype):
        t[name] = k.stack.enter_context(nc.sbuf_tensor(name, list(shape), dtype))
        return t[name]

    do_a = phase in ("A", "ALL")
    do_b = phase in ("B", "ALL")
    do_c = phase in ("C", "ALL")
    inner = "Internal"

    if do_a:
        dram("x", [NT, D], F32, "ExternalInput")
        for nm in ("Wk", "Wv", "Wq", "Wkp", "Wqp", "Wo"):
            dram(nm, [D, D], F32, "ExternalInput")
        for nm in ("bk", "bv", "bq", "bkp", "bqp", "phase_scale", "ln_g", "ln_b", "bo"):
            dram(nm, [D], F32, "ExternalInput")
        dram("wcacheA", [6, 128, DC * D], BF16, inner)
    dram("x1cache", [NTB, 128, DC * TB], BF16,
         "ExternalOutput" if phase == "A" else ("ExternalInput" if phase in ("B",) else inner))
    if do_b:
        dram("attn_gate_w", [D, EA], F32, "ExternalInput")
        dram("q_w", [EA, D, HD], F32, "ExternalInput")
        dram("q_b", [EA, HD], F32, "ExternalInput")
        dram("k_w", [D, HD], F32, "ExternalInput")
        dram("k_b", [HD], F32, "ExternalInput")
        dram("v_w", [D, HD], F32, "ExternalInput")
        dram("v_b", [HD], F32, "ExternalInput")
        dram("o_w", [EA, HD, D], F32, "ExternalInput")
        dram("o_b", [EA, D], F32, "ExternalInput")
        dram("rel_table", [2 * MAXP + 1, HD], F32, "ExternalInput")
        dram("ln1_w", [D], F32, "ExternalInput")
        dram("ln1_b", [D], F32, "ExternalInput")
        dram("wq_c", [EA, 128, DC * HD], BF16, inner)
        dram("Rimg", [4, 128, 640], BF16, inner)
        dram("rowbounceB", [D], F32, inner)
        dram("owc", [DC, 128, EA * 128], BF16, inner)
        dram("rows1d", [2, NT], BF16, inner)
        dram("gatc", [8, NT], BF16, inner)
    if do_b or do_c:
        kind_s = inner if phase == "ALL" else ("ExternalOutput" if phase == "B" else "ExternalInput")
        dram("scache", [NTB, 128, DC * TB], BF16, kind_s)
        dram("x2cache", [NTB, 128, DC * TB], BF16, kind_s)
    if do_c:
        dram("ff_gate_w", [D, EF], F32, "ExternalInput")
        dram("ff_w1", [EF, D, FH], F32, "ExternalInput")
        dram("ff_b1", [EF, FH], F32, "ExternalInput")
        dram("ff_w2", [EF, FH, D], F32, "ExternalInput")
        dram("ff_b2", [EF, D], F32, "ExternalInput")
        dram("ln2_w", [D], F32, "ExternalInput")
        dram("ln2_b", [D], F32, "ExternalInput")
        dram("w1_c", [EF, 128, DC * FH], BF16, inner)
        dram("w2_c", [EF, 128, 4 * D], BF16, inner)
        dram("gatfd", [EF, TB], BF16, inner)
        dram("out", [NT, D], F32, "ExternalOutput")

    sT, sV, sS, sG = k.sem("sT"), k.sem("sV"), k.sem("sS"), k.sem("sG")
    sDs, sDa, sDg = k.sem("sDs"), k.sem("sDa"), k.sem("sDg")
    esem = dict(tensor=sT, vector=sV, scalar=sS, gpsimd=sG)
    qsem = dict(sync=sDs, scalar=sDa, gpsimd=sDg)

    def dma(queue, out, in_, waits=()):
        return k.op(queue, lambda e, o=out, i=in_: e.dma_start(out=o, in_=i),
                    waits=waits, sem=qsem[queue], inc=16)

    def xpose_dma(out, in_, waits=()):
        return k.op("sync", lambda e, o=out, i=in_: e.dma_start_transpose(o, i),
                    waits=waits, sem=sDs, inc=16)

    def mm(out, lhsT, rhs, start, stop, waits=(), tok=True):
        return k.op("tensor",
                    lambda e, o=out, l=lhsT, r=rhs, st=start, sp=stop:
                    e.matmul(o, l, r, start=st, stop=sp),
                    waits=waits, sem=sT if tok else None)

    def act(out, in_, func, bias=None, scale=None, accum=None, waits=()):
        kw = {}
        if bias is not None:
            kw["bias"] = bias
        if scale is not None:
            kw["scale"] = scale
        if accum is not None:
            kw["accum_out"] = accum
        return k.op("scalar",
                    lambda e, o=out, i=in_, f=func, kw=kw: e.activation(out=o, in_=i, func=f, **kw),
                    waits=waits, sem=sS)

    def vop(name, eng="vector", waits=(), **kw):
        return k.op(eng, lambda e, kw=kw, n=name: getattr(e, n)(**kw),
                    waits=waits, sem=esem[eng])

    def vfence(tok):
        return k.op("vector", lambda e: e.drain(), waits=[tok], sem=sV)

    def sfence(tok):
        return k.op("scalar", lambda e: e.drain(), waits=[tok], sem=sS)

    poolBF = k.stack.enter_context(nc.sbuf_tensor("poolBF", [128, 80896], BF16))
    poolF = k.stack.enter_context(nc.sbuf_tensor("poolF", [128, 9216], F32))
    poolI = k.stack.enter_context(nc.sbuf_tensor("poolI", [128, 512], I32))
    _pool_off = {"bf": 0, "f32": 0, "i32": 0}

    def _pool_reset():
        _pool_off["bf"] = _pool_off["f32"] = _pool_off["i32"] = 0

    def pool_sbuf(name, shape, dtype):
        import functools, operator
        n = functools.reduce(operator.mul, shape[1:], 1)
        if dtype == BF16:
            key, pool, esz = "bf", poolBF, 2
        elif dtype == F32:
            key, pool, esz = "f32", poolF, 4
        else:
            key, pool, esz = "i32", poolI, 4
        off = _pool_off[key]
        off = (off + 15) & ~15
        assert off + n <= pool.shape[1], f"pool {key} overflow at {name}: {off}+{n}"
        _pool_off[key] = off + n
        ap = pool[:, off:off + n]
        if shape[0] != 128:
            ap = pool[0:shape[0], off:off + n]
        if len(shape) == 3:
            ap = ap.rearrange("p (a b) -> p a b", a=shape[1], b=shape[2])
        elif len(shape) == 4:
            ap = ap.rearrange("p (a b c) -> p a b c", a=shape[1], b=shape[2], c=shape[3])
        t[name] = ap
        return ap

    def phase_barrier():
        toks = [Tok(x, x.n) for x in (sT, sV, sS, sG, sDs, sDa, sDg) if x.n > 0]
        for eng in ENGINES:
            k.op(eng, lambda e: None, waits=toks)

    ident = sbuf("ident", (128, 128), F32)
    ones1x128 = sbuf("ones1x128", (1, 128), BF16)
    ones1x512 = sbuf("ones1x512", (1, 512), BF16)
    onescol = sbuf("onescol", (128, 1), BF16)

    cst = []
    cst.append(k.op("gpsimd", lambda e: e.memset(ones1x128[:], 1.0), sem=sG))
    cst.append(k.op("gpsimd", lambda e: e.memset(ones1x512[:], 1.0), sem=sG))
    cst.append(k.op("gpsimd", lambda e: e.memset(onescol[:], 1.0), sem=sG))
    cvec = sbuf("cvec", (128, 2), F32)
    cst.append(k.op("gpsimd", lambda e: e.memset(cvec[:, 0:1], math.pi / 2.0), sem=sG))
    cst.append(k.op("gpsimd", lambda e: e.memset(cvec[:, 1:2], EPS), sem=sG))
    HALFPI_AP = cvec[:, 0:1]
    EPS_AP1 = cvec[0:1, 1:2]
    tio = k.op("gpsimd", lambda e: e.iota(ident[:], [[1, 128]], base=0, channel_multiplier=-1,
                                          allow_small_or_imprecise_dtypes=True), sem=sG)
    tid = vop("tensor_scalar", waits=[tio], out=ident[:], in0=ident[:],
              scalar1=0.0, scalar2=None, op0=ALU.is_equal)
    cst.append(tid)
    CONST = cst

    PS = k.stack.enter_context(nc.psum_tensor("PS", [128, 4096], F32))
    bankR = [Res() for _ in range(8)]

    rowsA = sbuf("rowsA", (1, 3 * TB), F32)
    rowbf = sbuf("rowbf", (1, 2 * TB), BF16)
    rowsR = Res()
    rowbfR = Res()

    def bank(i):
        return PS[:, i * 512:(i + 1) * 512]

    def brow(i):
        return PS[0:1, i * 512:(i + 1) * 512]

    final_toks = []

    def dbg_dump(name, ap, waits, res_list):
        if not dbg:
            return
        o = dram("dbg_" + name, list(ap.shape), ap.dtype, "ExternalOutput")
        tk = dma("gpsimd", o, ap, waits=waits)
        for r in res_list:
            r.add_r(tk)
        final_toks.append(tk)

    zfn = sbuf("zfn", (1, 4), F32)
    zfnR = Res()
    gsc = sbuf("gsc", (128, 32), F32)
    gscR = Res()
    geq = sbuf("geq", (128, 3, 4, 16), F32)
    geqR = Res()

    def gates_chain(lg_ap, out_ap, ne, nt4, waits, out_resses):
        d0 = gscR.wr_deps() + geqR.wr_deps()
        m1 = gsc[:, 0:nt4]
        m2 = gsc[:, 4:4 + nt4]
        dd = gsc[:, 8:8 + nt4]
        ed = gsc[:, 12:12 + nt4]
        p1_ = gsc[:, 16:16 + nt4]
        g1 = gsc[:, 20:20 + nt4]
        g2 = gsc[:, 24:24 + nt4]
        eq1 = geq[:, 0, :, 0:ne]
        eq2 = geq[:, 1, :, 0:ne]
        tmp = geq[:, 2, :, 0:ne]

        def bc(ap):
            return ap.unsqueeze(2).broadcast_to((128, nt4, ne))

        tm1 = vop("tensor_reduce", waits=list(waits) + d0, out=m1, in_=lg_ap,
                  axis=mybir.AxisListType.X, op=ALU.max)
        te1 = vop("tensor_tensor", waits=[tm1], out=eq1, in0=lg_ap, in1=bc(m1), op=ALU.is_equal)
        tlm = vop("scalar_tensor_tensor", waits=[te1], out=eq2,
                  in0=eq1, scalar=-1e30, in1=lg_ap, op0=ALU.mult, op1=ALU.add)
        tm2 = vop("tensor_reduce", waits=[tlm], out=m2, in_=eq2,
                  axis=mybir.AxisListType.X, op=ALU.max)
        te2 = vop("tensor_tensor", waits=[tm2], out=eq2, in0=lg_ap, in1=bc(m2), op=ALU.is_equal)
        td = vop("tensor_tensor", waits=[tm2], out=dd, in0=m2, in1=m1, op=ALU.subtract)
        tex = act(ed, dd, AF.Exp, waits=[td])
        tp1 = vop("tensor_scalar", waits=[tex], out=p1_, in0=ed,
                  scalar1=1.0, scalar2=None, op0=ALU.add)
        tg1 = vop("reciprocal", waits=[tp1], out=g1, in_=p1_)
        tg2 = vop("tensor_scalar", waits=[tg1], out=g2, in0=g1,
                  scalar1=-1.0, scalar2=1.0, op0=ALU.mult, op1=ALU.add)
        ta = vop("tensor_tensor", waits=[tg1, te1], out=tmp, in0=eq1, in1=bc(g1), op=ALU.mult)
        tb_ = vop("tensor_tensor", waits=[tg2, te2], out=eq2, in0=eq2, in1=bc(g2), op=ALU.mult)
        od = []
        for r in out_resses:
            od += r.wr_deps()
        tfin = vop("tensor_tensor", waits=[ta, tb_] + od, out=out_ap, in0=tmp, in1=eq2, op=ALU.add)
        tvd = k.op("vector", lambda e: e.drain(), waits=[tfin], sem=sV)
        gscR.set_w(tvd)
        geqR.set_w(tvd)
        for r in out_resses:
            r.set_w(tvd)
        return tvd

    def chunkload(dst, src, waits=()):
        toks = []
        for c in range(DC):
            toks.append(dma("sync", dst[:, c:c + 1], src[c * 128:(c + 1) * 128], waits=waits))
        return toks

    def ln_rows(sum_bank, sq_bank, waits, need_mu):
        deps = list(waits) + rowsR.wr_deps()
        tmu = vop("tensor_scalar", waits=deps, out=rowsA[:, 0:TB], in0=brow(sum_bank),
                  scalar1=1.0 / D, scalar2=None, op0=ALU.mult)
        tm2 = vop("tensor_tensor", waits=[tmu], out=rowsA[:, 2 * TB:3 * TB],
                  in0=rowsA[:, 0:TB], in1=rowsA[:, 0:TB], op=ALU.mult)
        tvar = vop("scalar_tensor_tensor", waits=[tm2], out=rowsA[:, TB:2 * TB],
                   in0=brow(sq_bank), scalar=1.0 / D, in1=rowsA[:, 2 * TB:3 * TB],
                   op0=ALU.mult, op1=ALU.subtract)
        bankR[sum_bank].add_r(tmu)
        bankR[sq_bank].add_r(tvar)
        tln = act(rowsA[:, TB:2 * TB], rowsA[:, TB:2 * TB], AF.Ln, bias=EPS_AP1, waits=[tvar] + CONST)
        tex = act(rowsA[:, TB:2 * TB], rowsA[:, TB:2 * TB], AF.Exp, scale=-0.5, waits=[tln])
        deps2 = rowbfR.wr_deps()
        trb = vop("tensor_copy", waits=[tex] + deps2, out=rowbf[:, 0:TB],
                  in_=rowsA[:, TB:2 * TB])
        tmr = None
        if need_mu:
            tmm = vop("tensor_tensor", waits=[tex, tmu], out=rowsA[:, 2 * TB:3 * TB],
                      in0=rowsA[:, 0:TB], in1=rowsA[:, TB:2 * TB], op=ALU.mult)
            tmr = vop("tensor_copy", waits=[tmm] + deps2, out=rowbf[:, TB:2 * TB],
                      in_=rowsA[:, 2 * TB:3 * TB])
        tsd = k.op("scalar", lambda e: e.drain(), waits=[tex], sem=sS)
        tvd = k.op("vector", lambda e: e.drain(), waits=[tsd, tmr if tmr else trb], sem=sV)
        rowsR.set_w(tvd)
        rowbfR.set_w(tvd)
        return tvd, (tvd if need_mu else None)

    def ln_rows_f32(sum_bank, sq_bank, waits):
        return ln_rows(sum_bank, sq_bank, waits, True)

    if do_a:
        sbuf = pool_sbuf
        _pool_reset()
        scanmask = sbuf("scanmask", (128, TB), BF16)
        tmsk = k.op("gpsimd", lambda e: e.memset(scanmask[:], 1.0), sem=sG)
        for cs in range(0, TB, CHUNK):
            tmsk = k.op("gpsimd", lambda e, c=cs: e.memset(scanmask[:, c:c + 1], 0.0),
                        waits=[tmsk], sem=sG)
        CONST = CONST + [tmsk]
        biasA = sbuf("biasA", (128, 8 * DC), F32)
        BK, BV32, BQ, BKP, BQP, PSC, PS2PI, LNG = range(8)

        def bcol(idx, c):
            return biasA[:, idx * DC + c: idx * DC + c + 1]

        wstage = sbuf("wstage", (128, 2, D), F32)
        wconv = sbuf("wconv", (128, 2, D), BF16)
        lnbch = sbuf("lnbch", (128, DC), BF16)
        lnb_f = sbuf("lnb_f", (128, DC), F32)
        btrow = sbuf("btrow", (1, D), BF16)

        tb_bk = chunkload(biasA[:, BK * DC:], t["bk"])
        tb_bq = chunkload(biasA[:, BQ * DC:], t["bq"])
        tb_bkp = chunkload(biasA[:, BKP * DC:], t["bkp"])
        tb_bqp = chunkload(biasA[:, BQP * DC:], t["bqp"])
        tb_ps = chunkload(biasA[:, PSC * DC:], t["phase_scale"])
        tb_lng = chunkload(biasA[:, LNG * DC:], t["ln_g"])
        tb_bv = chunkload(biasA[:, BV32 * DC:], t["bv"])
        tv1 = vop("tensor_scalar", waits=tb_bv, out=biasA[:, BV32 * DC:(BV32 + 1) * DC],
                  in0=biasA[:, BV32 * DC:(BV32 + 1) * DC], scalar1=1.0 / 32.0, scalar2=None, op0=ALU.mult)
        tv2 = vop("tensor_scalar", waits=tb_ps, out=biasA[:, PS2PI * DC:(PS2PI + 1) * DC],
                  in0=biasA[:, PSC * DC:(PSC + 1) * DC], scalar1=INV2PI, scalar2=None, op0=ALU.mult)
        tb_lnb = chunkload(lnb_f, t["ln_b"])
        t_lnb = vop("tensor_copy", waits=tb_lnb, out=lnbch[:], in_=lnb_f[:])

        biasA_rdy = {BK: tb_bk, BQ: tb_bq, BKP: tb_bkp, BQP: tb_bqp,
                     PSC: tb_ps, LNG: tb_lng, BV32: [tv1], PS2PI: [tv2]}

        wnames = ["Wk", "Wv", "Wq", "Wkp", "Wqp", "Wo"]
        prepA_tok = [None] * 6
        conv_hist = [None, None]
        store_hist = [None, None]
        bt_mms = []
        prev_store = None
        for wi, wn in enumerate(wnames):
            for ci in range(DC):
                sl_ = (wi * DC + ci) % 2
                tl = dma("scalar" if sl_ == 0 else "sync",
                         wstage[:, sl_, :], t[wn][ci * 128:(ci + 1) * 128, :],
                         waits=[conv_hist[sl_]] if conv_hist[sl_] else [])
                waits_c = [tl] + ([store_hist[sl_]] if store_hist[sl_] else [])
                tc = vop("tensor_copy", eng="gpsimd", waits=waits_c, out=wconv[:, sl_, :], in_=wstage[:, sl_, :])
                conv_hist[sl_] = tc
                src_tok = tc
                if wn == "Wo":
                    m1 = mm(brow(4), lnbch[:, ci:ci + 1], wconv[:, sl_, 0:512],
                            start=(ci == 0), stop=(ci == DC - 1),
                            waits=[tc, t_lnb] + bankR[4].wr_deps() if ci == 0 else [tc, t_lnb])
                    m2 = mm(brow(5), lnbch[:, ci:ci + 1], wconv[:, sl_, 512:1024],
                            start=(ci == 0), stop=(ci == DC - 1),
                            waits=[tc] + (bankR[5].wr_deps() if ci == 0 else []))
                    bt_mms += [m1, m2]
                    src_tok = vop("tensor_scalar", eng="gpsimd", waits=[m1, m2] + biasA_rdy[LNG],
                                  out=wconv[:, sl_, :], in0=wconv[:, sl_, :], scalar1=bcol(LNG, ci),
                                  scalar2=None, op0=ALU.mult)
                    conv_hist[sl_] = src_tok
                prev_store = dma("gpsimd", t["wcacheA"][wi][:, ci * D:(ci + 1) * D],
                                 wconv[:, sl_, :], waits=[src_tok])
                store_hist[sl_] = prev_store
                prepA_tok[wi] = prev_store
        bankR[4].set_w(bt_mms[-2])
        bankR[5].set_w(bt_mms[-1])
        tbt = None
        for hb in range(2):
            tb_bo = dma("sync", rowsA[:, 0:TB], t["bo"][hb * 512:(hb + 1) * 512].unsqueeze(0),
                        waits=rowsR.wr_deps())
            rowsR.set_w(tb_bo)
            tbt1 = vop("tensor_tensor", waits=[bankR[4 + hb].w, tb_bo],
                       out=rowsA[:, TB:2 * TB], in0=brow(4 + hb), in1=rowsA[:, 0:TB],
                       op=ALU.add)
            rowsR.set_w(tbt1)
            bankR[4 + hb].add_r(tbt1)
            tbt = vop("tensor_copy", waits=[tbt1], out=btrow[:, hb * 512:(hb + 1) * 512],
                      in_=rowsA[:, TB:2 * TB])
            rowsR.add_r(tbt)
            tbt = vfence(tbt)

        wbuf = [sbuf("wbuf0", (128, DC, D), BF16), sbuf("wbuf1", (128, DC, D), BF16)]
        wbufR = [Res(), Res()]
        xinf = sbuf("xinf", (128, 2, D), F32)
        xinfR = [Res(), Res()]
        xinb = sbuf("xinb", (128, 2, D), BF16)
        xinbR = [Res(), Res()]
        xT = sbuf("xT", (128, 2, DC, TB), BF16)
        xTR = [Res(), Res()]
        KEY = sbuf("KEY", (128, DC, TB), BF16)
        VAL = sbuf("VAL", (128, DC, TB), BF16)
        QRY = sbuf("QRY", (128, DC, TB), BF16)
        PH = sbuf("PH", (128, DC, TB), F32)
        UI = sbuf("UI", (128, TB), I32)
        ABSR = sbuf("ABSR", (128, TB), F32)
        SIN = sbuf("SIN", (128, DC, TB), BF16)
        COS = sbuf("COS", (128, DC, TB), BF16)
        BR = sbuf("BR", (128, DC, TB), BF16)
        BI = sbuf("BI", (128, DC, TB), BF16)
        MR = sbuf("MR", (128, DC, TB), BF16)
        MI = sbuf("MI", (128, DC, TB), BF16)
        RET = sbuf("RET", (128, DC, TB), BF16)
        RRS = sbuf("RRS", (128, DC, TB), BF16)
        STL = sbuf("STL", (128, DC, TB), BF16)
        tens = {"KEY": KEY, "VAL": VAL, "QRY": QRY, "PH": PH, "SIN": SIN, "COS": COS,
                "BR": BR, "BI": BI, "MR": MR, "MI": MI, "RET": RET, "RRS": RRS,
                "STL": STL}
        bufR = {nm: [Res() for _ in range(DC)] for nm in tens}
        alias = {"TKP": "PH", "TQP": "PH", "SINK": "SIN", "COSK": "COS",
                 "SINQ": "SIN", "COSQ": "COS", "T1": "BR", "SQA": "BI", "X1M": "RET"}
        for a, b in alias.items():
            tens[a] = tens[b]
            bufR[a] = bufR[b]
        t.update({nm: tens[nm] for nm in tens})
        T1 = BR
        SQA = BI
        X1M = RET
        uiR = Res()
        absrR = Res()

        def load_w(widx, slot, extra=()):
            deps = wbufR[slot].wr_deps() + list(extra)
            tk = dma("scalar" if widx % 2 == 0 else "gpsimd",
                     wbuf[slot][:], t["wcacheA"][widx], waits=deps)
            wbufR[slot].set_w(tk)
            return [tk]

        for tb in range(NTB):
            sl = tb % 2
            xload = []
            for tt in range(4):
                ts_ = tt % 2
                tld = dma("sync", xinf[:, ts_, :], t["x"][(tb * 4 + tt) * 128:(tb * 4 + tt + 1) * 128, :],
                          waits=xinfR[ts_].wr_deps())
                xinfR[ts_].set_w(tld)
                tcv = vop("tensor_copy", waits=[tld] + xinbR[ts_].wr_deps(),
                          out=xinb[:, ts_, :], in_=xinf[:, ts_, :])
                xinfR[ts_].add_r(tcv)
                xinbR[ts_].set_w(tcv)
                ttr = xpose_dma(xT[:, sl, :, tt * 128:(tt + 1) * 128], xinb[:, ts_, :],
                                waits=[tcv] + (xTR[sl].wr_deps() if tt == 0 else []))
                xinbR[ts_].add_r(ttr)
                xload.append(ttr)
            xTR[sl].set_w(xload[-1])
            for tk in xload[:-1]:
                xTR[sl].add_r(tk)
            xTc = xT[:, sl]

            for pi, (pn, dstn, bidx, scl) in enumerate(
                    (("k", "KEY", BK, None), ("v", "VAL", BV32, 1.0 / 32.0), ("q", "QRY", BQ, None))):
                slot = pi % 2
                wld = load_w(pi, slot, extra=[prepA_tok[pi]] if tb == 0 else [])
                dst = t[dstn]
                for dc in range(DC):
                    bk_i = dc % 2
                    w0 = wld + [xTR[sl].w] + bankR[bk_i].wr_deps() + biasA_rdy[bidx]
                    mt = None
                    for ci in range(DC):
                        mt = mm(bank(bk_i), wbuf[slot][:, ci, dc * 128:(dc + 1) * 128],
                                xTc[:, ci, :], start=(ci == 0), stop=(ci == DC - 1),
                                waits=w0 if ci == 0 else [], tok=(ci == DC - 1))
                    bankR[bk_i].set_w(mt)
                    xTR[sl].add_r(mt)
                    wbufR[slot].add_r(mt)
                    te = act(dst[:, dc, :], bank(bk_i), AF.Identity, bias=bcol(bidx, dc),
                             scale=scl, waits=[mt] + bufR[dstn][dc].wr_deps())
                    bankR[bk_i].add_r(te)
                    bufR[dstn][dc].set_w(te)

            def phase_chain(srcn, bidx, seq):
                slot = seq % 2
                _wi = 3 if srcn == "KEY" else 4
                wld = load_w(_wi, slot, extra=[prepA_tok[_wi]] if tb == 0 else [])
                src = tens[srcn]
                for dc in range(DC):
                    bk_i = 2 + dc % 2
                    w0 = (wld + [bufR[srcn][c].w for c in range(DC)]
                          + bankR[bk_i].wr_deps() + biasA_rdy[bidx] + bufR["PH"][dc].wr_deps())
                    mt = None
                    for ci in range(DC):
                        mt = mm(bank(bk_i), wbuf[slot][:, ci, dc * 128:(dc + 1) * 128],
                                src[:, ci, :], start=(ci == 0), stop=(ci == DC - 1),
                                waits=w0 if ci == 0 else [], tok=(ci == DC - 1))
                    bankR[bk_i].set_w(mt)
                    wbufR[slot].add_r(mt)
                    for c in range(DC):
                        bufR[srcn][c].add_r(mt)
                    tt_ = act(PH[:, dc, :], bank(bk_i), AF.Tanh, bias=bcol(bidx, dc),
                              waits=[mt] + bufR["PH"][dc].wr_deps())
                    bankR[bk_i].add_r(tt_)
                    bufR["PH"][dc].set_w(tt_)
                for dc in range(DC):
                    tg1 = vop("tensor_scalar", eng="gpsimd",
                              waits=[bufR["PH"][dc].w] + biasA_rdy[PSC],
                              out=PH[:, dc, :], in0=PH[:, dc, :],
                              scalar1=bcol(PSC, dc), scalar2=0.0, op0=ALU.mult, op1=ALU.add)
                    bufR["PH"][dc].set_w(tg1)
                    tg2 = vop("tensor_scalar", waits=[tg1] + uiR.wr_deps(),
                              out=UI[:], in0=PH[:, dc, :],
                              scalar1=INV2PI, scalar2=None, op0=ALU.mult)
                    uiR.set_w(tg2)
                    bufR["PH"][dc].add_r(tg2)
                    tg3 = vop("scalar_tensor_tensor", waits=[tg2],
                              out=PH[:, dc, :], in0=UI[:], scalar=-TWOPI,
                              in1=PH[:, dc, :], op0=ALU.mult, op1=ALU.add)
                    uiR.add_r(tg3)
                    bufR["PH"][dc].set_w(tg3)
                    tsin = act(SIN[:, dc, :], PH[:, dc, :], AF.Sin,
                               waits=[tg3] + bufR["SIN"][dc].wr_deps())
                    bufR["SIN"][dc].set_w(tsin)
                    bufR["PH"][dc].add_r(tsin)
                    tabs = vop("scalar_tensor_tensor", waits=[tg3] + absrR.wr_deps(),
                               out=ABSR[:], in0=PH[:, dc, :], scalar=-1.0,
                               in1=PH[:, dc, :], op0=ALU.mult, op1=ALU.max)
                    absrR.set_w(tabs)
                    bufR["PH"][dc].add_r(tabs)
                    tcos = act(COS[:, dc, :], ABSR[:], AF.Sin, scale=-1.0,
                               bias=HALFPI_AP, waits=[tabs] + CONST + bufR["COS"][dc].wr_deps())
                    absrR.add_r(tcos)
                    bufR["COS"][dc].set_w(tcos)

            phase_chain("KEY", BKP, 3)
            if dbg and tb == 0:
                dbg_dump("SINK", SIN[:], [bufR["SIN"][c].w for c in range(DC)], bufR["SIN"])
                dbg_dump("COSK", COS[:], [bufR["COS"][c].w for c in range(DC)], bufR["COS"])
                dbg_dump("RK", PH[:], [bufR["PH"][c].w for c in range(DC)], bufR["PH"])
            for dc in range(DC):
                tbr = vop("tensor_tensor",
                          waits=[bufR["COS"][dc].w, bufR["VAL"][dc].w] + bufR["BR"][dc].wr_deps(),
                          out=BR[:, dc, :], in0=VAL[:, dc, :], in1=COS[:, dc, :], op=ALU.mult)
                bufR["BR"][dc].set_w(tbr)
                bufR["COS"][dc].add_r(tbr)
                bufR["VAL"][dc].add_r(tbr)
                tbi = vop("tensor_tensor",
                          waits=[bufR["SIN"][dc].w, bufR["VAL"][dc].w] + bufR["BI"][dc].wr_deps(),
                          out=BI[:, dc, :], in0=VAL[:, dc, :], in1=SIN[:, dc, :], op=ALU.mult)
                bufR["BI"][dc].set_w(tbi)
                bufR["SIN"][dc].add_r(tbi)
                bufR["VAL"][dc].add_r(tbi)
                tmr = vop("tensor_tensor_scan", waits=[tbr] + CONST + bufR["MR"][dc].wr_deps(),
                          out=MR[:, dc, :], data0=scanmask[:], data1=BR[:, dc, :],
                          initial=0.0, op0=ALU.mult, op1=ALU.add)
                bufR["MR"][dc].set_w(tmr)
                bufR["BR"][dc].add_r(tmr)
                tmi = vop("tensor_tensor_scan", waits=[tbi] + CONST + bufR["MI"][dc].wr_deps(),
                          out=MI[:, dc, :], data0=scanmask[:], data1=BI[:, dc, :],
                          initial=0.0, op0=ALU.mult, op1=ALU.add)
                bufR["MI"][dc].set_w(tmi)
                bufR["BI"][dc].add_r(tmi)

            if dbg and tb == 0:
                dbg_dump("KEY", KEY[:], [bufR["KEY"][c].w for c in range(DC)], bufR["KEY"])
                dbg_dump("VAL", VAL[:], [bufR["VAL"][c].w for c in range(DC)], bufR["VAL"])
                dbg_dump("MRD", MR[:], [bufR["MR"][c].w for c in range(DC)], bufR["MR"])
            phase_chain("QRY", BQP, 4)
            for dc in range(DC):
                tt1 = vop("tensor_tensor",
                          waits=[bufR["MR"][dc].w, bufR["COS"][dc].w] + bufR["BR"][dc].wr_deps(),
                          out=T1[:, dc, :], in0=MR[:, dc, :], in1=COS[:, dc, :], op=ALU.mult)
                bufR["BR"][dc].set_w(tt1)
                bufR["MR"][dc].add_r(tt1)
                bufR["COS"][dc].add_r(tt1)
                tt2 = vop("tensor_tensor", eng="gpsimd",
                          waits=[bufR["MI"][dc].w, bufR["SIN"][dc].w],
                          out=MI[:, dc, :], in0=MI[:, dc, :], in1=SIN[:, dc, :], op=ALU.mult)
                bufR["MI"][dc].set_w(tt2)
                bufR["SIN"][dc].add_r(tt2)
                tre = vop("tensor_tensor", waits=[tt1, tt2] + bufR["RET"][dc].wr_deps(),
                          out=RET[:, dc, :], in0=T1[:, dc, :], in1=MI[:, dc, :], op=ALU.add)
                bufR["RET"][dc].set_w(tre)
                bufR["BR"][dc].add_r(tre)
                bufR["MI"][dc].add_r(tre)
                tsq = vop("tensor_tensor", eng="gpsimd", waits=[tre] + bufR["BI"][dc].wr_deps(),
                          out=SQA[:, dc, :], in0=RET[:, dc, :], in1=RET[:, dc, :], op=ALU.mult)
                bufR["BI"][dc].set_w(tsq)
                bufR["RET"][dc].add_r(tsq)

            if dbg and tb == 0:
                dbg_dump("RET", RET[:], [bufR["RET"][c].w for c in range(DC)], bufR["RET"])
            m1 = m2 = None
            for dc in range(DC):
                m1 = mm(brow(4), onescol[:], RET[:, dc, :], start=(dc == 0), stop=(dc == DC - 1),
                        waits=[bufR["RET"][dc].w] + CONST + (bankR[4].wr_deps() if dc == 0 else []),
                        tok=(dc == DC - 1))
                m2 = mm(brow(5), onescol[:], SQA[:, dc, :], start=(dc == 0), stop=(dc == DC - 1),
                        waits=[bufR["SQA"][dc].w] + (bankR[5].wr_deps() if dc == 0 else []),
                        tok=(dc == DC - 1))
            bankR[4].set_w(m1)
            bankR[5].set_w(m2)
            for dc in range(DC):
                bufR["RET"][dc].add_r(m1)
                bufR["SQA"][dc].add_r(m2)
            trb, tmr_ = ln_rows(4, 5, [m1, m2], True)
            tbc = mm(bank(6), ones1x128[:], rowbf[:, 0:TB], start=True, stop=True,
                     waits=[trb] + CONST + bankR[6].wr_deps())
            bankR[6].set_w(tbc)
            rowbfR.add_r(tbc)
            tbc7 = mm(bank(7), ones1x128[:], rowbf[:, TB:2 * TB], start=True, stop=True,
                      waits=[tmr_] + CONST + bankR[7].wr_deps())
            bankR[7].set_w(tbc7)
            rowbfR.add_r(tbc7)
            for dc in range(DC):
                trr = vop("tensor_tensor", waits=[tbc, bufR["RET"][dc].w] + bufR["RRS"][dc].wr_deps(),
                          out=RRS[:, dc, :], in0=RET[:, dc, :], in1=bank(6), op=ALU.mult)
                bufR["RET"][dc].add_r(trr)
                bankR[6].add_r(trr)
                trr = vop("tensor_tensor", waits=[tbc7, trr],
                          out=RRS[:, dc, :], in0=RRS[:, dc, :], in1=bank(7), op=ALU.subtract)
                bufR["RRS"][dc].set_w(trr)
                bankR[7].add_r(trr)

            slot = 5 % 2
            wld = load_w(5, slot, extra=[tbt] + ([prepA_tok[5]] if tb == 0 else []))
            for dc in range(DC):
                bk_i = dc % 2
                w0 = wld + [bufR["RRS"][c].w for c in range(DC)] + bankR[bk_i].wr_deps()
                for ci in range(DC):
                    mm(bank(bk_i), wbuf[slot][:, ci, dc * 128:(dc + 1) * 128],
                       RRS[:, ci, :], start=(ci == 0), stop=False,
                       waits=w0 if ci == 0 else [], tok=False)
                tbm = mm(bank(bk_i), btrow[:, dc * 128:(dc + 1) * 128], ones1x512[:],
                         start=False, stop=True, waits=[tbt] + CONST)
                bankR[bk_i].set_w(tbm)
                wbufR[slot].add_r(tbm)
                for c in range(DC):
                    bufR["RRS"][c].add_r(tbm)
                tst = vop("scalar_tensor_tensor", waits=[tbm, xTR[sl].w] + bufR["STL"][dc].wr_deps(),
                          out=STL[:, dc, :], in0=xTc[:, dc, :], scalar=2.0,
                          in1=bank(bk_i), op0=ALU.mult, op1=ALU.add)
                bankR[bk_i].add_r(tst)
                xTR[sl].add_r(tst)
                bufR["STL"][dc].set_w(tst)

            if dbg and tb == 0:
                dbg_dump("STL", STL[:], [bufR["STL"][c].w for c in range(DC)], bufR["STL"])
            for dc in range(DC):
                tsq = vop("tensor_tensor", eng="gpsimd",
                          waits=[bufR["STL"][dc].w] + bufR["SQA"][dc].wr_deps(),
                          out=SQA[:, dc, :], in0=STL[:, dc, :], in1=STL[:, dc, :], op=ALU.mult)
                bufR["SQA"][dc].set_w(tsq)
                bufR["STL"][dc].add_r(tsq)
            m1 = m2 = None
            for dc in range(DC):
                m1 = mm(brow(4), onescol[:], STL[:, dc, :], start=(dc == 0), stop=(dc == DC - 1),
                        waits=[bufR["STL"][dc].w] + (bankR[4].wr_deps() if dc == 0 else []),
                        tok=(dc == DC - 1))
                m2 = mm(brow(5), onescol[:], SQA[:, dc, :], start=(dc == 0), stop=(dc == DC - 1),
                        waits=[bufR["SQA"][dc].w] + (bankR[5].wr_deps() if dc == 0 else []),
                        tok=(dc == DC - 1))
            bankR[4].set_w(m1)
            bankR[5].set_w(m2)
            for dc in range(DC):
                bufR["STL"][dc].add_r(m1)
                bufR["SQA"][dc].add_r(m2)
            trb0, _ = ln_rows(4, 5, [m1, m2], False)
            tbc0 = mm(bank(6), ones1x128[:], rowbf[:, 0:TB], start=True, stop=True,
                      waits=[trb0] + bankR[6].wr_deps())
            bankR[6].set_w(tbc0)
            rowbfR.add_r(tbc0)
            for dc in range(DC):
                tx1 = vop("tensor_tensor", waits=[tbc0, bufR["STL"][dc].w] + bufR["X1M"][dc].wr_deps(),
                          out=X1M[:, dc, :], in0=STL[:, dc, :], in1=bank(6), op=ALU.mult)
                bufR["X1M"][dc].set_w(tx1)
                bufR["STL"][dc].add_r(tx1)
                bankR[6].add_r(tx1)
                td = dma("gpsimd", t["x1cache"][tb][:, dc * TB:(dc + 1) * TB], X1M[:, dc, :], waits=[tx1])
                bufR["X1M"][dc].add_r(td)
                final_toks.append(td)



    if do_b:
        if do_a:
            phase_barrier()
        sbuf = pool_sbuf
        _pool_reset()
        ln1w = sbuf("ln1w", (128, DC), F32)
        ln1b_bf = sbuf("ln1b_bf", (128, DC), BF16)
        ln1b_f = sbuf("ln1b_f", (128, DC), F32)
        tkw1 = chunkload(ln1w, t["ln1_w"])
        tkb1 = chunkload(ln1b_f, t["ln1_b"])
        tlb1 = vop("tensor_copy", waits=tkb1, out=ln1b_bf[:], in_=ln1b_f[:])

        gws = sbuf("gws", (128, DC, EA), BF16)
        kws = sbuf("kws", (128, DC, HD), BF16)
        vws = sbuf("vws", (128, DC, HD), BF16)
        obs = sbuf("obs", (8, D), BF16)
        relT = sbuf("relT", (128, 132), BF16)
        bgate = sbuf("bgate", (1, EA), BF16)
        vbias_row = sbuf("vbias_row", (1, HD), BF16)
        kqbias = sbuf("kqbias", (128, 2 + EA), F32)
        wstB = sbuf("wstB", (128, 2, 512), F32)
        wcvB = sbuf("wcvB", (128, 2, 512), BF16)
        rowB = sbuf("rowB", (1, 2 * HD), F32)
        wstR = Res()
        wcvR = Res()
        rowBR = Res()

        bias_mm = []
        for ci in range(DC):
            sl_ = ci % 2
            tl = dma("scalar", wstB[:, sl_, 0:EA], t["attn_gate_w"][ci * 128:(ci + 1) * 128, :],
                     waits=wstR.wr_deps())
            wstR.set_w(tl)
            tc = vop("tensor_copy", eng="gpsimd", waits=[tl] + wcvR.wr_deps(), out=wcvB[:, sl_, 0:EA], in_=wstB[:, sl_, 0:EA])
            wstR.add_r(tc)
            wcvR.set_w(tc)
            m1 = mm(PS[0:1, 3584:3584 + EA], ln1b_bf[:, ci:ci + 1], wcvB[:, sl_, 0:EA],
                    start=(ci == 0), stop=(ci == DC - 1), waits=[tc, tlb1] + (bankR[7].wr_deps() if ci == 0 else []))
            bias_mm.append(m1)
            tg = vop("tensor_scalar", eng="gpsimd", waits=[m1] + tkw1, out=gws[:, ci, :], in0=wcvB[:, sl_, 0:EA],
                     scalar1=ln1w[:, ci:ci + 1], scalar2=None, op0=ALU.mult)
            wcvR.add_r(tg)
        bankR[7].set_w(bias_mm[-1])
        tbg = vop("tensor_copy", waits=[bias_mm[-1]], out=bgate[:], in_=PS[0:1, 3584:3584 + EA])
        bankR[7].add_r(tbg)
        tbg = vfence(tbg)

        kvbias_toks = []
        for wi, (wn, bn, dstw) in enumerate((("k_w", "k_b", kws), ("v_w", "v_b", vws))):
            bias_mm = []
            for ci in range(DC):
                sl_ = ci % 2
                tl = dma("scalar", wstB[:, sl_, 0:HD], t[wn][ci * 128:(ci + 1) * 128, :],
                         waits=wstR.wr_deps())
                wstR.set_w(tl)
                tc = vop("tensor_copy", eng="gpsimd", waits=[tl] + wcvR.wr_deps(), out=wcvB[:, sl_, 0:HD], in_=wstB[:, sl_, 0:HD])
                wstR.add_r(tc)
                wcvR.set_w(tc)
                m1 = mm(PS[0:1, 3584:3584 + HD], ln1b_bf[:, ci:ci + 1], wcvB[:, sl_, 0:HD],
                        start=(ci == 0), stop=(ci == DC - 1), waits=[tc, tlb1] + (bankR[7].wr_deps() if ci == 0 else []))
                bias_mm.append(m1)
                tg = vop("tensor_scalar", eng="gpsimd", waits=[m1] + tkw1, out=dstw[:, ci, :], in0=wcvB[:, sl_, 0:HD],
                         scalar1=ln1w[:, ci:ci + 1], scalar2=None, op0=ALU.mult)
                wcvR.add_r(tg)
            bankR[7].set_w(bias_mm[-1])
            tbl = dma("sync", rowB[:, 0:HD], t[bn].unsqueeze(0), waits=rowBR.wr_deps())
            tba = vop("tensor_tensor", waits=[tbl, bias_mm[-1]], out=rowB[:, HD:2 * HD],
                      in0=PS[0:1, 3584:3584 + HD], in1=rowB[:, 0:HD], op=ALU.add)
            rowBR.set_w(tba)
            bankR[7].add_r(tba)
            if wn == "v_w":
                tvb = vop("tensor_copy", waits=[tba], out=vbias_row[:], in_=rowB[:, HD:2 * HD])
                tvb = vfence(tvb)
                rowBR.add_r(tvb)
                kvbias_toks.append(tvb)
            else:
                tba = vfence(tba)
                tst = dma("gpsimd", t["rowbounceB"][0:HD], rowB[0:1, HD:2 * HD], waits=[tba])
                rowBR.add_r(tst)
                tld = dma("gpsimd", kqbias[:, 0:1], t["rowbounceB"][0:HD], waits=[tst])
                kvbias_toks.append(tld)

        wqprep_tok = [None] * EA
        for e in range(EA):
            bias_mm = []
            for ci in range(DC):
                sl_ = ci % 2
                tl = dma("scalar", wstB[:, sl_, 0:HD], t["q_w"][e, ci * 128:(ci + 1) * 128, :],
                         waits=wstR.wr_deps())
                wstR.set_w(tl)
                tc = vop("tensor_copy", eng="gpsimd", waits=[tl] + wcvR.wr_deps(), out=wcvB[:, sl_, 0:HD], in_=wstB[:, sl_, 0:HD])
                wstR.add_r(tc)
                wcvR.set_w(tc)
                m1 = mm(PS[0:1, 3584:3584 + HD], ln1b_bf[:, ci:ci + 1], wcvB[:, sl_, 0:HD],
                        start=(ci == 0), stop=(ci == DC - 1), waits=[tc, tlb1] + (bankR[7].wr_deps() if ci == 0 else []))
                bias_mm.append(m1)
                tg = vop("tensor_scalar", eng="gpsimd", waits=[m1] + tkw1, out=wcvB[:, sl_, 0:HD], in0=wcvB[:, sl_, 0:HD],
                         scalar1=ln1w[:, ci:ci + 1], scalar2=HD ** -0.5, op0=ALU.mult, op1=ALU.mult)
                wcvR.set_w(tg)
                tsv = dma("gpsimd", t["wq_c"][e][:, ci * HD:(ci + 1) * HD], wcvB[:, sl_, 0:HD], waits=[tg])
                wcvR.add_r(tsv)
                wqprep_tok[e] = tsv
            bankR[7].set_w(bias_mm[-1])
            tbl = dma("sync", rowB[:, 0:HD], t["q_b"][e].unsqueeze(0), waits=rowBR.wr_deps())
            tba = vop("scalar_tensor_tensor", waits=[tbl, bias_mm[-1]], out=rowB[:, HD:2 * HD],
                      in0=PS[0:1, 3584:3584 + HD], scalar=1.0, in1=rowB[:, 0:HD],
                      op0=ALU.mult, op1=ALU.add)
            rowBR.set_w(tba)
            bankR[7].add_r(tba)
            tsc = vop("tensor_scalar", waits=[tba], out=rowB[:, HD:2 * HD], in0=rowB[:, HD:2 * HD],
                      scalar1=HD ** -0.5, scalar2=None, op0=ALU.mult)
            tsc = vfence(tsc)
            rowBR.set_w(tsc)
            tst = dma("gpsimd", t["rowbounceB"][0:HD], rowB[0:1, HD:2 * HD], waits=[tsc])
            rowBR.add_r(tst)
            tld = dma("gpsimd", kqbias[:, 2 + e:3 + e], t["rowbounceB"][0:HD], waits=[tst])
            kvbias_toks.append(tld)

        ow_toks = []
        for e in range(EA):
            for hb in range(2):
                sl_ = hb % 2
                tl = dma("scalar", wstB[:, sl_, :], t["o_w"][e][:, hb * 512:(hb + 1) * 512],
                         waits=wstR.wr_deps())
                wstR.set_w(tl)
                tc = vop("tensor_copy", eng="gpsimd", waits=[tl] + wcvR.wr_deps(), out=wcvB[:, sl_, :], in_=wstB[:, sl_, :])
                wstR.add_r(tc)
                wcvR.set_w(tc)
                for dq in range(4):
                    dc_ = hb * 4 + dq
                    tsv = dma("gpsimd", t["owc"][dc_][:, e * 128:(e + 1) * 128],
                              wcvB[:, sl_, dq * 128:(dq + 1) * 128], waits=[tc])
                    wcvR.add_r(tsv)
                    ow_toks.append(tsv)
        obstage = sbuf("obstage", (8, D), F32)
        tob_l = dma("scalar", obstage[:], t["o_b"][:, :])
        tob = vop("tensor_copy", waits=[tob_l], out=obs[:], in_=obstage[:])
        tob = vfence(tob)

        trl = dma("scalar", wstB[:, 0, 0:HD], t["rel_table"][0:128, :], waits=wstR.wr_deps())
        wstR.set_w(trl)
        ttr = k.op("tensor", lambda e: e.transpose(PS[:, 3584:3584 + 128], wstB[:, 0, 0:HD], ident[:]),
                   waits=[trl, tid] + bankR[7].wr_deps(), sem=sT)
        bankR[7].set_w(ttr)
        wstR.add_r(ttr)
        tre = vop("tensor_copy", waits=[ttr], out=relT[:, 0:128], in_=PS[:, 3584:3584 + 128])
        bankR[7].add_r(tre)
        trl2 = dma("scalar", rowB[:, 0:HD], t["rel_table"][128].unsqueeze(0), waits=rowBR.wr_deps())
        rowBR.set_w(trl2)
        ttr2 = k.op("tensor", lambda e: e.transpose(PS[:, 3584:3584 + 1], rowB[:, 0:HD], ident[0:1, 0:1]),
                    waits=[trl2, tid, tre] + bankR[7].wr_deps(), sem=sT)
        bankR[7].set_w(ttr2)
        rowBR.add_r(ttr2)
        tre2 = vop("tensor_copy", waits=[ttr2], out=relT[:, 128:129], in_=PS[:, 3584:3584 + 1])
        bankR[7].add_r(tre2)
        relT_rdy = [vfence(tre2)]
        prepB_done = [tbg, tob] + kvbias_toks + ow_toks + relT_rdy

        x1b = sbuf("x1b", (128, DC, TB), BF16)
        x1bR = [Res() for _ in range(DC)]
        XN = sbuf("XN", (128, DC, TB), BF16)
        xnR = [Res() for _ in range(DC)]
        SQB = XN
        sqbR = xnR
        kF = sbuf("kF", (128, NT), BF16)
        kFR = [Res() for _ in range(NTB)]
        vK = sbuf("vK", (128, 32, HD), BF16)
        vKR = [Res() for _ in range(32)]
        LGF = sbuf("LGF", (16, TB), F32)
        lgfR = Res()
        LGA = sbuf("LGA", (128, 4, 16), F32)
        lgaR = [Res() for _ in range(4)]
        GTS = sbuf("GTS", (128, 32, EA), F32)
        gtsR = [Res() for _ in range(32)]
        gstg = sbuf("gstg", (8, 128), BF16)
        gstgR = Res()
        gATb = sbuf("gATb", (8, TB), BF16)
        gATbR = Res()
        rowtb = sbuf("rowtb", (1, 2 * TB), BF16)
        rowtbR = Res()

        rows1_toks = {}

        def load_x1(tb, extra=()):
            deps = list(extra)
            for dc in range(DC):
                deps += x1bR[dc].wr_deps()
            tk_ = dma("gpsimd", x1b[:], t["x1cache"][tb], waits=deps)
            for dc in range(DC):
                x1bR[dc].set_w(tk_)
            return [tk_] * DC

        def ln1_and_xn(tb, phaseA_done):
            tx1 = load_x1(tb, extra=phaseA_done)
            for dc in range(DC):
                tsq = vop("tensor_tensor", eng="gpsimd", waits=[tx1[dc]] + sqbR[dc].wr_deps(),
                          out=SQB[:, dc, :], in0=x1b[:, dc, :], in1=x1b[:, dc, :], op=ALU.mult)
                sqbR[dc].set_w(tsq)
            m1 = m2 = None
            for dc in range(DC):
                m1 = mm(brow(4), onescol[:], x1b[:, dc, :], start=(dc == 0), stop=(dc == DC - 1),
                        waits=[tx1[dc]] + CONST + (bankR[4].wr_deps() if dc == 0 else []),
                        tok=(dc == DC - 1))
                m2 = mm(brow(5), onescol[:], SQB[:, dc, :], start=(dc == 0), stop=(dc == DC - 1),
                        waits=[sqbR[dc].w] + (bankR[5].wr_deps() if dc == 0 else []),
                        tok=(dc == DC - 1))
            bankR[4].set_w(m1)
            bankR[5].set_w(m2)
            for dc in range(DC):
                x1bR[dc].add_r(m1)
                sqbR[dc].add_r(m2)
            trb, tmr_ = ln_rows(4, 5, [m1, m2], True)
            tc1 = dma("gpsimd", t["rows1d"][0, tb * TB:(tb + 1) * TB], rowbf[0:1, 0:TB],
                      waits=[trb, tmr_])
            tc2 = dma("gpsimd", t["rows1d"][1, tb * TB:(tb + 1) * TB], rowbf[0:1, TB:2 * TB],
                      waits=[tmr_])
            rowbfR.add_r(tc1)
            rowbfR.add_r(tc2)
            rows1_toks[tb] = [tc1, tc2]
            return make_xn(tb, rowbf[:, 0:TB], rowbf[:, TB:2 * TB], [trb, tmr_], rowbfR)

        def make_xn(tb, rstd_ap, mu_ap, row_toks, row_res):
            tbc = mm(bank(6), ones1x128[:], rstd_ap, start=True, stop=True,
                     waits=list(row_toks) + CONST + bankR[6].wr_deps())
            bankR[6].set_w(tbc)
            row_res.add_r(tbc)
            tbc7 = mm(bank(7), ones1x128[:], mu_ap, start=True, stop=True,
                      waits=list(row_toks) + CONST + bankR[7].wr_deps())
            bankR[7].set_w(tbc7)
            row_res.add_r(tbc7)
            toks = []
            for dc in range(DC):
                t1_ = vop("tensor_tensor", waits=[tbc, x1bR[dc].w] + xnR[dc].wr_deps(),
                          out=XN[:, dc, :], in0=x1b[:, dc, :], in1=bank(6), op=ALU.mult)
                x1bR[dc].add_r(t1_)
                bankR[6].add_r(t1_)
                t2_ = vop("tensor_tensor", waits=[tbc7, t1_],
                          out=XN[:, dc, :], in0=XN[:, dc, :], in1=bank(7), op=ALU.subtract)
                bankR[7].add_r(t2_)
                xnR[dc].set_w(t2_)
                toks.append(t2_)
            return toks


        gat_toks = []
        for tb in range(NTB):
            xn_toks = ln1_and_xn(tb, final_toks[-1:] if do_a else [])
            mt = None
            for ci in range(DC):
                mt = mm(bank(0), kws[:, ci, :], XN[:, ci, :], start=(ci == 0), stop=(ci == DC - 1),
                        waits=(xn_toks + prepB_done + bankR[0].wr_deps()) if ci == 0 else [],
                        tok=(ci == DC - 1))
            bankR[0].set_w(mt)
            for dc in range(DC):
                xnR[dc].add_r(mt)
            te = act(kF[:, tb * TB:(tb + 1) * TB], bank(0), AF.Identity, bias=kqbias[:, 0:1],
                     waits=[mt] + kFR[tb].wr_deps())
            bankR[0].add_r(te)
            kFR[tb].set_w(te)
            for tt in range(4):
                mt = None
                for ci in range(DC):
                    mt = mm(PS[:, 512:512 + HD], XN[:, ci, tt * 128:(tt + 1) * 128], vws[:, ci, :],
                            start=(ci == 0), stop=False,
                            waits=(xn_toks + bankR[1].wr_deps()) if ci == 0 else [], tok=False)
                mb = mm(PS[:, 512:512 + HD], ones1x128[:], vbias_row[:],
                        start=False, stop=True, waits=CONST + prepB_done)
                bankR[1].set_w(mb)
                for dc in range(DC):
                    xnR[dc].add_r(mb)
                tev = vop("tensor_copy", waits=[mb] + vKR[tb * 4 + tt].wr_deps(),
                          out=vK[:, tb * 4 + tt, :], in_=PS[:, 512:512 + HD])
                bankR[1].add_r(tev)
                vKR[tb * 4 + tt].set_w(tev)
            mt = None
            for ci in range(DC):
                mt = mm(PS[0:8, 1024:1024 + TB], gws[:, ci, :], XN[:, ci, :],
                        start=(ci == 0), stop=False,
                        waits=(xn_toks + bankR[2].wr_deps()) if ci == 0 else [], tok=False)
            mb = mm(PS[0:8, 1024:1024 + TB], bgate[:], ones1x512[:], start=False, stop=True,
                    waits=CONST + prepB_done)
            bankR[2].set_w(mb)
            for dc in range(DC):
                xnR[dc].add_r(mb)
            tlg = vop("tensor_copy", waits=[mb] + lgfR.wr_deps(), out=LGF[0:8, 0:TB],
                      in_=PS[0:8, 1024:1024 + TB])
            bankR[2].add_r(tlg)
            lgfR.set_w(tlg)
            tla_toks = []
            for tt in range(4):
                tpt = k.op("tensor",
                           lambda e, o=PS[:, 1536:1536 + 8], i=LGF[0:8, tt * 128:(tt + 1) * 128]:
                           e.transpose(o, i, ident[0:8, 0:8]),
                           waits=[tlg, tid] + bankR[3].wr_deps(), sem=sT)
                bankR[3].set_w(tpt)
                lgfR.add_r(tpt)
                tla = vop("tensor_copy", waits=[tpt] + lgaR[tt].wr_deps(),
                          out=LGA[:, tt, 0:8], in_=PS[:, 1536:1536 + 8])
                bankR[3].add_r(tla)
                lgaR[tt].set_w(tla)
                tla_toks.append(tla)
            tg = gates_chain(LGA[:, :, 0:8], GTS[:, tb * 4:(tb + 1) * 4, :], EA, 4,
                             tla_toks, [gtsR[tb * 4 + i] for i in range(4)])
            for tt in range(4):
                tgt = k.op("tensor",
                           lambda e, o=PS[0:8, 1536 + 16:1536 + 16 + 128], i=GTS[:, tb * 4 + tt, :]:
                           e.transpose(o, i, ident[:]),
                           waits=[tg, tid] + bankR[3].wr_deps(), sem=sT)
                bankR[3].set_w(tgt)
                gtsR[tb * 4 + tt].add_r(tgt)
                tge = vop("tensor_copy", waits=[tgt] + gstgR.wr_deps(),
                          out=gstg[:], in_=PS[0:8, 1536 + 16:1536 + 16 + 128])
                bankR[3].add_r(tge)
                tge = vfence(tge)
                gstgR.set_w(tge)
                tgd = dma("gpsimd", t["gatc"][:, (tb * 4 + tt) * 128:(tb * 4 + tt + 1) * 128],
                          gstg[:], waits=[tge])
                gstgR.add_r(tgd)
                gat_toks.append(tgd)


        kF_all = kFR[NTB - 1].w
        vK_all = vKR[31].w
        QE = sbuf("QE", (128, EA, TB), BF16)
        qeR = [Res() for _ in range(EA)]
        _wq0 = sbuf("wqbuf0", (128, DC, HD), BF16)
        _wqr0 = Res()
        wqbuf = [_wq0, _wq0]
        wqR = [_wqr0, _wqr0]
        RELS = sbuf("RELS", (128, 4, 132), F32)
        relsR = [Res() for _ in range(4)]
        RB = sbuf("RB", (128, 2, 640), BF16)
        rbR = [Res(), Res()]
        BAND = sbuf("BAND", (128, 4, 384), BF16)
        bandR = [Res() for _ in range(4)]
        P1 = sbuf("P1", (128, 2, NT), BF16)
        p1R = [Res(), Res()]
        PT = sbuf("PT", (128, 32, EA, 128), BF16)
        ptR = Res()
        ZP = sbuf("ZP", (128, 2, 8), F32)
        zpR = [Res(), Res()]
        ZS = sbuf("ZS", (128, 2, 2), F32)
        zsR = [Res(), Res()]
        FV = sbuf("FV", (128, 2, EA), F32)
        fvR = [Res(), Res()]
        frows = sbuf("frows", (1, 1024), BF16)
        frowsR = Res()
        FBt = sbuf("FBt", (128, EA, 128), BF16)
        fbR = [Res() for _ in range(EA)]
        CXP = sbuf("CXP", (128, EA, TB), BF16)
        cxpR = [Res() for _ in range(EA)]
        owdc = sbuf("owdc", (128, 2, EA, 128), BF16)
        owdcR = [Res(), Res()]
        SOUT = sbuf("SOUT", (128, 2, TB), BF16)
        soutR = [Res(), Res()]
        X2O = SOUT
        x2oR = soutR
        rimgR = [Res() for _ in range(4)]

        sstore_toks = []
        regA = (0, 1)
        regB = (2, 3)

        for tb in range(NTB):
            tx1 = load_x1(tb)
            cxp_w_toks = []
            trl1 = dma("gpsimd", rowtb[0:1, 0:TB].bitcast(BF16) if False else rowtb[0:1, 0:TB],
                       t["rows1d"][0, tb * TB:(tb + 1) * TB],
                       waits=rowtbR.wr_deps() + rows1_toks[tb])
            trl2 = dma("gpsimd", rowtb[0:1, TB:2 * TB], t["rows1d"][1, tb * TB:(tb + 1) * TB],
                       waits=rows1_toks[tb])
            rowtbR.set_w(trl2)
            rowtbR.add_r(trl1)
            tgl = dma("gpsimd", gATb[:], t["gatc"][:, tb * TB:(tb + 1) * TB],
                      waits=gATbR.wr_deps() + gat_toks)
            gATbR.set_w(tgl)
            xn_toks = make_xn(tb, rowtb[:, 0:TB], rowtb[:, TB:2 * TB], [trl1, trl2], rowtbR)
            for e in range(EA):
                slot = e % 2
                deps = wqR[slot].wr_deps()
                tkq = dma("scalar", wqbuf[slot][:], t["wq_c"][e],
                          waits=deps + ([wqprep_tok[e]] + kvbias_toks if tb == 0 else []))
                wqR[slot].set_w(tkq)
                wq_toks = [tkq]
                bk_i = 2 + e % 2
                mt = None
                for ci in range(DC):
                    mt = mm(bank(bk_i), wqbuf[slot][:, ci, :], XN[:, ci, :],
                            start=(ci == 0), stop=(ci == DC - 1),
                            waits=(wq_toks + xn_toks + bankR[bk_i].wr_deps()) if ci == 0 else [],
                            tok=(ci == DC - 1))
                bankR[bk_i].set_w(mt)
                wqR[slot].add_r(mt)
                for dc in range(DC):
                    xnR[dc].add_r(mt)
                te = act(QE[:, e, :], bank(bk_i), AF.Identity, bias=kqbias[:, 2 + e:3 + e],
                         waits=[mt] + qeR[e].wr_deps())
                bankR[bk_i].add_r(te)
                qeR[e].set_w(sfence(te))

            def emit_fb_ctx(qprev, xb_toks):
                jj = tb * 4 + qprev
                fsl = qprev % 2
                ev_list = []
                for e in range(EA):
                    tft = k.op("tensor",
                               lambda e_, i=FV[:, fsl, e:e + 1]:
                               e_.transpose(PS[0:1, 3204:3204 + 128], i, ident[:]),
                               waits=[fvR[fsl].w, tid] + bankR[6].wr_deps(), sem=sT)
                    bankR[6].set_w(tft)
                    fvR[fsl].add_r(tft)
                    tfr = vop("tensor_copy", waits=[tft] + (frowsR.wr_deps() if e == 0 else []),
                              out=frows[0:1, e * 128:(e + 1) * 128],
                              in_=PS[0:1, 3204:3204 + 128])
                    bankR[6].add_r(tfr)
                    ev_list.append(tfr)
                tfd2 = k.op("vector", lambda e_: e_.drain(), waits=[ev_list[-1]], sem=sV)
                frowsR.set_w(tfd2)
                for tk_ in ev_list:
                    frowsR.add_r(tk_)
                for e in range(EA):
                    tfb = mm(PS[:, 3332:3332 + 128], ones1x128[:],
                             frows[0:1, e * 128:(e + 1) * 128],
                             start=True, stop=True, waits=[tfd2] + CONST + bankR[6].wr_deps())
                    bankR[6].set_w(tfb)
                    frowsR.add_r(tfb)
                    tfc = vop("tensor_copy", waits=[tfb] + fbR[e].wr_deps(),
                              out=FBt[:, e, :], in_=PS[:, 3332:3332 + 128])
                    bankR[6].add_r(tfc)
                    fbR[e].set_w(tfc)
                m6 = m7 = None
                for kk in range(32):
                    w0 = (xb_toks + [vK_all]
                          + bankR[6].wr_deps() + bankR[7].wr_deps()) if kk == 0 else []
                    m7 = mm(PS[:, 3584:4096], vK[:, kk, :], PT[:, kk, 0:4, :],
                            start=(kk == 0), stop=(kk == 31), waits=w0, tok=(kk == 31))
                    m6 = mm(PS[:, 3072:3584], vK[:, kk, :], PT[:, kk, 4:8, :],
                            start=(kk == 0), stop=(kk == 31), waits=[], tok=(kk == 31))
                bankR[7].set_w(m7)
                bankR[6].set_w(m6)
                ptR.add_r(m6)
                ptR.add_r(m7)
                for e in range(EA):
                    bnk = 7 if e < 4 else 6
                    coff = 3584 + (e % 4) * 128 if e < 4 else 3072 + (e % 4) * 128
                    tcx = vop("tensor_tensor",
                              waits=[m7 if e < 4 else m6, fbR[e].w]
                              + (cxpR[e].wr_deps() if qprev == 0 else []),
                              out=CXP[:, e, qprev * 128:(qprev + 1) * 128],
                              in0=PS[:, coff:coff + 128], in1=FBt[:, e, :], op=ALU.mult)
                    bankR[bnk].add_r(tcx)
                    fbR[e].add_r(tcx)
                    cxp_w_toks.append(tcx)
                    if qprev == 3:
                        cxpR[e].set_w(tcx)
                    else:
                        cxpR[e].add_r(tcx)

            prev_xb = None
            gunit = 0
            for qtl in range(4):
                j = tb * 4 + qtl
                j0 = max(0, 128 * (j - 1))
                j1 = min(NT, 128 * (j + 2))
                wwid = j1 - j0
                moff = 256 if j == 0 else 128
                band_toks = [None] * EA
                rels_toks = [None] * EA
                fsl = qtl % 2
                for e in range(EA):
                    sl_e = e % 4
                    trm = mm(PS[:, 3072:3072 + 132], QE[:, e, qtl * 128:(qtl + 1) * 128],
                             relT[:, 0:132], start=True, stop=True,
                             waits=[qeR[e].w] + relT_rdy + bankR[6].wr_deps())
                    bankR[6].set_w(trm)
                    qeR[e].add_r(trm)
                    tre_ = vop("tensor_copy", waits=[trm] + relsR[sl_e].wr_deps(),
                               out=RELS[:, sl_e, :], in_=PS[:, 3072:3072 + 132])
                    bankR[6].add_r(tre_)
                    relsR[sl_e].set_w(tre_)
                    rels_toks[e] = tre_
                    sl_r = e % 2
                    tr1 = vop("tensor_scalar", eng="gpsimd", waits=[tre_] + rbR[sl_r].wr_deps(),
                              out=RB[:, sl_r, 0:192], in0=RB[:, sl_r, 0:192], scalar1=0.0,
                              scalar2=RELS[:, sl_e, 0:1], op0=ALU.mult, op1=ALU.add)
                    tr2 = vop("tensor_copy", eng="gpsimd", waits=[tre_, tr1],
                              out=RB[:, sl_r, 192:321], in_=RELS[:, sl_e, 0:129])
                    tr3 = vop("tensor_scalar", eng="gpsimd", waits=[tre_, tr2],
                              out=RB[:, sl_r, 321:640], in0=RB[:, sl_r, 321:640], scalar1=0.0,
                              scalar2=RELS[:, sl_e, 128:129], op0=ALU.mult, op1=ALU.add)
                    relsR[sl_e].add_r(tr1)
                    relsR[sl_e].add_r(tr2)
                    relsR[sl_e].add_r(tr3)
                    rbR[sl_r].set_w(tr3)
                    par = e % 4
                    trs = dma("gpsimd", t["Rimg"][par], RB[:, sl_r, :],
                              waits=[tr3] + rimgR[par].wr_deps())
                    rbR[sl_r].add_r(trs)
                    rimgR[par].set_w(trs)
                    src = AP(tensor=t["Rimg"].tensor, offset=par * 128 * 640 + moff,
                             ap=[[639, 128], [1, wwid]])
                    tgt_ = dma("gpsimd", BAND[:, par, 0:wwid], src,
                               waits=[trs] + bandR[par].wr_deps())
                    rimgR[par].add_r(tgt_)
                    bandR[par].set_w(tgt_)
                    band_toks[e] = tgt_
                if prev_xb is not None:
                    emit_fb_ctx(qtl - 1, prev_xb)
                xbar_toks = []
                for e in range(EA):
                    sl_e = e % 4
                    par = e % 4
                    sl_p = e % 2
                    tre_ = rels_toks[e]
                    tgt_ = band_toks[e]
                    zidx = 0
                    exp_toks = []
                    zp_deps = zpR[sl_p].wr_deps()
                    for u in range(4):
                        reg = gunit % 3
                        gunit += 1
                        psoff = reg * 1024
                        b0, b1 = 2 * reg, 2 * reg + 1
                        rdep = bankR[b0].wr_deps() + bankR[b1].wr_deps()
                        mts = []
                        for half in range(2):
                            mt = mm(PS[:, psoff + half * 512: psoff + (half + 1) * 512],
                                    QE[:, e, qtl * 128:(qtl + 1) * 128],
                                    kF[:, u * 1024 + half * 512: u * 1024 + (half + 1) * 512],
                                    start=True, stop=True,
                                    waits=([qeR[e].w, kF_all] + rdep) if half == 0 else [],
                                    tok=True)
                            mts.append(mt)
                        qeR[e].add_r(mts[-1])
                        bankR[b0].set_w(mts[0])
                        bankR[b1].set_w(mts[1])
                        us, ue = 1024 * u, 1024 * (u + 1)
                        ba, bb = max(us, j0), min(ue, j1)
                        tband = None
                        if ba < bb:
                            tband = vop("tensor_tensor",
                                        waits=[tgt_, mts[0] if ba < us + 512 else mts[1],
                                               mts[1] if bb > us + 512 else mts[0]],
                                        out=PS[:, psoff + ba - us: psoff + bb - us],
                                        in0=PS[:, psoff + ba - us: psoff + bb - us],
                                        in1=BAND[:, par, ba - j0: bb - j0], op=ALU.add)
                            bandR[par].add_r(tband)
                            bankR[b0].set_w(tband)
                            bankR[b1].set_w(tband)
                        segs = []
                        if us < j0:
                            segs.append((us, min(ue, j0), RELS[:, sl_e, 0:1]))
                        if ba < bb:
                            segs.append((ba, bb, None))
                        if ue > j1:
                            segs.append((max(us, j1), ue, RELS[:, sl_e, 128:129]))
                        for (a, b, bias_ap) in segs:
                            dep = [mts[0]] if a < us + 512 else []
                            if b > us + 512:
                                dep.append(mts[1])
                            if tband is not None:
                                dep.append(tband)
                            tex = act(P1[:, sl_p, a:b], PS[:, psoff + a - us: psoff + b - us],
                                      AF.Exp, bias=bias_ap,
                                      accum=ZP[:, sl_p, zidx:zidx + 1],
                                      waits=dep + [tre_] + zp_deps
                                      + (p1R[sl_p].wr_deps() if (u == 0 and zidx == 0) else []))
                            relsR[sl_e].add_r(tex)
                            bankR[b0].add_r(tex)
                            bankR[b1].add_r(tex)
                            exp_toks.append(tex)
                            zidx += 1
                    tfe = sfence(exp_toks[-1])
                    p1R[sl_p].set_w(tfe)
                    for tk_ in exp_toks[:-1]:
                        p1R[sl_p].add_r(tk_)
                    tz = vop("tensor_reduce", waits=exp_toks + [tfe] + zsR[sl_p].wr_deps(),
                             out=ZS[:, sl_p, 0:1], in_=ZP[:, sl_p, 0:zidx],
                             axis=mybir.AxisListType.X, op=ALU.add)
                    zpR[sl_p].set_w(tz)
                    tiv = vop("reciprocal", waits=[tz], out=ZS[:, sl_p, 1:2], in_=ZS[:, sl_p, 0:1])
                    zsR[sl_p].set_w(tiv)
                    tf = vop("tensor_tensor",
                             waits=[tiv, gtsR[j].w] + (fvR[fsl].wr_deps() if e == 0 else []),
                             out=FV[:, fsl, e:e + 1], in0=GTS[:, j, e:e + 1], in1=ZS[:, sl_p, 1:2],
                             op=ALU.mult)
                    zsR[sl_p].add_r(tf)
                    gtsR[j].add_r(tf)
                    if e == EA - 1:
                        tfd = k.op("vector", lambda e_: e_.drain(), waits=[tf], sem=sV)
                        fvR[fsl].set_w(tfd)
                    txb = xpose_dma(PT[:, :, e, :], P1[:, sl_p, :],
                                    waits=[p1R[sl_p].w] + ([ptR.w] + ptR.readers if e == 0 else []))
                    p1R[sl_p].add_r(txb)
                    xbar_toks.append(txb)
                ptR.set_w(xbar_toks[-1])
                for tk_ in xbar_toks[:-1]:
                    ptR.add_r(tk_)
                prev_xb = xbar_toks
            emit_fb_ctx(3, prev_xb)

            for dc in range(DC):
                sl_o = dc % 2
                towd = dma("scalar", owdc[:, sl_o, :, :], t["owc"][dc],
                           waits=owdcR[sl_o].wr_deps() + (ow_toks if tb == 0 else []))
                owdcR[sl_o].set_w(towd)
                owt = [towd]
                bk_i = dc % 2
                w0 = ([cxpR[e].w for e in range(EA)] + cxp_w_toks
                      + bankR[bk_i].wr_deps() + [gATbR.w] + owt + [tob])
                mt = None
                for e in range(EA):
                    mt = mm(bank(bk_i), owdc[:, sl_o, e, :], CXP[:, e, :],
                            start=(e == 0), stop=False, waits=w0 if e == 0 else [], tok=False)
                mb = mm(bank(bk_i), obs[0:8, dc * 128:(dc + 1) * 128],
                        gATb[:], start=False, stop=True)
                bankR[bk_i].set_w(mb)
                gATbR.add_r(mb)
                owdcR[sl_o].add_r(mb)
                for e in range(EA):
                    cxpR[e].add_r(mb)
                ts1 = vop("scalar_tensor_tensor", waits=[mb, x1bR[dc].w] + soutR[dc % 2].wr_deps(),
                          out=SOUT[:, dc % 2, :], in0=x1b[:, dc, :], scalar=2.0,
                          in1=bank(bk_i), op0=ALU.mult, op1=ALU.add)
                soutR[dc % 2].set_w(ts1)
                bankR[bk_i].add_r(ts1)
                x1bR[dc].add_r(ts1)
                td1 = dma("gpsimd", t["scache"][tb][:, dc * TB:(dc + 1) * TB], SOUT[:, dc % 2, :], waits=[ts1])
                soutR[dc % 2].add_r(td1)
                sstore_toks.append(td1)
                ts2 = vop("tensor_tensor", waits=[mb, x1bR[dc].w] + x2oR[dc % 2].wr_deps(),
                          out=X2O[:, dc % 2, :], in0=x1b[:, dc, :], in1=bank(bk_i), op=ALU.add)
                x2oR[dc % 2].set_w(ts2)
                bankR[bk_i].add_r(ts2)
                x1bR[dc].add_r(ts2)
                td2 = dma("gpsimd", t["x2cache"][tb][:, dc * TB:(dc + 1) * TB], X2O[:, dc % 2, :], waits=[ts2])
                x2oR[dc % 2].add_r(td2)
                sstore_toks.append(td2)

            if dbg and tb == 0:
                dbg_dump("XN", XN[:], [xnR[c].w for c in range(DC)], xnR)
                dbg_dump("QE", QE[:], [qeR[e].w for e in range(EA)], qeR)
                dbg_dump("CXP", CXP[:], [cxpR[e].w for e in range(EA)], cxpR)
        if dbg:
            dbg_dump("KFD", kF[:], [kFR[i].w for i in range(NTB)], kFR)
            dbg_dump("VKD", vK[:], [vKR[i].w for i in range(32)], vKR)
            dbg_dump("GTSD", GTS[:], [gtsR[i].w for i in range(32)], gtsR)
        final_toks += sstore_toks


    if do_c:
        if do_b:
            phase_barrier()
        sbuf = pool_sbuf
        _pool_reset()
        ln2w = sbuf("ln2w", (128, DC), F32)
        ln2b_f = sbuf("ln2b_f", (128, DC), F32)
        tkw2 = chunkload(ln2w, t["ln2_w"])
        tkb2 = chunkload(ln2b_f, t["ln2_b"])
        ones1x128f = sbuf("ones1x128f", (1, 128), F32)
        tof = k.op("gpsimd", lambda e: e.memset(ones1x128f[:], 1.0), sem=sG)
        CONSTF = CONST + [tof]

        fgws = sbuf("fgws", (128, DC, EF), BF16)
        b1c = sbuf("b1c", (128, EF * 4), F32)
        fb2s = sbuf("fb2s", (16, D), BF16)
        wstC = sbuf("wstC", (128, 2, 1024), F32)
        wstCR = [Res(), Res()]

        prepC_toks = []
        for ci in range(DC):
            sl_ = ci % 2
            tl = dma("scalar", wstC[:, sl_, 0:EF], t["ff_gate_w"][ci * 128:(ci + 1) * 128, :],
                     waits=wstCR[sl_].wr_deps())
            wstCR[sl_].set_w(tl)
            tc = vop("tensor_copy", eng="gpsimd", waits=[tl], out=fgws[:, ci, :],
                     in_=wstC[:, sl_, 0:EF])
            wstCR[sl_].add_r(tc)
            prepC_toks.append(tc)
        prep_b1 = [None] * EF
        for f in range(EF):
            tld = None
            for hb in range(4):
                tld = dma("sync", b1c[:, f * 4 + hb: f * 4 + hb + 1],
                          t["ff_b1"][f][hb * 128:(hb + 1) * 128])
            prep_b1[f] = tld
        tfl = dma("scalar", wstC[0:16, 0, :], t["ff_b2"][:, :], waits=wstCR[0].wr_deps())
        wstCR[0].set_w(tfl)
        tfb2 = vop("tensor_copy", waits=[tfl], out=fb2s[:], in_=wstC[0:16, 0, :])
        wstCR[0].add_r(tfb2)
        tfb2 = vfence(tfb2)
        prep_w1 = [None] * EF
        prep_w2 = [None] * EF

        X2B = sbuf("X2B", (128, DC, TB), BF16)
        x2bR = [Res() for _ in range(DC)]
        SB = sbuf("SB", (128, DC, TB), BF16)
        sbR = [Res() for _ in range(DC)]
        XN2 = sbuf("XN2", (128, DC, TB), BF16)
        xn2R = [Res() for _ in range(DC)]
        SQC = XN2
        sqcR = xn2R
        LGF2 = sbuf("LGF2", (16, TB), F32)
        lgf2R = Res()
        LGA2 = sbuf("LGA2", (128, 4, 16), F32)
        lga2R = [Res() for _ in range(4)]
        GTSF = sbuf("GTSF", (128, 4, 16), F32)
        gtsfR = [Res() for _ in range(4)]
        gATf = sbuf("gATf", (16, TB), BF16)
        gatfR = Res()
        gstg2 = sbuf("gstg2", (16, 128), BF16)
        gstg2R = Res()
        grow = sbuf("grow", (1, TB), BF16)
        growR = Res()
        wm1 = [sbuf("wm10", (128, DC, FH), BF16), sbuf("wm11", (128, DC, FH), BF16)]
        wm1R = [Res(), Res()]
        wm2 = [sbuf("wm20", (128, 4, D), BF16), sbuf("wm21", (128, 4, D), BF16)]
        wm2R = [Res(), Res()]
        HST = sbuf("HST", (128, 2, TB), BF16)
        hstR = [Res(), Res()]
        HP = sbuf("HP", (128, 8, 4, TB), BF16)
        hpR = [[Res() for _ in range(4)] for _ in range(8)]
        OFF = sbuf("OFF", (128, DC, TB), F32)
        offR = [Res() for _ in range(DC)]
        OSTG = sbuf("OSTG", (128, 1, D), F32)
        ostgR = [Res()]

        def load_cache(name, dst, dstR, tb, extra=()):
            deps = list(extra)
            for dc in range(DC):
                deps += dstR[dc].wr_deps()
            tk_ = dma("gpsimd", dst[:], t[name][tb], waits=deps)
            for dc in range(DC):
                dstR[dc].set_w(tk_)
            return [tk_] * DC

        out_toks = []
        for tb in range(NTB):
            bwait = [final_toks[-1]] if (do_b and tb == 0) else []
            tx2 = load_cache("x2cache", X2B, x2bR, tb, extra=bwait)
            tsb = load_cache("scache", SB, sbR, tb, extra=bwait)
            for dc in range(DC):
                tsq = vop("tensor_tensor", eng="gpsimd", waits=[tx2[dc]] + sqcR[dc].wr_deps(),
                          out=SQC[:, dc, :], in0=X2B[:, dc, :], in1=X2B[:, dc, :], op=ALU.mult)
                sqcR[dc].set_w(tsq)
            m1 = m2 = None
            for dc in range(DC):
                m1 = mm(brow(4), onescol[:], X2B[:, dc, :], start=(dc == 0), stop=(dc == DC - 1),
                        waits=[tx2[dc]] + CONST + (bankR[4].wr_deps() if dc == 0 else []),
                        tok=(dc == DC - 1))
                m2 = mm(brow(5), onescol[:], SQC[:, dc, :], start=(dc == 0), stop=(dc == DC - 1),
                        waits=[sqcR[dc].w] + (bankR[5].wr_deps() if dc == 0 else []),
                        tok=(dc == DC - 1))
            bankR[4].set_w(m1)
            bankR[5].set_w(m2)
            for dc in range(DC):
                x2bR[dc].add_r(m1)
                sqcR[dc].add_r(m2)
            if dbg and tb == 0:
                dbg_dump("X2BE", X2B[:], [m1, m2], x2bR)
            trb, tmr_ = ln_rows(4, 5, [m1, m2], True)
            tbc = mm(bank(6), ones1x128[:], rowbf[:, 0:TB], start=True, stop=True,
                     waits=[trb] + CONST + bankR[6].wr_deps())
            bankR[6].set_w(tbc)
            rowbfR.add_r(tbc)
            tbc7 = mm(bank(7), ones1x128[:], rowbf[:, TB:2 * TB], start=True, stop=True,
                      waits=[tmr_] + CONST + bankR[7].wr_deps())
            bankR[7].set_w(tbc7)
            rowbfR.add_r(tbc7)
            xn2_toks = []
            for dc in range(DC):
                t1_ = vop("tensor_tensor", waits=[tbc, x2bR[dc].w, sqcR[dc].readers[-1] if sqcR[dc].readers else None, m2],
                          out=XN2[:, dc, :], in0=X2B[:, dc, :], in1=bank(6), op=ALU.mult)
                x2bR[dc].add_r(t1_)
                bankR[6].add_r(t1_)
                t2_ = vop("tensor_tensor", waits=[tbc7, t1_],
                          out=XN2[:, dc, :], in0=XN2[:, dc, :], in1=bank(7), op=ALU.subtract)
                bankR[7].add_r(t2_)
                t3_ = vop("tensor_scalar", eng="gpsimd", waits=[t2_] + tkw2 + tkb2,
                          out=XN2[:, dc, :], in0=XN2[:, dc, :],
                          scalar1=ln2w[:, dc:dc + 1], scalar2=ln2b_f[:, dc:dc + 1],
                          op0=ALU.mult, op1=ALU.add)
                xn2R[dc].set_w(t3_)
                xn2_toks.append(t3_)
            mt = None
            for ci in range(DC):
                mt = mm(PS[0:16, 1024:1024 + TB], fgws[:, ci, :], XN2[:, ci, :],
                        start=(ci == 0), stop=(ci == DC - 1),
                        waits=(xn2_toks + prepC_toks + bankR[2].wr_deps()) if ci == 0 else [],
                        tok=(ci == DC - 1))
            bankR[2].set_w(mt)
            for dc in range(DC):
                xn2R[dc].add_r(mt)
            tlg = vop("tensor_copy", waits=[mt] + lgf2R.wr_deps(), out=LGF2[:],
                      in_=PS[0:16, 1024:1024 + TB])
            bankR[2].add_r(tlg)
            lgf2R.set_w(tlg)
            tla_toks = []
            for tt in range(4):
                tpt = k.op("tensor",
                           lambda e, o=PS[:, 1536:1536 + 16], i=LGF2[0:16, tt * 128:(tt + 1) * 128]:
                           e.transpose(o, i, ident[0:16, 0:16]),
                           waits=[tlg, tid] + bankR[3].wr_deps(), sem=sT)
                bankR[3].set_w(tpt)
                lgf2R.add_r(tpt)
                tla = vop("tensor_copy", waits=[tpt] + lga2R[tt].wr_deps(),
                          out=LGA2[:, tt, :], in_=PS[:, 1536:1536 + 16])
                bankR[3].add_r(tla)
                lga2R[tt].set_w(tla)
                tla_toks.append(tla)
            tg = gates_chain(LGA2[:, :, :], GTSF[:, :, :], EF, 4, tla_toks,
                             [gtsfR[i] for i in range(4)])
            for tt in range(4):
                tgt = k.op("tensor",
                           lambda e, o=PS[0:16, 1536 + 16:1536 + 16 + 128], i=GTSF[:, tt, :]:
                           e.transpose(o, i, ident[:]),
                           waits=[tg, tid] + bankR[3].wr_deps(), sem=sT)
                bankR[3].set_w(tgt)
                gtsfR[tt].add_r(tgt)
                tge = vop("tensor_copy", waits=[tgt] + (gatfR.wr_deps() if tt == 0 else []),
                          out=gATf[0:16, tt * 128:(tt + 1) * 128],
                          in_=PS[0:16, 1536 + 16:1536 + 16 + 128])
                bankR[3].add_r(tge)
                if tt == 3:
                    gatfR.set_w(vfence(tge))
            tgd = dma("gpsimd", t["gatfd"], gATf[:], waits=[gatfR.w])
            gatfR.add_r(tgd)
            if dbg and tb == 0:
                dbg_dump("XN2", XN2[:], [xn2R[c].w for c in range(DC)], xn2R)
                dbg_dump("GATF", gATf[:], [gatfR.w], [gatfR])
                dbg_dump("X2B", X2B[:], [x2bR[c].w for c in range(DC)], x2bR)
                dbg_dump("ROWC", rowbf[:], [rowbfR.w], [rowbfR])

            for ph in range(2):
                hp_toks = []
                for f8 in range(8):
                    f = ph * 8 + f8
                    slot = f % 2
                    deps = wm1R[slot].wr_deps()
                    if tb == 0:
                        w1toks = []
                        st_tok = None
                        for ci in range(DC):
                            sl_ = ci % 2
                            tl = dma("scalar", wstC[:, sl_, 0:FH],
                                     t["ff_w1"][f, ci * 128:(ci + 1) * 128, :],
                                     waits=wstCR[sl_].wr_deps())
                            wstCR[sl_].set_w(tl)
                            tc = vop("tensor_copy", eng="gpsimd",
                                     waits=[tl] + (deps if ci == 0 else []),
                                     out=wm1[slot][:, ci, :], in_=wstC[:, sl_, 0:FH])
                            wstCR[sl_].add_r(tc)
                            w1toks.append(tc)
                            st_tok = dma("gpsimd", t["w1_c"][f][:, ci * FH:(ci + 1) * FH],
                                         wm1[slot][:, ci, :], waits=[tc])
                        wm1R[slot].set_w(w1toks[-1])
                        for tk_ in w1toks[:-1]:
                            wm1R[slot].add_r(tk_)
                        wm1R[slot].add_r(st_tok)
                        prep_w1[f] = st_tok
                    else:
                        tkw = dma("scalar", wm1[slot][:], t["w1_c"][f],
                                  waits=deps + ([prep_w1[f]] if tb == 1 else []))
                        wm1R[slot].set_w(tkw)
                        w1toks = [tkw]
                    tgr = dma("gpsimd", grow[:], t["gatfd"][f], waits=[tgd] + growR.wr_deps())
                    growR.set_w(tgr)
                    tgb = mm(bank(5), ones1x128[:], grow[:], start=True, stop=True,
                             waits=[tgr] + CONST + bankR[5].wr_deps())
                    bankR[5].set_w(tgb)
                    growR.add_r(tgb)
                    for hb in range(4):
                        bk_i = 2 + hb % 2
                        mt = None
                        for ci in range(DC):
                            mt = mm(bank(bk_i), wm1[slot][:, ci, hb * 128:(hb + 1) * 128],
                                    XN2[:, ci, :], start=(ci == 0), stop=(ci == DC - 1),
                                    waits=(w1toks + xn2_toks + bankR[bk_i].wr_deps()) if ci == 0 else [],
                                    tok=(ci == DC - 1))
                        bankR[bk_i].set_w(mt)
                        wm1R[slot].add_r(mt)
                        for dc in range(DC):
                            xn2R[dc].add_r(mt)
                        th = act(HST[:, hb % 2, :], bank(bk_i), AF.Relu,
                                 bias=b1c[:, f * 4 + hb:f * 4 + hb + 1],
                                 waits=[mt] + ([prep_b1[f]] if tb == 0 else []) + hstR[hb % 2].wr_deps())
                        bankR[bk_i].add_r(th)
                        hstR[hb % 2].set_w(th)
                        tp = vop("tensor_tensor", waits=[th, tgb] + hpR[f8][hb].wr_deps(),
                                 out=HP[:, f8, hb, :], in0=HST[:, hb % 2, :], in1=bank(5),
                                 op=ALU.mult)
                        hstR[hb % 2].add_r(tp)
                        bankR[5].add_r(tp)
                        hpR[f8][hb].set_w(tp)
                        hp_toks.append(tp)
                lastm = [None] * 8
                for f8 in range(8):
                    f = ph * 8 + f8
                    slot = f % 2
                    deps = wm2R[slot].wr_deps()
                    if tb == 0:
                        w2toks = []
                        st_tok = None
                        for hc in range(4):
                            sl_ = hc % 2
                            tl = dma("scalar", wstC[:, sl_, :],
                                     t["ff_w2"][f, hc * 128:(hc + 1) * 128, :],
                                     waits=wstCR[sl_].wr_deps())
                            wstCR[sl_].set_w(tl)
                            tc = vop("tensor_copy", eng="gpsimd",
                                     waits=[tl] + (deps if hc == 0 else []),
                                     out=wm2[slot][:, hc, :], in_=wstC[:, sl_, :])
                            wstCR[sl_].add_r(tc)
                            w2toks.append(tc)
                            st_tok = dma("gpsimd", t["w2_c"][f][:, hc * D:(hc + 1) * D],
                                         wm2[slot][:, hc, :], waits=[tc])
                        wm2R[slot].set_w(w2toks[-1])
                        for tk_ in w2toks[:-1]:
                            wm2R[slot].add_r(tk_)
                        wm2R[slot].add_r(st_tok)
                        prep_w2[f] = st_tok
                    else:
                        tkw = dma("scalar", wm2[slot][:], t["w2_c"][f],
                                  waits=deps + ([prep_w2[f]] if tb == 1 else []))
                        wm2R[slot].set_w(tkw)
                        w2toks = [tkw]
                    for dc in range(DC):
                        first = (f8 == 0)
                        w0 = []
                        if first:
                            w0 = hp_toks + bankR[dc].wr_deps()
                        if dc == 0:
                            w0 = w0 + w2toks
                        mt = None
                        for hc in range(4):
                            last_mm = (f8 == 7 and hc == 3 and ph == 1 and False)
                            mt = mm(bank(dc), wm2[slot][:, hc, dc * 128:(dc + 1) * 128],
                                    HP[:, f8, hc, :],
                                    start=(f8 == 0 and hc == 0), stop=False,
                                    waits=w0 if hc == 0 else [], tok=(hc == 3))
                        lastm[dc] = mt
                        wm2R[slot].add_r(mt)
                        for hc in range(4):
                            hpR[f8][hc].add_r(mt)
                for dc in range(DC):
                    if ph == 1:
                        mfin = mm(bank(dc), fb2s[0:16, dc * 128:(dc + 1) * 128], gATf[:],
                                  start=False, stop=True, waits=[lastm[dc], gatfR.w, tfb2])
                        gatfR.add_r(mfin)
                    else:
                        mfin = lastm[dc]
                    bankR[dc].set_w(mfin)
                    tse = vop("tensor_tensor", waits=[mfin, sbR[dc].w, tsb[dc]],
                              out=SB[:, dc, :], in0=SB[:, dc, :], in1=bank(dc), op=ALU.add)
                    bankR[dc].add_r(tse)
                    sbR[dc].set_w(tse)

            for dc in range(DC):
                tsq = vop("tensor_tensor", eng="gpsimd", waits=[sbR[dc].w] + sqcR[dc].wr_deps(),
                          out=SQC[:, dc, :], in0=SB[:, dc, :], in1=SB[:, dc, :], op=ALU.mult)
                sqcR[dc].set_w(tsq)
                sbR[dc].add_r(tsq)
            m1 = m2 = None
            for dc in range(DC):
                m1 = mm(brow(4), onescol[:], SB[:, dc, :], start=(dc == 0), stop=(dc == DC - 1),
                        waits=[sbR[dc].w] + (bankR[4].wr_deps() if dc == 0 else []),
                        tok=(dc == DC - 1))
                m2 = mm(brow(5), onescol[:], SQC[:, dc, :], start=(dc == 0), stop=(dc == DC - 1),
                        waits=[sqcR[dc].w] + (bankR[5].wr_deps() if dc == 0 else []),
                        tok=(dc == DC - 1))
            bankR[4].set_w(m1)
            bankR[5].set_w(m2)
            for dc in range(DC):
                sbR[dc].add_r(m1)
                sqcR[dc].add_r(m2)
            trb, tmr_ = ln_rows(4, 5, [m1, m2], True)
            tbc = mm(bank(6), ones1x128f[:], rowsA[:, TB:2 * TB], start=True, stop=True,
                     waits=[trb, tmr_] + CONSTF + bankR[6].wr_deps())
            bankR[6].set_w(tbc)
            rowsR.add_r(tbc)
            tbc7 = mm(bank(7), ones1x128f[:], rowsA[:, 2 * TB:3 * TB], start=True, stop=True,
                      waits=[tmr_] + CONSTF + bankR[7].wr_deps())
            bankR[7].set_w(tbc7)
            rowsR.add_r(tbc7)
            off_toks = []
            for dc in range(DC):
                t1_ = vop("tensor_tensor", waits=[tbc, sbR[dc].w] + offR[dc].wr_deps(),
                          out=OFF[:, dc, :], in0=SB[:, dc, :], in1=bank(6), op=ALU.mult)
                sbR[dc].add_r(t1_)
                bankR[6].add_r(t1_)
                t2_ = vop("tensor_tensor", waits=[tbc7, t1_],
                          out=OFF[:, dc, :], in0=OFF[:, dc, :], in1=bank(7), op=ALU.subtract)
                bankR[7].add_r(t2_)
                offR[dc].set_w(t2_)
                off_toks.append(t2_)
            for tt in range(4):
                sl = 0
                ev_toks = []
                for dc in range(DC):
                    bnk = dc % 2
                    tptr = k.op("tensor",
                                lambda e, o=PS[:, bnk * 512:bnk * 512 + 128],
                                i=OFF[:, dc, tt * 128:(tt + 1) * 128]:
                                e.transpose(o, i, ident[:]),
                                waits=[off_toks[dc], tid] + bankR[bnk].wr_deps(), sem=sT)
                    bankR[bnk].set_w(tptr)
                    offR[dc].add_r(tptr)
                    tev = vop("tensor_copy", waits=[tptr] + (ostgR[sl].wr_deps() if dc == 0 else []),
                              out=OSTG[:, sl, dc * 128:(dc + 1) * 128],
                              in_=PS[:, bnk * 512:bnk * 512 + 128])
                    bankR[bnk].add_r(tev)
                    ev_toks.append(tev)
                ostgR[sl].set_w(ev_toks[-1])
                for tk_ in ev_toks[:-1]:
                    ostgR[sl].add_r(tk_)
                tdo = dma("sync", t["out"][(tb * 4 + tt) * 128:(tb * 4 + tt + 1) * 128, :],
                          OSTG[:, sl, :], waits=ev_toks)
                ostgR[sl].add_r(tdo)
                out_toks.append(tdo)
        final_toks += out_toks

    k.ops["gpsimd"].append((lambda e: None, [tk for tk in final_toks], None, 0))

    k.finalize()
    return nc, t


def _ins_a(inputs, core):
    return {
        "x": np.ascontiguousarray(inputs["L0_attn_x"][core]),
        "Wk": inputs["L0_attn_Wk"], "bk": inputs["L0_attn_bk"],
        "Wv": inputs["L0_attn_Wv"], "bv": inputs["L0_attn_bv"],
        "Wq": inputs["L0_attn_Wq"], "bq": inputs["L0_attn_bq"],
        "Wkp": inputs["L0_attn_Wkp"], "bkp": inputs["L0_attn_bkp"],
        "Wqp": inputs["L0_attn_Wqp"], "bqp": inputs["L0_attn_bqp"],
        "phase_scale": inputs["L0_attn_phase_scale"],
        "ln_g": inputs["L0_attn_ln_g"], "ln_b": inputs["L0_attn_ln_b"],
        "Wo": inputs["L0_attn_Wo"], "bo": inputs["L0_attn_bo"],
    }


def _ins_b(inputs):
    return {
        "attn_gate_w": inputs["L0_ffn_attn_gate_w"],
        "q_w": inputs["L0_ffn_q_w"], "q_b": inputs["L0_ffn_q_b"],
        "k_w": inputs["L0_ffn_k_w"], "k_b": inputs["L0_ffn_k_b"],
        "v_w": inputs["L0_ffn_v_w"], "v_b": inputs["L0_ffn_v_b"],
        "o_w": inputs["L0_ffn_o_w"], "o_b": inputs["L0_ffn_o_b"],
        "rel_table": inputs["L0_ffn_rel_table"],
        "ln1_w": inputs["L0_ffn_ln1_w"], "ln1_b": inputs["L0_ffn_ln1_b"],
    }


def _ins_c(inputs):
    return {
        "ff_gate_w": inputs["L0_ffn_ff_gate_w"],
        "ff_w1": inputs["L0_ffn_ff_w1"], "ff_b1": inputs["L0_ffn_ff_b1"],
        "ff_w2": inputs["L0_ffn_ff_w2"], "ff_b2": inputs["L0_ffn_ff_b2"],
        "ln2_w": inputs["L0_ffn_ln2_w"], "ln2_b": inputs["L0_ffn_ln2_b"],
    }


def kernel(**inputs):
    cores = list(range(8))
    nc, _ = build_program("ALL")
    im = []
    for c in cores:
        m = _ins_a(inputs, c)
        m.update(_ins_b(inputs))
        m.update(_ins_c(inputs))
        im.append(m)
    res = run_bass_kernel_spmd(nc, im, core_ids=cores)
    return np.stack([res.results[c]["out"] for c in cores])


# version 08
# speedup vs baseline: 1.2015x; 1.0230x over previous
import contextlib
import math

import numpy as np
import ml_dtypes

import concourse.bass as bass
import concourse.mybir as mybir
from concourse.bass import AP
from concourse.bass_utils import run_bass_kernel_spmd

F32 = mybir.dt.float32
BF16 = mybir.dt.bfloat16
I32 = mybir.dt.int32
AF = mybir.ActivationFunctionType
ALU = mybir.AluOpType
BF = ml_dtypes.bfloat16

NT = 4096
D = 1024
DC = 8
TB = 512
NTB = NT // TB
CHUNK = 64
HD = 128
EA, EF, FH = 8, 16, 512
MAXP = 64
EPS = 1e-5
INV2PI = 1.0 / (2.0 * math.pi)
TWOPI = 2.0 * math.pi

ENGINES = ("tensor", "vector", "scalar", "gpsimd", "sync")


class Sem:
    __slots__ = ("h", "n", "name")

    def __init__(self, h, name):
        self.h = h
        self.n = 0
        self.name = name


class Tok:
    __slots__ = ("sem", "val")

    def __init__(self, sem, val):
        self.sem = sem
        self.val = val


class Res:
    __slots__ = ("w", "readers")

    def __init__(self):
        self.w = None
        self.readers = []

    def wr_deps(self):
        return ([self.w] if self.w else []) + self.readers

    def rd_deps(self):
        return [self.w] if self.w else []

    def set_w(self, tok):
        self.w = tok
        self.readers = []

    def add_r(self, tok):
        if tok is not None:
            self.readers.append(tok)


class KB:

    def __init__(self, nc):
        self.nc = nc
        self.ops = {e: [] for e in ENGINES}
        self.stack = contextlib.ExitStack()
        self.n_inst = 0

    def sem(self, name):
        return Sem(self.stack.enter_context(self.nc.semaphore(name)), name)

    def op(self, eng, fn, waits=(), sem=None, inc=1):
        ws = [w for w in waits if w is not None]
        tok = None
        if sem is not None:
            sem.n += inc
            tok = Tok(sem, sem.n)
        self.ops[eng].append((fn, ws, sem, inc))
        self.n_inst += 1
        return tok

    def finalize(self):
        with self.nc.Block() as block:
            for ename in ENGINES:
                lst = self.ops[ename]
                if not lst:
                    continue

                def make(lst, ename=ename):
                    def emit(eng):
                        maxw = {}
                        for idx, (fn, ws, sem, inc) in enumerate(lst):
                            for tok in ws:
                                if maxw.get(id(tok.sem), -1) < tok.val:
                                    eng.wait_ge(tok.sem.h, tok.val)
                                    maxw[id(tok.sem)] = tok.val
                            try:
                                inst = fn(eng)
                            except Exception as ex:
                                raise RuntimeError(f"emit failed: engine={ename} opidx={idx}: {ex}") from ex
                            if sem is not None and inst is not None:
                                inst.then_inc(sem.h, inc)
                    return emit

                getattr(block, ename)(make(lst))
        self.stack.close()


def build_program(phase, dbg=False):
    nc = bass.Bass("TRN2", target_bir_lowering=False)
    k = KB(nc)
    t = {}

    def dram(name, shape, dtype, kind):
        t[name] = nc.dram_tensor(name, shape, dtype, kind=kind).ap()
        return t[name]

    def sbuf(name, shape, dtype):
        t[name] = k.stack.enter_context(nc.sbuf_tensor(name, list(shape), dtype))
        return t[name]

    do_a = phase in ("A", "ALL")
    do_b = phase in ("B", "ALL")
    do_c = phase in ("C", "ALL")
    inner = "Internal"

    if do_a:
        dram("x", [NT, D], F32, "ExternalInput")
        for nm in ("Wk", "Wv", "Wq", "Wkp", "Wqp", "Wo"):
            dram(nm, [D, D], F32, "ExternalInput")
        for nm in ("bk", "bv", "bq", "bkp", "bqp", "phase_scale", "ln_g", "ln_b", "bo"):
            dram(nm, [D], F32, "ExternalInput")
        dram("wcacheA", [6, 128, DC * D], BF16, inner)
    dram("x1cache", [NTB, 128, DC * TB], BF16,
         "ExternalOutput" if phase == "A" else ("ExternalInput" if phase in ("B",) else inner))
    if do_b:
        dram("attn_gate_w", [D, EA], F32, "ExternalInput")
        dram("q_w", [EA, D, HD], F32, "ExternalInput")
        dram("q_b", [EA, HD], F32, "ExternalInput")
        dram("k_w", [D, HD], F32, "ExternalInput")
        dram("k_b", [HD], F32, "ExternalInput")
        dram("v_w", [D, HD], F32, "ExternalInput")
        dram("v_b", [HD], F32, "ExternalInput")
        dram("o_w", [EA, HD, D], F32, "ExternalInput")
        dram("o_b", [EA, D], F32, "ExternalInput")
        dram("rel_table", [2 * MAXP + 1, HD], F32, "ExternalInput")
        dram("ln1_w", [D], F32, "ExternalInput")
        dram("ln1_b", [D], F32, "ExternalInput")
        dram("wq_c", [EA, 128, DC * HD], BF16, inner)
        dram("Rimg", [4, 128, 640], BF16, inner)
        dram("rowbounceB", [D], F32, inner)
        dram("owc", [DC, 128, EA * 128], BF16, inner)
        dram("rows1d", [2, NT], BF16, inner)
        dram("gatc", [8, NT], BF16, inner)
    if do_b or do_c:
        kind_s = inner if phase == "ALL" else ("ExternalOutput" if phase == "B" else "ExternalInput")
        dram("scache", [NTB, 128, DC * TB], BF16, kind_s)
        dram("x2cache", [NTB, 128, DC * TB], BF16, kind_s)
    if do_c:
        dram("ff_gate_w", [D, EF], F32, "ExternalInput")
        dram("ff_w1", [EF, D, FH], F32, "ExternalInput")
        dram("ff_b1", [EF, FH], F32, "ExternalInput")
        dram("ff_w2", [EF, FH, D], F32, "ExternalInput")
        dram("ff_b2", [EF, D], F32, "ExternalInput")
        dram("ln2_w", [D], F32, "ExternalInput")
        dram("ln2_b", [D], F32, "ExternalInput")
        dram("w1_c", [EF, 128, DC * FH], BF16, inner)
        dram("w2_c", [EF, 128, 4 * D], BF16, inner)
        dram("gatfd", [EF, TB], BF16, inner)
        dram("out", [NT, D], F32, "ExternalOutput")

    sT, sV, sS, sG = k.sem("sT"), k.sem("sV"), k.sem("sS"), k.sem("sG")
    sDs, sDa, sDg = k.sem("sDs"), k.sem("sDa"), k.sem("sDg")
    esem = dict(tensor=sT, vector=sV, scalar=sS, gpsimd=sG)
    qsem = dict(sync=sDs, scalar=sDa, gpsimd=sDg)

    def dma(queue, out, in_, waits=()):
        return k.op(queue, lambda e, o=out, i=in_: e.dma_start(out=o, in_=i),
                    waits=waits, sem=qsem[queue], inc=16)

    def xpose_dma(out, in_, waits=()):
        return k.op("sync", lambda e, o=out, i=in_: e.dma_start_transpose(o, i),
                    waits=waits, sem=sDs, inc=16)

    def mm(out, lhsT, rhs, start, stop, waits=(), tok=True):
        return k.op("tensor",
                    lambda e, o=out, l=lhsT, r=rhs, st=start, sp=stop:
                    e.matmul(o, l, r, start=st, stop=sp),
                    waits=waits, sem=sT if tok else None)

    def act(out, in_, func, bias=None, scale=None, accum=None, waits=()):
        kw = {}
        if bias is not None:
            kw["bias"] = bias
        if scale is not None:
            kw["scale"] = scale
        if accum is not None:
            kw["accum_out"] = accum
        return k.op("scalar",
                    lambda e, o=out, i=in_, f=func, kw=kw: e.activation(out=o, in_=i, func=f, **kw),
                    waits=waits, sem=sS)

    def vop(name, eng="vector", waits=(), **kw):
        return k.op(eng, lambda e, kw=kw, n=name: getattr(e, n)(**kw),
                    waits=waits, sem=esem[eng])

    def vfence(tok):
        return k.op("vector", lambda e: e.drain(), waits=[tok], sem=sV)

    def sfence(tok):
        return k.op("scalar", lambda e: e.drain(), waits=[tok], sem=sS)

    poolBF = k.stack.enter_context(nc.sbuf_tensor("poolBF", [128, 80896], BF16))
    poolF = k.stack.enter_context(nc.sbuf_tensor("poolF", [128, 9216], F32))
    poolI = k.stack.enter_context(nc.sbuf_tensor("poolI", [128, 512], I32))
    _pool_off = {"bf": 0, "f32": 0, "i32": 0}

    def _pool_reset():
        _pool_off["bf"] = _pool_off["f32"] = _pool_off["i32"] = 0

    def pool_sbuf(name, shape, dtype):
        import functools, operator
        n = functools.reduce(operator.mul, shape[1:], 1)
        if dtype == BF16:
            key, pool, esz = "bf", poolBF, 2
        elif dtype == F32:
            key, pool, esz = "f32", poolF, 4
        else:
            key, pool, esz = "i32", poolI, 4
        off = _pool_off[key]
        off = (off + 15) & ~15
        assert off + n <= pool.shape[1], f"pool {key} overflow at {name}: {off}+{n}"
        _pool_off[key] = off + n
        ap = pool[:, off:off + n]
        if shape[0] != 128:
            ap = pool[0:shape[0], off:off + n]
        if len(shape) == 3:
            ap = ap.rearrange("p (a b) -> p a b", a=shape[1], b=shape[2])
        elif len(shape) == 4:
            ap = ap.rearrange("p (a b c) -> p a b c", a=shape[1], b=shape[2], c=shape[3])
        t[name] = ap
        return ap

    def phase_barrier():
        toks = [Tok(x, x.n) for x in (sT, sV, sS, sG, sDs, sDa, sDg) if x.n > 0]
        for eng in ENGINES:
            k.op(eng, lambda e: None, waits=toks)

    ident = sbuf("ident", (128, 128), F32)
    ones1x128 = sbuf("ones1x128", (1, 128), BF16)
    ones1x512 = sbuf("ones1x512", (1, 512), BF16)
    onescol = sbuf("onescol", (128, 1), BF16)

    cst = []
    cst.append(k.op("gpsimd", lambda e: e.memset(ones1x128[:], 1.0), sem=sG))
    cst.append(k.op("gpsimd", lambda e: e.memset(ones1x512[:], 1.0), sem=sG))
    cst.append(k.op("gpsimd", lambda e: e.memset(onescol[:], 1.0), sem=sG))
    cvec = sbuf("cvec", (128, 2), F32)
    cst.append(k.op("gpsimd", lambda e: e.memset(cvec[:, 0:1], math.pi / 2.0), sem=sG))
    cst.append(k.op("gpsimd", lambda e: e.memset(cvec[:, 1:2], EPS), sem=sG))
    HALFPI_AP = cvec[:, 0:1]
    EPS_AP1 = cvec[0:1, 1:2]
    tio = k.op("gpsimd", lambda e: e.iota(ident[:], [[1, 128]], base=0, channel_multiplier=-1,
                                          allow_small_or_imprecise_dtypes=True), sem=sG)
    tid = vop("tensor_scalar", waits=[tio], out=ident[:], in0=ident[:],
              scalar1=0.0, scalar2=None, op0=ALU.is_equal)
    cst.append(tid)
    CONST = cst

    PS = k.stack.enter_context(nc.psum_tensor("PS", [128, 4096], F32))
    bankR = [Res() for _ in range(8)]

    rowsA = sbuf("rowsA", (1, 3 * TB), F32)
    rowbf = sbuf("rowbf", (1, 2 * TB), BF16)
    rowsR = Res()
    rowbfR = Res()

    def bank(i):
        return PS[:, i * 512:(i + 1) * 512]

    def brow(i):
        return PS[0:1, i * 512:(i + 1) * 512]

    final_toks = []

    def dbg_dump(name, ap, waits, res_list):
        if not dbg:
            return
        o = dram("dbg_" + name, list(ap.shape), ap.dtype, "ExternalOutput")
        tk = dma("gpsimd", o, ap, waits=waits)
        for r in res_list:
            r.add_r(tk)
        final_toks.append(tk)

    zfn = sbuf("zfn", (1, 4), F32)
    zfnR = Res()
    gsc = sbuf("gsc", (128, 32), F32)
    gscR = Res()
    geq = sbuf("geq", (128, 3, 4, 16), F32)
    geqR = Res()

    def gates_chain(lg_ap, out_ap, ne, nt4, waits, out_resses):
        d0 = gscR.wr_deps() + geqR.wr_deps()
        m1 = gsc[:, 0:nt4]
        m2 = gsc[:, 4:4 + nt4]
        dd = gsc[:, 8:8 + nt4]
        ed = gsc[:, 12:12 + nt4]
        p1_ = gsc[:, 16:16 + nt4]
        g1 = gsc[:, 20:20 + nt4]
        g2 = gsc[:, 24:24 + nt4]
        eq1 = geq[:, 0, :, 0:ne]
        eq2 = geq[:, 1, :, 0:ne]
        tmp = geq[:, 2, :, 0:ne]

        def bc(ap):
            return ap.unsqueeze(2).broadcast_to((128, nt4, ne))

        tm1 = vop("tensor_reduce", waits=list(waits) + d0, out=m1, in_=lg_ap,
                  axis=mybir.AxisListType.X, op=ALU.max)
        te1 = vop("tensor_tensor", waits=[tm1], out=eq1, in0=lg_ap, in1=bc(m1), op=ALU.is_equal)
        tlm = vop("scalar_tensor_tensor", waits=[te1], out=eq2,
                  in0=eq1, scalar=-1e30, in1=lg_ap, op0=ALU.mult, op1=ALU.add)
        tm2 = vop("tensor_reduce", waits=[tlm], out=m2, in_=eq2,
                  axis=mybir.AxisListType.X, op=ALU.max)
        te2 = vop("tensor_tensor", waits=[tm2], out=eq2, in0=lg_ap, in1=bc(m2), op=ALU.is_equal)
        td = vop("tensor_tensor", waits=[tm2], out=dd, in0=m2, in1=m1, op=ALU.subtract)
        tex = act(ed, dd, AF.Exp, waits=[td])
        tp1 = vop("tensor_scalar", waits=[tex], out=p1_, in0=ed,
                  scalar1=1.0, scalar2=None, op0=ALU.add)
        tg1 = vop("reciprocal", waits=[tp1], out=g1, in_=p1_)
        tg2 = vop("tensor_scalar", waits=[tg1], out=g2, in0=g1,
                  scalar1=-1.0, scalar2=1.0, op0=ALU.mult, op1=ALU.add)
        ta = vop("tensor_tensor", waits=[tg1, te1], out=tmp, in0=eq1, in1=bc(g1), op=ALU.mult)
        tb_ = vop("tensor_tensor", waits=[tg2, te2], out=eq2, in0=eq2, in1=bc(g2), op=ALU.mult)
        od = []
        for r in out_resses:
            od += r.wr_deps()
        tfin = vop("tensor_tensor", waits=[ta, tb_] + od, out=out_ap, in0=tmp, in1=eq2, op=ALU.add)
        tvd = k.op("vector", lambda e: e.drain(), waits=[tfin], sem=sV)
        gscR.set_w(tvd)
        geqR.set_w(tvd)
        for r in out_resses:
            r.set_w(tvd)
        return tvd

    def chunkload(dst, src, waits=()):
        toks = []
        for c in range(DC):
            toks.append(dma("sync", dst[:, c:c + 1], src[c * 128:(c + 1) * 128], waits=waits))
        return toks

    def ln_rows(sum_bank, sq_bank, waits, need_mu):
        deps = list(waits) + rowsR.wr_deps()
        tmu = vop("tensor_scalar", waits=deps, out=rowsA[:, 0:TB], in0=brow(sum_bank),
                  scalar1=1.0 / D, scalar2=None, op0=ALU.mult)
        tm2 = vop("tensor_tensor", waits=[tmu], out=rowsA[:, 2 * TB:3 * TB],
                  in0=rowsA[:, 0:TB], in1=rowsA[:, 0:TB], op=ALU.mult)
        tvar = vop("scalar_tensor_tensor", waits=[tm2], out=rowsA[:, TB:2 * TB],
                   in0=brow(sq_bank), scalar=1.0 / D, in1=rowsA[:, 2 * TB:3 * TB],
                   op0=ALU.mult, op1=ALU.subtract)
        bankR[sum_bank].add_r(tmu)
        bankR[sq_bank].add_r(tvar)
        tln = act(rowsA[:, TB:2 * TB], rowsA[:, TB:2 * TB], AF.Ln, bias=EPS_AP1, waits=[tvar] + CONST)
        tex = act(rowsA[:, TB:2 * TB], rowsA[:, TB:2 * TB], AF.Exp, scale=-0.5, waits=[tln])
        deps2 = rowbfR.wr_deps()
        trb = vop("tensor_copy", waits=[tex] + deps2, out=rowbf[:, 0:TB],
                  in_=rowsA[:, TB:2 * TB])
        tmr = None
        if need_mu:
            tmm = vop("tensor_tensor", waits=[tex, tmu], out=rowsA[:, 2 * TB:3 * TB],
                      in0=rowsA[:, 0:TB], in1=rowsA[:, TB:2 * TB], op=ALU.mult)
            tmr = vop("tensor_copy", waits=[tmm] + deps2, out=rowbf[:, TB:2 * TB],
                      in_=rowsA[:, 2 * TB:3 * TB])
        tsd = k.op("scalar", lambda e: e.drain(), waits=[tex], sem=sS)
        tvd = k.op("vector", lambda e: e.drain(), waits=[tsd, tmr if tmr else trb], sem=sV)
        rowsR.set_w(tvd)
        rowbfR.set_w(tvd)
        return tvd, (tvd if need_mu else None)

    def ln_rows_f32(sum_bank, sq_bank, waits):
        return ln_rows(sum_bank, sq_bank, waits, True)

    if do_a:
        sbuf = pool_sbuf
        _pool_reset()
        scanmask = sbuf("scanmask", (128, TB), BF16)
        tmsk = k.op("gpsimd", lambda e: e.memset(scanmask[:], 1.0), sem=sG)
        for cs in range(0, TB, CHUNK):
            tmsk = k.op("gpsimd", lambda e, c=cs: e.memset(scanmask[:, c:c + 1], 0.0),
                        waits=[tmsk], sem=sG)
        CONST = CONST + [tmsk]
        biasA = sbuf("biasA", (128, 8 * DC), F32)
        BK, BV32, BQ, BKP, BQP, PSC, PS2PI, LNG = range(8)

        def bcol(idx, c):
            return biasA[:, idx * DC + c: idx * DC + c + 1]

        wstage = sbuf("wstage", (128, 2, D), F32)
        wconv = sbuf("wconv", (128, 2, D), BF16)
        lnbch = sbuf("lnbch", (128, DC), BF16)
        lnb_f = sbuf("lnb_f", (128, DC), F32)
        btrow = sbuf("btrow", (1, D), BF16)

        tb_bk = chunkload(biasA[:, BK * DC:], t["bk"])
        tb_bq = chunkload(biasA[:, BQ * DC:], t["bq"])
        tb_bkp = chunkload(biasA[:, BKP * DC:], t["bkp"])
        tb_bqp = chunkload(biasA[:, BQP * DC:], t["bqp"])
        tb_ps = chunkload(biasA[:, PSC * DC:], t["phase_scale"])
        tb_lng = chunkload(biasA[:, LNG * DC:], t["ln_g"])
        tb_bv = chunkload(biasA[:, BV32 * DC:], t["bv"])
        tv1 = vop("tensor_scalar", waits=tb_bv, out=biasA[:, BV32 * DC:(BV32 + 1) * DC],
                  in0=biasA[:, BV32 * DC:(BV32 + 1) * DC], scalar1=1.0 / 32.0, scalar2=None, op0=ALU.mult)
        tv2 = vop("tensor_scalar", waits=tb_ps, out=biasA[:, PS2PI * DC:(PS2PI + 1) * DC],
                  in0=biasA[:, PSC * DC:(PSC + 1) * DC], scalar1=INV2PI, scalar2=None, op0=ALU.mult)
        tb_lnb = chunkload(lnb_f, t["ln_b"])
        t_lnb = vop("tensor_copy", waits=tb_lnb, out=lnbch[:], in_=lnb_f[:])

        biasA_rdy = {BK: tb_bk, BQ: tb_bq, BKP: tb_bkp, BQP: tb_bqp,
                     PSC: tb_ps, LNG: tb_lng, BV32: [tv1], PS2PI: [tv2]}

        wnames = ["Wk", "Wv", "Wq", "Wkp", "Wqp", "Wo"]
        prepA_tok = [None] * 6
        conv_hist = [None, None]
        store_hist = [None, None]
        bt_mms = []
        prev_store = None
        for wi, wn in enumerate(wnames):
            for ci in range(DC):
                sl_ = (wi * DC + ci) % 2
                tl = dma("scalar" if sl_ == 0 else "sync",
                         wstage[:, sl_, :], t[wn][ci * 128:(ci + 1) * 128, :],
                         waits=[conv_hist[sl_]] if conv_hist[sl_] else [])
                waits_c = [tl] + ([store_hist[sl_]] if store_hist[sl_] else [])
                tc = vop("tensor_copy", eng="gpsimd", waits=waits_c, out=wconv[:, sl_, :], in_=wstage[:, sl_, :])
                conv_hist[sl_] = tc
                src_tok = tc
                if wn == "Wo":
                    m1 = mm(brow(4), lnbch[:, ci:ci + 1], wconv[:, sl_, 0:512],
                            start=(ci == 0), stop=(ci == DC - 1),
                            waits=[tc, t_lnb] + bankR[4].wr_deps() if ci == 0 else [tc, t_lnb])
                    m2 = mm(brow(5), lnbch[:, ci:ci + 1], wconv[:, sl_, 512:1024],
                            start=(ci == 0), stop=(ci == DC - 1),
                            waits=[tc] + (bankR[5].wr_deps() if ci == 0 else []))
                    bt_mms += [m1, m2]
                    src_tok = vop("tensor_scalar", eng="gpsimd", waits=[m1, m2] + biasA_rdy[LNG],
                                  out=wconv[:, sl_, :], in0=wconv[:, sl_, :], scalar1=bcol(LNG, ci),
                                  scalar2=None, op0=ALU.mult)
                    conv_hist[sl_] = src_tok
                prev_store = dma("sync" if sl_ == 0 else "scalar",
                                 t["wcacheA"][wi][:, ci * D:(ci + 1) * D],
                                 wconv[:, sl_, :], waits=[src_tok])
                store_hist[sl_] = prev_store
                prepA_tok[wi] = prev_store
        bankR[4].set_w(bt_mms[-2])
        bankR[5].set_w(bt_mms[-1])
        tbt = None
        for hb in range(2):
            tb_bo = dma("sync", rowsA[:, 0:TB], t["bo"][hb * 512:(hb + 1) * 512].unsqueeze(0),
                        waits=rowsR.wr_deps())
            rowsR.set_w(tb_bo)
            tbt1 = vop("tensor_tensor", waits=[bankR[4 + hb].w, tb_bo],
                       out=rowsA[:, TB:2 * TB], in0=brow(4 + hb), in1=rowsA[:, 0:TB],
                       op=ALU.add)
            rowsR.set_w(tbt1)
            bankR[4 + hb].add_r(tbt1)
            tbt = vop("tensor_copy", waits=[tbt1], out=btrow[:, hb * 512:(hb + 1) * 512],
                      in_=rowsA[:, TB:2 * TB])
            rowsR.add_r(tbt)
            tbt = vfence(tbt)

        wbuf = [sbuf("wbuf0", (128, DC, D), BF16), sbuf("wbuf1", (128, DC, D), BF16)]
        wbufR = [Res(), Res()]
        xinf = sbuf("xinf", (128, 2, D), F32)
        xinfR = [Res(), Res()]
        xinb = sbuf("xinb", (128, 2, D), BF16)
        xinbR = [Res(), Res()]
        xT = sbuf("xT", (128, 2, DC, TB), BF16)
        xTR = [Res(), Res()]
        KEY = sbuf("KEY", (128, DC, TB), BF16)
        VAL = sbuf("VAL", (128, DC, TB), BF16)
        QRY = sbuf("QRY", (128, DC, TB), BF16)
        PH = sbuf("PH", (128, DC, TB), F32)
        UI = sbuf("UI", (128, TB), I32)
        ABSR = sbuf("ABSR", (128, TB), F32)
        SIN = sbuf("SIN", (128, DC, TB), BF16)
        COS = sbuf("COS", (128, DC, TB), BF16)
        BR = sbuf("BR", (128, DC, TB), BF16)
        BI = sbuf("BI", (128, DC, TB), BF16)
        MR = sbuf("MR", (128, DC, TB), BF16)
        MI = sbuf("MI", (128, DC, TB), BF16)
        RET = sbuf("RET", (128, DC, TB), BF16)
        RRS = sbuf("RRS", (128, DC, TB), BF16)
        STL = sbuf("STL", (128, DC, TB), BF16)
        tens = {"KEY": KEY, "VAL": VAL, "QRY": QRY, "PH": PH, "SIN": SIN, "COS": COS,
                "BR": BR, "BI": BI, "MR": MR, "MI": MI, "RET": RET, "RRS": RRS,
                "STL": STL}
        bufR = {nm: [Res() for _ in range(DC)] for nm in tens}
        alias = {"TKP": "PH", "TQP": "PH", "SINK": "SIN", "COSK": "COS",
                 "SINQ": "SIN", "COSQ": "COS", "T1": "BR", "SQA": "BI", "X1M": "RET"}
        for a, b in alias.items():
            tens[a] = tens[b]
            bufR[a] = bufR[b]
        t.update({nm: tens[nm] for nm in tens})
        T1 = BR
        SQA = BI
        X1M = RET
        uiR = Res()
        absrR = Res()

        def load_w(widx, slot, extra=()):
            deps = wbufR[slot].wr_deps() + list(extra)
            tk = dma("scalar" if widx % 2 == 0 else "gpsimd",
                     wbuf[slot][:], t["wcacheA"][widx], waits=deps)
            wbufR[slot].set_w(tk)
            return [tk]

        for tb in range(NTB):
            sl = tb % 2
            xload = []
            for tt in range(4):
                ts_ = tt % 2
                tld = dma("sync", xinf[:, ts_, :], t["x"][(tb * 4 + tt) * 128:(tb * 4 + tt + 1) * 128, :],
                          waits=xinfR[ts_].wr_deps())
                xinfR[ts_].set_w(tld)
                tcv = vop("tensor_copy", waits=[tld] + xinbR[ts_].wr_deps(),
                          out=xinb[:, ts_, :], in_=xinf[:, ts_, :])
                xinfR[ts_].add_r(tcv)
                xinbR[ts_].set_w(tcv)
                ttr = xpose_dma(xT[:, sl, :, tt * 128:(tt + 1) * 128], xinb[:, ts_, :],
                                waits=[tcv] + (xTR[sl].wr_deps() if tt == 0 else []))
                xinbR[ts_].add_r(ttr)
                xload.append(ttr)
            xTR[sl].set_w(xload[-1])
            for tk in xload[:-1]:
                xTR[sl].add_r(tk)
            xTc = xT[:, sl]

            for pi, (pn, dstn, bidx, scl) in enumerate(
                    (("k", "KEY", BK, None), ("v", "VAL", BV32, 1.0 / 32.0), ("q", "QRY", BQ, None))):
                slot = pi % 2
                wld = load_w(pi, slot, extra=[prepA_tok[pi]] if tb == 0 else [])
                dst = t[dstn]
                for dc in range(DC):
                    bk_i = dc % 2
                    w0 = wld + [xTR[sl].w] + bankR[bk_i].wr_deps() + biasA_rdy[bidx]
                    mt = None
                    for ci in range(DC):
                        mt = mm(bank(bk_i), wbuf[slot][:, ci, dc * 128:(dc + 1) * 128],
                                xTc[:, ci, :], start=(ci == 0), stop=(ci == DC - 1),
                                waits=w0 if ci == 0 else [], tok=(ci == DC - 1))
                    bankR[bk_i].set_w(mt)
                    xTR[sl].add_r(mt)
                    wbufR[slot].add_r(mt)
                    te = act(dst[:, dc, :], bank(bk_i), AF.Identity, bias=bcol(bidx, dc),
                             scale=scl, waits=[mt] + bufR[dstn][dc].wr_deps())
                    bankR[bk_i].add_r(te)
                    bufR[dstn][dc].set_w(te)

            def phase_chain(srcn, bidx, seq):
                slot = seq % 2
                _wi = 3 if srcn == "KEY" else 4
                wld = load_w(_wi, slot, extra=[prepA_tok[_wi]] if tb == 0 else [])
                src = tens[srcn]
                for dc in range(DC):
                    bk_i = 2 + dc % 2
                    w0 = (wld + [bufR[srcn][c].w for c in range(DC)]
                          + bankR[bk_i].wr_deps() + biasA_rdy[bidx] + bufR["PH"][dc].wr_deps())
                    mt = None
                    for ci in range(DC):
                        mt = mm(bank(bk_i), wbuf[slot][:, ci, dc * 128:(dc + 1) * 128],
                                src[:, ci, :], start=(ci == 0), stop=(ci == DC - 1),
                                waits=w0 if ci == 0 else [], tok=(ci == DC - 1))
                    bankR[bk_i].set_w(mt)
                    wbufR[slot].add_r(mt)
                    for c in range(DC):
                        bufR[srcn][c].add_r(mt)
                    tt_ = act(PH[:, dc, :], bank(bk_i), AF.Tanh, bias=bcol(bidx, dc),
                              waits=[mt] + bufR["PH"][dc].wr_deps())
                    bankR[bk_i].add_r(tt_)
                    bufR["PH"][dc].set_w(tt_)
                for dc in range(DC):
                    tg1 = vop("tensor_scalar", eng="gpsimd",
                              waits=[bufR["PH"][dc].w] + biasA_rdy[PSC],
                              out=PH[:, dc, :], in0=PH[:, dc, :],
                              scalar1=bcol(PSC, dc), scalar2=0.0, op0=ALU.mult, op1=ALU.add)
                    bufR["PH"][dc].set_w(tg1)
                    tg2 = vop("tensor_scalar", waits=[tg1] + uiR.wr_deps(),
                              out=UI[:], in0=PH[:, dc, :],
                              scalar1=INV2PI, scalar2=None, op0=ALU.mult)
                    uiR.set_w(tg2)
                    bufR["PH"][dc].add_r(tg2)
                    tg3 = vop("scalar_tensor_tensor", waits=[tg2],
                              out=PH[:, dc, :], in0=UI[:], scalar=-TWOPI,
                              in1=PH[:, dc, :], op0=ALU.mult, op1=ALU.add)
                    uiR.add_r(tg3)
                    bufR["PH"][dc].set_w(tg3)
                    tsin = act(SIN[:, dc, :], PH[:, dc, :], AF.Sin,
                               waits=[tg3] + bufR["SIN"][dc].wr_deps())
                    bufR["SIN"][dc].set_w(tsin)
                    bufR["PH"][dc].add_r(tsin)
                    tabs = vop("scalar_tensor_tensor", waits=[tg3] + absrR.wr_deps(),
                               out=ABSR[:], in0=PH[:, dc, :], scalar=-1.0,
                               in1=PH[:, dc, :], op0=ALU.mult, op1=ALU.max)
                    absrR.set_w(tabs)
                    bufR["PH"][dc].add_r(tabs)
                    tcos = act(COS[:, dc, :], ABSR[:], AF.Sin, scale=-1.0,
                               bias=HALFPI_AP, waits=[tabs] + CONST + bufR["COS"][dc].wr_deps())
                    absrR.add_r(tcos)
                    bufR["COS"][dc].set_w(tcos)

            phase_chain("KEY", BKP, 3)
            if dbg and tb == 0:
                dbg_dump("SINK", SIN[:], [bufR["SIN"][c].w for c in range(DC)], bufR["SIN"])
                dbg_dump("COSK", COS[:], [bufR["COS"][c].w for c in range(DC)], bufR["COS"])
                dbg_dump("RK", PH[:], [bufR["PH"][c].w for c in range(DC)], bufR["PH"])
            for dc in range(DC):
                tbr = vop("tensor_tensor",
                          waits=[bufR["COS"][dc].w, bufR["VAL"][dc].w] + bufR["BR"][dc].wr_deps(),
                          out=BR[:, dc, :], in0=VAL[:, dc, :], in1=COS[:, dc, :], op=ALU.mult)
                bufR["BR"][dc].set_w(tbr)
                bufR["COS"][dc].add_r(tbr)
                bufR["VAL"][dc].add_r(tbr)
                tbi = vop("tensor_tensor",
                          waits=[bufR["SIN"][dc].w, bufR["VAL"][dc].w] + bufR["BI"][dc].wr_deps(),
                          out=BI[:, dc, :], in0=VAL[:, dc, :], in1=SIN[:, dc, :], op=ALU.mult)
                bufR["BI"][dc].set_w(tbi)
                bufR["SIN"][dc].add_r(tbi)
                bufR["VAL"][dc].add_r(tbi)
                tmr = vop("tensor_tensor_scan", waits=[tbr] + CONST + bufR["MR"][dc].wr_deps(),
                          out=MR[:, dc, :], data0=scanmask[:], data1=BR[:, dc, :],
                          initial=0.0, op0=ALU.mult, op1=ALU.add)
                bufR["MR"][dc].set_w(tmr)
                bufR["BR"][dc].add_r(tmr)
                tmi = vop("tensor_tensor_scan", waits=[tbi] + CONST + bufR["MI"][dc].wr_deps(),
                          out=MI[:, dc, :], data0=scanmask[:], data1=BI[:, dc, :],
                          initial=0.0, op0=ALU.mult, op1=ALU.add)
                bufR["MI"][dc].set_w(tmi)
                bufR["BI"][dc].add_r(tmi)

            if dbg and tb == 0:
                dbg_dump("KEY", KEY[:], [bufR["KEY"][c].w for c in range(DC)], bufR["KEY"])
                dbg_dump("VAL", VAL[:], [bufR["VAL"][c].w for c in range(DC)], bufR["VAL"])
                dbg_dump("MRD", MR[:], [bufR["MR"][c].w for c in range(DC)], bufR["MR"])
            phase_chain("QRY", BQP, 4)
            for dc in range(DC):
                tt1 = vop("tensor_tensor",
                          waits=[bufR["MR"][dc].w, bufR["COS"][dc].w] + bufR["BR"][dc].wr_deps(),
                          out=T1[:, dc, :], in0=MR[:, dc, :], in1=COS[:, dc, :], op=ALU.mult)
                bufR["BR"][dc].set_w(tt1)
                bufR["MR"][dc].add_r(tt1)
                bufR["COS"][dc].add_r(tt1)
                tt2 = vop("tensor_tensor", eng="gpsimd",
                          waits=[bufR["MI"][dc].w, bufR["SIN"][dc].w],
                          out=MI[:, dc, :], in0=MI[:, dc, :], in1=SIN[:, dc, :], op=ALU.mult)
                bufR["MI"][dc].set_w(tt2)
                bufR["SIN"][dc].add_r(tt2)
                tre = vop("tensor_tensor", waits=[tt1, tt2] + bufR["RET"][dc].wr_deps(),
                          out=RET[:, dc, :], in0=T1[:, dc, :], in1=MI[:, dc, :], op=ALU.add)
                bufR["RET"][dc].set_w(tre)
                bufR["BR"][dc].add_r(tre)
                bufR["MI"][dc].add_r(tre)
                tsq = vop("tensor_tensor", eng="gpsimd", waits=[tre] + bufR["BI"][dc].wr_deps(),
                          out=SQA[:, dc, :], in0=RET[:, dc, :], in1=RET[:, dc, :], op=ALU.mult)
                bufR["BI"][dc].set_w(tsq)
                bufR["RET"][dc].add_r(tsq)

            if dbg and tb == 0:
                dbg_dump("RET", RET[:], [bufR["RET"][c].w for c in range(DC)], bufR["RET"])
            m1 = m2 = None
            for dc in range(DC):
                m1 = mm(brow(4), onescol[:], RET[:, dc, :], start=(dc == 0), stop=(dc == DC - 1),
                        waits=[bufR["RET"][dc].w] + CONST + (bankR[4].wr_deps() if dc == 0 else []),
                        tok=(dc == DC - 1))
                m2 = mm(brow(5), onescol[:], SQA[:, dc, :], start=(dc == 0), stop=(dc == DC - 1),
                        waits=[bufR["SQA"][dc].w] + (bankR[5].wr_deps() if dc == 0 else []),
                        tok=(dc == DC - 1))
            bankR[4].set_w(m1)
            bankR[5].set_w(m2)
            for dc in range(DC):
                bufR["RET"][dc].add_r(m1)
                bufR["SQA"][dc].add_r(m2)
            trb, tmr_ = ln_rows(4, 5, [m1, m2], True)
            tbc = mm(bank(6), ones1x128[:], rowbf[:, 0:TB], start=True, stop=True,
                     waits=[trb] + CONST + bankR[6].wr_deps())
            bankR[6].set_w(tbc)
            rowbfR.add_r(tbc)
            tbc7 = mm(bank(7), ones1x128[:], rowbf[:, TB:2 * TB], start=True, stop=True,
                      waits=[tmr_] + CONST + bankR[7].wr_deps())
            bankR[7].set_w(tbc7)
            rowbfR.add_r(tbc7)
            for dc in range(DC):
                trr = vop("tensor_tensor", waits=[tbc, bufR["RET"][dc].w] + bufR["RRS"][dc].wr_deps(),
                          out=RRS[:, dc, :], in0=RET[:, dc, :], in1=bank(6), op=ALU.mult)
                bufR["RET"][dc].add_r(trr)
                bankR[6].add_r(trr)
                trr = vop("tensor_tensor", waits=[tbc7, trr],
                          out=RRS[:, dc, :], in0=RRS[:, dc, :], in1=bank(7), op=ALU.subtract)
                bufR["RRS"][dc].set_w(trr)
                bankR[7].add_r(trr)

            slot = 5 % 2
            wld = load_w(5, slot, extra=[tbt] + ([prepA_tok[5]] if tb == 0 else []))
            for dc in range(DC):
                bk_i = dc % 2
                w0 = wld + [bufR["RRS"][c].w for c in range(DC)] + bankR[bk_i].wr_deps()
                for ci in range(DC):
                    mm(bank(bk_i), wbuf[slot][:, ci, dc * 128:(dc + 1) * 128],
                       RRS[:, ci, :], start=(ci == 0), stop=False,
                       waits=w0 if ci == 0 else [], tok=False)
                tbm = mm(bank(bk_i), btrow[:, dc * 128:(dc + 1) * 128], ones1x512[:],
                         start=False, stop=True, waits=[tbt] + CONST)
                bankR[bk_i].set_w(tbm)
                wbufR[slot].add_r(tbm)
                for c in range(DC):
                    bufR["RRS"][c].add_r(tbm)
                tst = vop("scalar_tensor_tensor", waits=[tbm, xTR[sl].w] + bufR["STL"][dc].wr_deps(),
                          out=STL[:, dc, :], in0=xTc[:, dc, :], scalar=2.0,
                          in1=bank(bk_i), op0=ALU.mult, op1=ALU.add)
                bankR[bk_i].add_r(tst)
                xTR[sl].add_r(tst)
                bufR["STL"][dc].set_w(tst)

            if dbg and tb == 0:
                dbg_dump("STL", STL[:], [bufR["STL"][c].w for c in range(DC)], bufR["STL"])
            for dc in range(DC):
                tsq = vop("tensor_tensor", eng="gpsimd",
                          waits=[bufR["STL"][dc].w] + bufR["SQA"][dc].wr_deps(),
                          out=SQA[:, dc, :], in0=STL[:, dc, :], in1=STL[:, dc, :], op=ALU.mult)
                bufR["SQA"][dc].set_w(tsq)
                bufR["STL"][dc].add_r(tsq)
            m1 = m2 = None
            for dc in range(DC):
                m1 = mm(brow(4), onescol[:], STL[:, dc, :], start=(dc == 0), stop=(dc == DC - 1),
                        waits=[bufR["STL"][dc].w] + (bankR[4].wr_deps() if dc == 0 else []),
                        tok=(dc == DC - 1))
                m2 = mm(brow(5), onescol[:], SQA[:, dc, :], start=(dc == 0), stop=(dc == DC - 1),
                        waits=[bufR["SQA"][dc].w] + (bankR[5].wr_deps() if dc == 0 else []),
                        tok=(dc == DC - 1))
            bankR[4].set_w(m1)
            bankR[5].set_w(m2)
            for dc in range(DC):
                bufR["STL"][dc].add_r(m1)
                bufR["SQA"][dc].add_r(m2)
            trb0, _ = ln_rows(4, 5, [m1, m2], False)
            tbc0 = mm(bank(6), ones1x128[:], rowbf[:, 0:TB], start=True, stop=True,
                      waits=[trb0] + bankR[6].wr_deps())
            bankR[6].set_w(tbc0)
            rowbfR.add_r(tbc0)
            x1m_toks = []
            for dc in range(DC):
                tx1 = vop("tensor_tensor", waits=[tbc0, bufR["STL"][dc].w] + bufR["X1M"][dc].wr_deps(),
                          out=X1M[:, dc, :], in0=STL[:, dc, :], in1=bank(6), op=ALU.mult)
                bufR["X1M"][dc].set_w(tx1)
                bufR["STL"][dc].add_r(tx1)
                bankR[6].add_r(tx1)
                x1m_toks.append(tx1)
            td = dma("gpsimd", t["x1cache"][tb][:, :], X1M[:, :, :], waits=x1m_toks)
            for dc in range(DC):
                bufR["X1M"][dc].add_r(td)
            final_toks.append(td)



    if do_b:
        if do_a:
            phase_barrier()
        sbuf = pool_sbuf
        _pool_reset()
        ln1w = sbuf("ln1w", (128, DC), F32)
        ln1b_bf = sbuf("ln1b_bf", (128, DC), BF16)
        ln1b_f = sbuf("ln1b_f", (128, DC), F32)
        tkw1 = chunkload(ln1w, t["ln1_w"])
        tkb1 = chunkload(ln1b_f, t["ln1_b"])
        tlb1 = vop("tensor_copy", waits=tkb1, out=ln1b_bf[:], in_=ln1b_f[:])

        gws = sbuf("gws", (128, DC, EA), BF16)
        kws = sbuf("kws", (128, DC, HD), BF16)
        vws = sbuf("vws", (128, DC, HD), BF16)
        obs = sbuf("obs", (8, D), BF16)
        relT = sbuf("relT", (128, 132), BF16)
        bgate = sbuf("bgate", (1, EA), BF16)
        vbias_row = sbuf("vbias_row", (1, HD), BF16)
        kqbias = sbuf("kqbias", (128, 2 + EA), F32)
        wstB = sbuf("wstB", (128, 2, 512), F32)
        wcvB = sbuf("wcvB", (128, 2, 512), BF16)
        rowB = sbuf("rowB", (1, 2 * HD), F32)
        wstR = Res()
        wcvR = Res()
        rowBR = Res()

        bias_mm = []
        for ci in range(DC):
            sl_ = ci % 2
            tl = dma("scalar", wstB[:, sl_, 0:EA], t["attn_gate_w"][ci * 128:(ci + 1) * 128, :],
                     waits=wstR.wr_deps())
            wstR.set_w(tl)
            tc = vop("tensor_copy", eng="gpsimd", waits=[tl] + wcvR.wr_deps(), out=wcvB[:, sl_, 0:EA], in_=wstB[:, sl_, 0:EA])
            wstR.add_r(tc)
            wcvR.set_w(tc)
            m1 = mm(PS[0:1, 3584:3584 + EA], ln1b_bf[:, ci:ci + 1], wcvB[:, sl_, 0:EA],
                    start=(ci == 0), stop=(ci == DC - 1), waits=[tc, tlb1] + (bankR[7].wr_deps() if ci == 0 else []))
            bias_mm.append(m1)
            tg = vop("tensor_scalar", eng="gpsimd", waits=[m1] + tkw1, out=gws[:, ci, :], in0=wcvB[:, sl_, 0:EA],
                     scalar1=ln1w[:, ci:ci + 1], scalar2=None, op0=ALU.mult)
            wcvR.add_r(tg)
        bankR[7].set_w(bias_mm[-1])
        tbg = vop("tensor_copy", waits=[bias_mm[-1]], out=bgate[:], in_=PS[0:1, 3584:3584 + EA])
        bankR[7].add_r(tbg)
        tbg = vfence(tbg)

        kvbias_toks = []
        for wi, (wn, bn, dstw) in enumerate((("k_w", "k_b", kws), ("v_w", "v_b", vws))):
            bias_mm = []
            for ci in range(DC):
                sl_ = ci % 2
                tl = dma("scalar", wstB[:, sl_, 0:HD], t[wn][ci * 128:(ci + 1) * 128, :],
                         waits=wstR.wr_deps())
                wstR.set_w(tl)
                tc = vop("tensor_copy", eng="gpsimd", waits=[tl] + wcvR.wr_deps(), out=wcvB[:, sl_, 0:HD], in_=wstB[:, sl_, 0:HD])
                wstR.add_r(tc)
                wcvR.set_w(tc)
                m1 = mm(PS[0:1, 3584:3584 + HD], ln1b_bf[:, ci:ci + 1], wcvB[:, sl_, 0:HD],
                        start=(ci == 0), stop=(ci == DC - 1), waits=[tc, tlb1] + (bankR[7].wr_deps() if ci == 0 else []))
                bias_mm.append(m1)
                tg = vop("tensor_scalar", eng="gpsimd", waits=[m1] + tkw1, out=dstw[:, ci, :], in0=wcvB[:, sl_, 0:HD],
                         scalar1=ln1w[:, ci:ci + 1], scalar2=None, op0=ALU.mult)
                wcvR.add_r(tg)
            bankR[7].set_w(bias_mm[-1])
            tbl = dma("sync", rowB[:, 0:HD], t[bn].unsqueeze(0), waits=rowBR.wr_deps())
            tba = vop("tensor_tensor", waits=[tbl, bias_mm[-1]], out=rowB[:, HD:2 * HD],
                      in0=PS[0:1, 3584:3584 + HD], in1=rowB[:, 0:HD], op=ALU.add)
            rowBR.set_w(tba)
            bankR[7].add_r(tba)
            if wn == "v_w":
                tvb = vop("tensor_copy", waits=[tba], out=vbias_row[:], in_=rowB[:, HD:2 * HD])
                tvb = vfence(tvb)
                rowBR.add_r(tvb)
                kvbias_toks.append(tvb)
            else:
                tba = vfence(tba)
                tst = dma("gpsimd", t["rowbounceB"][0:HD], rowB[0:1, HD:2 * HD], waits=[tba])
                rowBR.add_r(tst)
                tld = dma("gpsimd", kqbias[:, 0:1], t["rowbounceB"][0:HD], waits=[tst])
                kvbias_toks.append(tld)

        wqprep_tok = [None] * EA
        for e in range(EA):
            bias_mm = []
            for ci in range(DC):
                sl_ = ci % 2
                tl = dma("scalar", wstB[:, sl_, 0:HD], t["q_w"][e, ci * 128:(ci + 1) * 128, :],
                         waits=wstR.wr_deps())
                wstR.set_w(tl)
                tc = vop("tensor_copy", eng="gpsimd", waits=[tl] + wcvR.wr_deps(), out=wcvB[:, sl_, 0:HD], in_=wstB[:, sl_, 0:HD])
                wstR.add_r(tc)
                wcvR.set_w(tc)
                m1 = mm(PS[0:1, 3584:3584 + HD], ln1b_bf[:, ci:ci + 1], wcvB[:, sl_, 0:HD],
                        start=(ci == 0), stop=(ci == DC - 1), waits=[tc, tlb1] + (bankR[7].wr_deps() if ci == 0 else []))
                bias_mm.append(m1)
                tg = vop("tensor_scalar", eng="gpsimd", waits=[m1] + tkw1, out=wcvB[:, sl_, 0:HD], in0=wcvB[:, sl_, 0:HD],
                         scalar1=ln1w[:, ci:ci + 1], scalar2=HD ** -0.5, op0=ALU.mult, op1=ALU.mult)
                wcvR.set_w(tg)
                tsv = dma("sync", t["wq_c"][e][:, ci * HD:(ci + 1) * HD], wcvB[:, sl_, 0:HD], waits=[tg])
                wcvR.add_r(tsv)
                wqprep_tok[e] = tsv
            bankR[7].set_w(bias_mm[-1])
            tbl = dma("sync", rowB[:, 0:HD], t["q_b"][e].unsqueeze(0), waits=rowBR.wr_deps())
            tba = vop("scalar_tensor_tensor", waits=[tbl, bias_mm[-1]], out=rowB[:, HD:2 * HD],
                      in0=PS[0:1, 3584:3584 + HD], scalar=1.0, in1=rowB[:, 0:HD],
                      op0=ALU.mult, op1=ALU.add)
            rowBR.set_w(tba)
            bankR[7].add_r(tba)
            tsc = vop("tensor_scalar", waits=[tba], out=rowB[:, HD:2 * HD], in0=rowB[:, HD:2 * HD],
                      scalar1=HD ** -0.5, scalar2=None, op0=ALU.mult)
            tsc = vfence(tsc)
            rowBR.set_w(tsc)
            tst = dma("gpsimd", t["rowbounceB"][0:HD], rowB[0:1, HD:2 * HD], waits=[tsc])
            rowBR.add_r(tst)
            tld = dma("gpsimd", kqbias[:, 2 + e:3 + e], t["rowbounceB"][0:HD], waits=[tst])
            kvbias_toks.append(tld)

        ow_toks = []
        for e in range(EA):
            for hb in range(2):
                sl_ = hb % 2
                tl = dma("scalar", wstB[:, sl_, :], t["o_w"][e][:, hb * 512:(hb + 1) * 512],
                         waits=wstR.wr_deps())
                wstR.set_w(tl)
                tc = vop("tensor_copy", eng="gpsimd", waits=[tl] + wcvR.wr_deps(), out=wcvB[:, sl_, :], in_=wstB[:, sl_, :])
                wstR.add_r(tc)
                wcvR.set_w(tc)
                for dq in range(4):
                    dc_ = hb * 4 + dq
                    tsv = dma("sync", t["owc"][dc_][:, e * 128:(e + 1) * 128],
                              wcvB[:, sl_, dq * 128:(dq + 1) * 128], waits=[tc])
                    wcvR.add_r(tsv)
                    ow_toks.append(tsv)
        obstage = sbuf("obstage", (8, D), F32)
        tob_l = dma("scalar", obstage[:], t["o_b"][:, :])
        tob = vop("tensor_copy", waits=[tob_l], out=obs[:], in_=obstage[:])
        tob = vfence(tob)

        trl = dma("scalar", wstB[:, 0, 0:HD], t["rel_table"][0:128, :], waits=wstR.wr_deps())
        wstR.set_w(trl)
        ttr = k.op("tensor", lambda e: e.transpose(PS[:, 3584:3584 + 128], wstB[:, 0, 0:HD], ident[:]),
                   waits=[trl, tid] + bankR[7].wr_deps(), sem=sT)
        bankR[7].set_w(ttr)
        wstR.add_r(ttr)
        tre = vop("tensor_copy", waits=[ttr], out=relT[:, 0:128], in_=PS[:, 3584:3584 + 128])
        bankR[7].add_r(tre)
        trl2 = dma("scalar", rowB[:, 0:HD], t["rel_table"][128].unsqueeze(0), waits=rowBR.wr_deps())
        rowBR.set_w(trl2)
        ttr2 = k.op("tensor", lambda e: e.transpose(PS[:, 3584:3584 + 1], rowB[:, 0:HD], ident[0:1, 0:1]),
                    waits=[trl2, tid, tre] + bankR[7].wr_deps(), sem=sT)
        bankR[7].set_w(ttr2)
        rowBR.add_r(ttr2)
        tre2 = vop("tensor_copy", waits=[ttr2], out=relT[:, 128:129], in_=PS[:, 3584:3584 + 1])
        bankR[7].add_r(tre2)
        relT_rdy = [vfence(tre2)]
        prepB_done = [tbg, tob] + kvbias_toks + ow_toks + relT_rdy

        x1b = sbuf("x1b", (128, DC, TB), BF16)
        x1bR = [Res() for _ in range(DC)]
        XN = sbuf("XN", (128, DC, TB), BF16)
        xnR = [Res() for _ in range(DC)]
        SQB = XN
        sqbR = xnR
        kF = sbuf("kF", (128, NT), BF16)
        kFR = [Res() for _ in range(NTB)]
        vK = sbuf("vK", (128, 32, HD), BF16)
        vKR = [Res() for _ in range(32)]
        LGF = sbuf("LGF", (16, TB), F32)
        lgfR = Res()
        LGA = sbuf("LGA", (128, 4, 16), F32)
        lgaR = [Res() for _ in range(4)]
        GTS = sbuf("GTS", (128, 32, EA), F32)
        gtsR = [Res() for _ in range(32)]
        gstg = sbuf("gstg", (8, 128), BF16)
        gstgR = Res()
        gATb = sbuf("gATb", (8, TB), BF16)
        gATbR = Res()
        rowtb = sbuf("rowtb", (1, 2 * TB), BF16)
        rowtbR = Res()

        rows1_toks = {}

        def load_x1(tb, extra=()):
            deps = list(extra)
            for dc in range(DC):
                deps += x1bR[dc].wr_deps()
            tk_ = dma("gpsimd", x1b[:], t["x1cache"][tb], waits=deps)
            for dc in range(DC):
                x1bR[dc].set_w(tk_)
            return [tk_] * DC

        def ln1_and_xn(tb, phaseA_done):
            tx1 = load_x1(tb, extra=phaseA_done)
            for dc in range(DC):
                tsq = vop("tensor_tensor", eng="gpsimd", waits=[tx1[dc]] + sqbR[dc].wr_deps(),
                          out=SQB[:, dc, :], in0=x1b[:, dc, :], in1=x1b[:, dc, :], op=ALU.mult)
                sqbR[dc].set_w(tsq)
            m1 = m2 = None
            for dc in range(DC):
                m1 = mm(brow(4), onescol[:], x1b[:, dc, :], start=(dc == 0), stop=(dc == DC - 1),
                        waits=[tx1[dc]] + CONST + (bankR[4].wr_deps() if dc == 0 else []),
                        tok=(dc == DC - 1))
                m2 = mm(brow(5), onescol[:], SQB[:, dc, :], start=(dc == 0), stop=(dc == DC - 1),
                        waits=[sqbR[dc].w] + (bankR[5].wr_deps() if dc == 0 else []),
                        tok=(dc == DC - 1))
            bankR[4].set_w(m1)
            bankR[5].set_w(m2)
            for dc in range(DC):
                x1bR[dc].add_r(m1)
                sqbR[dc].add_r(m2)
            trb, tmr_ = ln_rows(4, 5, [m1, m2], True)
            tc1 = dma("gpsimd", t["rows1d"][0, tb * TB:(tb + 1) * TB], rowbf[0:1, 0:TB],
                      waits=[trb, tmr_])
            tc2 = dma("gpsimd", t["rows1d"][1, tb * TB:(tb + 1) * TB], rowbf[0:1, TB:2 * TB],
                      waits=[tmr_])
            rowbfR.add_r(tc1)
            rowbfR.add_r(tc2)
            rows1_toks[tb] = [tc1, tc2]
            return make_xn(tb, rowbf[:, 0:TB], rowbf[:, TB:2 * TB], [trb, tmr_], rowbfR)

        def make_xn(tb, rstd_ap, mu_ap, row_toks, row_res):
            tbc = mm(bank(6), ones1x128[:], rstd_ap, start=True, stop=True,
                     waits=list(row_toks) + CONST + bankR[6].wr_deps())
            bankR[6].set_w(tbc)
            row_res.add_r(tbc)
            tbc7 = mm(bank(7), ones1x128[:], mu_ap, start=True, stop=True,
                      waits=list(row_toks) + CONST + bankR[7].wr_deps())
            bankR[7].set_w(tbc7)
            row_res.add_r(tbc7)
            toks = []
            for dc in range(DC):
                t1_ = vop("tensor_tensor", waits=[tbc, x1bR[dc].w] + xnR[dc].wr_deps(),
                          out=XN[:, dc, :], in0=x1b[:, dc, :], in1=bank(6), op=ALU.mult)
                x1bR[dc].add_r(t1_)
                bankR[6].add_r(t1_)
                t2_ = vop("tensor_tensor", waits=[tbc7, t1_],
                          out=XN[:, dc, :], in0=XN[:, dc, :], in1=bank(7), op=ALU.subtract)
                bankR[7].add_r(t2_)
                xnR[dc].set_w(t2_)
                toks.append(t2_)
            return toks


        gat_toks = []
        for tb in range(NTB):
            xn_toks = ln1_and_xn(tb, final_toks[-1:] if do_a else [])
            mt = None
            for ci in range(DC):
                mt = mm(bank(0), kws[:, ci, :], XN[:, ci, :], start=(ci == 0), stop=(ci == DC - 1),
                        waits=(xn_toks + prepB_done + bankR[0].wr_deps()) if ci == 0 else [],
                        tok=(ci == DC - 1))
            bankR[0].set_w(mt)
            for dc in range(DC):
                xnR[dc].add_r(mt)
            te = act(kF[:, tb * TB:(tb + 1) * TB], bank(0), AF.Identity, bias=kqbias[:, 0:1],
                     waits=[mt] + kFR[tb].wr_deps())
            bankR[0].add_r(te)
            kFR[tb].set_w(te)
            for tt in range(4):
                mt = None
                for ci in range(DC):
                    mt = mm(PS[:, 512:512 + HD], XN[:, ci, tt * 128:(tt + 1) * 128], vws[:, ci, :],
                            start=(ci == 0), stop=False,
                            waits=(xn_toks + bankR[1].wr_deps()) if ci == 0 else [], tok=False)
                mb = mm(PS[:, 512:512 + HD], ones1x128[:], vbias_row[:],
                        start=False, stop=True, waits=CONST + prepB_done)
                bankR[1].set_w(mb)
                for dc in range(DC):
                    xnR[dc].add_r(mb)
                tev = vop("tensor_copy", waits=[mb] + vKR[tb * 4 + tt].wr_deps(),
                          out=vK[:, tb * 4 + tt, :], in_=PS[:, 512:512 + HD])
                bankR[1].add_r(tev)
                vKR[tb * 4 + tt].set_w(tev)
            mt = None
            for ci in range(DC):
                mt = mm(PS[0:8, 1024:1024 + TB], gws[:, ci, :], XN[:, ci, :],
                        start=(ci == 0), stop=False,
                        waits=(xn_toks + bankR[2].wr_deps()) if ci == 0 else [], tok=False)
            mb = mm(PS[0:8, 1024:1024 + TB], bgate[:], ones1x512[:], start=False, stop=True,
                    waits=CONST + prepB_done)
            bankR[2].set_w(mb)
            for dc in range(DC):
                xnR[dc].add_r(mb)
            tlg = vop("tensor_copy", waits=[mb] + lgfR.wr_deps(), out=LGF[0:8, 0:TB],
                      in_=PS[0:8, 1024:1024 + TB])
            bankR[2].add_r(tlg)
            lgfR.set_w(tlg)
            tla_toks = []
            for tt in range(4):
                tpt = k.op("tensor",
                           lambda e, o=PS[:, 1536:1536 + 8], i=LGF[0:8, tt * 128:(tt + 1) * 128]:
                           e.transpose(o, i, ident[0:8, 0:8]),
                           waits=[tlg, tid] + bankR[3].wr_deps(), sem=sT)
                bankR[3].set_w(tpt)
                lgfR.add_r(tpt)
                tla = vop("tensor_copy", waits=[tpt] + lgaR[tt].wr_deps(),
                          out=LGA[:, tt, 0:8], in_=PS[:, 1536:1536 + 8])
                bankR[3].add_r(tla)
                lgaR[tt].set_w(tla)
                tla_toks.append(tla)
            tg = gates_chain(LGA[:, :, 0:8], GTS[:, tb * 4:(tb + 1) * 4, :], EA, 4,
                             tla_toks, [gtsR[tb * 4 + i] for i in range(4)])
            for tt in range(4):
                tgt = k.op("tensor",
                           lambda e, o=PS[0:8, 1536 + 16:1536 + 16 + 128], i=GTS[:, tb * 4 + tt, :]:
                           e.transpose(o, i, ident[:]),
                           waits=[tg, tid] + bankR[3].wr_deps(), sem=sT)
                bankR[3].set_w(tgt)
                gtsR[tb * 4 + tt].add_r(tgt)
                tge = vop("tensor_copy", waits=[tgt] + gstgR.wr_deps(),
                          out=gstg[:], in_=PS[0:8, 1536 + 16:1536 + 16 + 128])
                bankR[3].add_r(tge)
                tge = vfence(tge)
                gstgR.set_w(tge)
                tgd = dma("gpsimd", t["gatc"][:, (tb * 4 + tt) * 128:(tb * 4 + tt + 1) * 128],
                          gstg[:], waits=[tge])
                gstgR.add_r(tgd)
                gat_toks.append(tgd)


        kF_all = kFR[NTB - 1].w
        vK_all = vKR[31].w
        QE = sbuf("QE", (128, EA, TB), BF16)
        qeR = [Res() for _ in range(EA)]
        _wq0 = sbuf("wqbuf0", (128, DC, HD), BF16)
        _wqr0 = Res()
        wqbuf = [_wq0, _wq0]
        wqR = [_wqr0, _wqr0]
        RELS = sbuf("RELS", (128, 4, 132), F32)
        relsR = [Res() for _ in range(4)]
        RB = sbuf("RB", (128, 2, 640), BF16)
        rbR = [Res(), Res()]
        BAND = sbuf("BAND", (128, 4, 384), BF16)
        bandR = [Res() for _ in range(4)]
        P1 = sbuf("P1", (128, 2, NT), BF16)
        p1R = [Res(), Res()]
        PT = sbuf("PT", (128, 32, EA, 128), BF16)
        ptR = Res()
        ZP = sbuf("ZP", (128, 2, 8), F32)
        zpR = [Res(), Res()]
        ZS = sbuf("ZS", (128, 2, 2), F32)
        zsR = [Res(), Res()]
        FV = sbuf("FV", (128, 2, EA), F32)
        fvR = [Res(), Res()]
        frows = sbuf("frows", (1, 1024), BF16)
        frowsR = Res()
        FBt = sbuf("FBt", (128, EA, 128), BF16)
        fbR = [Res() for _ in range(EA)]
        CXP = sbuf("CXP", (128, EA, TB), BF16)
        cxpR = [Res() for _ in range(EA)]
        owdc = sbuf("owdc", (128, 2, EA, 128), BF16)
        owdcR = [Res(), Res()]
        SOUT = sbuf("SOUT", (128, 2, TB), BF16)
        soutR = [Res(), Res()]
        X2O = SOUT
        x2oR = soutR
        rimgR = [Res() for _ in range(4)]

        sstore_toks = []
        regA = (0, 1)
        regB = (2, 3)

        for tb in range(NTB):
            tx1 = load_x1(tb)
            cxp_w_toks = []
            trl1 = dma("gpsimd", rowtb[0:1, 0:TB].bitcast(BF16) if False else rowtb[0:1, 0:TB],
                       t["rows1d"][0, tb * TB:(tb + 1) * TB],
                       waits=rowtbR.wr_deps() + rows1_toks[tb])
            trl2 = dma("gpsimd", rowtb[0:1, TB:2 * TB], t["rows1d"][1, tb * TB:(tb + 1) * TB],
                       waits=rows1_toks[tb])
            rowtbR.set_w(trl2)
            rowtbR.add_r(trl1)
            tgl = dma("gpsimd", gATb[:], t["gatc"][:, tb * TB:(tb + 1) * TB],
                      waits=gATbR.wr_deps() + gat_toks)
            gATbR.set_w(tgl)
            xn_toks = make_xn(tb, rowtb[:, 0:TB], rowtb[:, TB:2 * TB], [trl1, trl2], rowtbR)
            for e in range(EA):
                slot = e % 2
                deps = wqR[slot].wr_deps()
                tkq = dma("scalar", wqbuf[slot][:], t["wq_c"][e],
                          waits=deps + ([wqprep_tok[e]] + kvbias_toks if tb == 0 else []))
                wqR[slot].set_w(tkq)
                wq_toks = [tkq]
                bk_i = 2 + e % 2
                mt = None
                for ci in range(DC):
                    mt = mm(bank(bk_i), wqbuf[slot][:, ci, :], XN[:, ci, :],
                            start=(ci == 0), stop=(ci == DC - 1),
                            waits=(wq_toks + xn_toks + bankR[bk_i].wr_deps()) if ci == 0 else [],
                            tok=(ci == DC - 1))
                bankR[bk_i].set_w(mt)
                wqR[slot].add_r(mt)
                for dc in range(DC):
                    xnR[dc].add_r(mt)
                te = act(QE[:, e, :], bank(bk_i), AF.Identity, bias=kqbias[:, 2 + e:3 + e],
                         waits=[mt] + qeR[e].wr_deps())
                bankR[bk_i].add_r(te)
                qeR[e].set_w(sfence(te))

            def emit_fb_ctx(qprev, xb_toks):
                jj = tb * 4 + qprev
                fsl = qprev % 2
                ev_list = []
                for e in range(EA):
                    tft = k.op("tensor",
                               lambda e_, i=FV[:, fsl, e:e + 1]:
                               e_.transpose(PS[0:1, 3204:3204 + 128], i, ident[:]),
                               waits=[fvR[fsl].w, tid] + bankR[6].wr_deps(), sem=sT)
                    bankR[6].set_w(tft)
                    fvR[fsl].add_r(tft)
                    tfr = vop("tensor_copy", waits=[tft] + (frowsR.wr_deps() if e == 0 else []),
                              out=frows[0:1, e * 128:(e + 1) * 128],
                              in_=PS[0:1, 3204:3204 + 128])
                    bankR[6].add_r(tfr)
                    ev_list.append(tfr)
                tfd2 = k.op("vector", lambda e_: e_.drain(), waits=[ev_list[-1]], sem=sV)
                frowsR.set_w(tfd2)
                for tk_ in ev_list:
                    frowsR.add_r(tk_)
                for e in range(EA):
                    tfb = mm(PS[:, 3332:3332 + 128], ones1x128[:],
                             frows[0:1, e * 128:(e + 1) * 128],
                             start=True, stop=True, waits=[tfd2] + CONST + bankR[6].wr_deps())
                    bankR[6].set_w(tfb)
                    frowsR.add_r(tfb)
                    tfc = vop("tensor_copy", waits=[tfb] + fbR[e].wr_deps(),
                              out=FBt[:, e, :], in_=PS[:, 3332:3332 + 128])
                    bankR[6].add_r(tfc)
                    fbR[e].set_w(tfc)
                m6 = m7 = None
                for kk in range(32):
                    w0 = (xb_toks + [vK_all]
                          + bankR[6].wr_deps() + bankR[7].wr_deps()) if kk == 0 else []
                    m7 = mm(PS[:, 3584:4096], vK[:, kk, :], PT[:, kk, 0:4, :],
                            start=(kk == 0), stop=(kk == 31), waits=w0, tok=(kk == 31))
                    m6 = mm(PS[:, 3072:3584], vK[:, kk, :], PT[:, kk, 4:8, :],
                            start=(kk == 0), stop=(kk == 31), waits=[], tok=(kk == 31))
                bankR[7].set_w(m7)
                bankR[6].set_w(m6)
                ptR.add_r(m6)
                ptR.add_r(m7)
                for e in range(EA):
                    bnk = 7 if e < 4 else 6
                    coff = 3584 + (e % 4) * 128 if e < 4 else 3072 + (e % 4) * 128
                    tcx = vop("tensor_tensor",
                              waits=[m7 if e < 4 else m6, fbR[e].w]
                              + (cxpR[e].wr_deps() if qprev == 0 else []),
                              out=CXP[:, e, qprev * 128:(qprev + 1) * 128],
                              in0=PS[:, coff:coff + 128], in1=FBt[:, e, :], op=ALU.mult)
                    bankR[bnk].add_r(tcx)
                    fbR[e].add_r(tcx)
                    cxp_w_toks.append(tcx)
                    if qprev == 3:
                        cxpR[e].set_w(tcx)
                    else:
                        cxpR[e].add_r(tcx)

            prev_xb = None
            gunit = 0
            for qtl in range(4):
                j = tb * 4 + qtl
                j0 = max(0, 128 * (j - 1))
                j1 = min(NT, 128 * (j + 2))
                wwid = j1 - j0
                moff = 256 if j == 0 else 128
                band_toks = [None] * EA
                rels_toks = [None] * EA
                fsl = qtl % 2
                for e in range(EA):
                    sl_e = e % 4
                    trm = mm(PS[:, 3072:3072 + 132], QE[:, e, qtl * 128:(qtl + 1) * 128],
                             relT[:, 0:132], start=True, stop=True,
                             waits=[qeR[e].w] + relT_rdy + bankR[6].wr_deps())
                    bankR[6].set_w(trm)
                    qeR[e].add_r(trm)
                    tre_ = vop("tensor_copy", waits=[trm] + relsR[sl_e].wr_deps(),
                               out=RELS[:, sl_e, :], in_=PS[:, 3072:3072 + 132])
                    bankR[6].add_r(tre_)
                    relsR[sl_e].set_w(tre_)
                    rels_toks[e] = tre_
                    sl_r = e % 2
                    tr1 = vop("tensor_scalar", eng="gpsimd", waits=[tre_] + rbR[sl_r].wr_deps(),
                              out=RB[:, sl_r, 0:192], in0=RB[:, sl_r, 0:192], scalar1=0.0,
                              scalar2=RELS[:, sl_e, 0:1], op0=ALU.mult, op1=ALU.add)
                    tr2 = vop("tensor_copy", eng="gpsimd", waits=[tre_, tr1],
                              out=RB[:, sl_r, 192:321], in_=RELS[:, sl_e, 0:129])
                    tr3 = vop("tensor_scalar", eng="gpsimd", waits=[tre_, tr2],
                              out=RB[:, sl_r, 321:640], in0=RB[:, sl_r, 321:640], scalar1=0.0,
                              scalar2=RELS[:, sl_e, 128:129], op0=ALU.mult, op1=ALU.add)
                    relsR[sl_e].add_r(tr1)
                    relsR[sl_e].add_r(tr2)
                    relsR[sl_e].add_r(tr3)
                    rbR[sl_r].set_w(tr3)
                    par = e % 4
                    trs = dma("scalar", t["Rimg"][par], RB[:, sl_r, :],
                              waits=[tr3] + rimgR[par].wr_deps())
                    rbR[sl_r].add_r(trs)
                    rimgR[par].set_w(trs)
                    src = AP(tensor=t["Rimg"].tensor, offset=par * 128 * 640 + moff,
                             ap=[[639, 128], [1, wwid]])
                    tgt_ = dma("scalar", BAND[:, par, 0:wwid], src,
                               waits=[trs] + bandR[par].wr_deps())
                    rimgR[par].add_r(tgt_)
                    bandR[par].set_w(tgt_)
                    band_toks[e] = tgt_
                if prev_xb is not None:
                    emit_fb_ctx(qtl - 1, prev_xb)
                xbar_toks = []
                for e in range(EA):
                    sl_e = e % 4
                    par = e % 4
                    sl_p = e % 2
                    tre_ = rels_toks[e]
                    tgt_ = band_toks[e]
                    zidx = 0
                    exp_toks = []
                    zp_deps = zpR[sl_p].wr_deps()
                    for u in range(4):
                        reg = gunit % 3
                        gunit += 1
                        psoff = reg * 1024
                        b0, b1 = 2 * reg, 2 * reg + 1
                        rdep = bankR[b0].wr_deps() + bankR[b1].wr_deps()
                        mts = []
                        for half in range(2):
                            mt = mm(PS[:, psoff + half * 512: psoff + (half + 1) * 512],
                                    QE[:, e, qtl * 128:(qtl + 1) * 128],
                                    kF[:, u * 1024 + half * 512: u * 1024 + (half + 1) * 512],
                                    start=True, stop=True,
                                    waits=([qeR[e].w, kF_all] + rdep) if half == 0 else [],
                                    tok=True)
                            mts.append(mt)
                        qeR[e].add_r(mts[-1])
                        bankR[b0].set_w(mts[0])
                        bankR[b1].set_w(mts[1])
                        us, ue = 1024 * u, 1024 * (u + 1)
                        ba, bb = max(us, j0), min(ue, j1)
                        tband = None
                        if ba < bb:
                            tband = vop("tensor_tensor",
                                        waits=[tgt_, mts[0] if ba < us + 512 else mts[1],
                                               mts[1] if bb > us + 512 else mts[0]],
                                        out=PS[:, psoff + ba - us: psoff + bb - us],
                                        in0=PS[:, psoff + ba - us: psoff + bb - us],
                                        in1=BAND[:, par, ba - j0: bb - j0], op=ALU.add)
                            bandR[par].add_r(tband)
                            bankR[b0].set_w(tband)
                            bankR[b1].set_w(tband)
                        segs = []
                        if us < j0:
                            segs.append((us, min(ue, j0), RELS[:, sl_e, 0:1]))
                        if ba < bb:
                            segs.append((ba, bb, None))
                        if ue > j1:
                            segs.append((max(us, j1), ue, RELS[:, sl_e, 128:129]))
                        for (a, b, bias_ap) in segs:
                            dep = [mts[0]] if a < us + 512 else []
                            if b > us + 512:
                                dep.append(mts[1])
                            if tband is not None:
                                dep.append(tband)
                            tex = act(P1[:, sl_p, a:b], PS[:, psoff + a - us: psoff + b - us],
                                      AF.Exp, bias=bias_ap,
                                      accum=ZP[:, sl_p, zidx:zidx + 1],
                                      waits=dep + [tre_] + zp_deps
                                      + (p1R[sl_p].wr_deps() if (u == 0 and zidx == 0) else []))
                            relsR[sl_e].add_r(tex)
                            bankR[b0].add_r(tex)
                            bankR[b1].add_r(tex)
                            exp_toks.append(tex)
                            zidx += 1
                    tfe = sfence(exp_toks[-1])
                    p1R[sl_p].set_w(tfe)
                    for tk_ in exp_toks[:-1]:
                        p1R[sl_p].add_r(tk_)
                    tz = vop("tensor_reduce", waits=exp_toks + [tfe] + zsR[sl_p].wr_deps(),
                             out=ZS[:, sl_p, 0:1], in_=ZP[:, sl_p, 0:zidx],
                             axis=mybir.AxisListType.X, op=ALU.add)
                    zpR[sl_p].set_w(tz)
                    tiv = vop("reciprocal", waits=[tz], out=ZS[:, sl_p, 1:2], in_=ZS[:, sl_p, 0:1])
                    zsR[sl_p].set_w(tiv)
                    tf = vop("tensor_tensor",
                             waits=[tiv, gtsR[j].w] + (fvR[fsl].wr_deps() if e == 0 else []),
                             out=FV[:, fsl, e:e + 1], in0=GTS[:, j, e:e + 1], in1=ZS[:, sl_p, 1:2],
                             op=ALU.mult)
                    zsR[sl_p].add_r(tf)
                    gtsR[j].add_r(tf)
                    if e == EA - 1:
                        tfd = k.op("vector", lambda e_: e_.drain(), waits=[tf], sem=sV)
                        fvR[fsl].set_w(tfd)
                    txb = xpose_dma(PT[:, :, e, :], P1[:, sl_p, :],
                                    waits=[p1R[sl_p].w] + ([ptR.w] + ptR.readers if e == 0 else []))
                    p1R[sl_p].add_r(txb)
                    xbar_toks.append(txb)
                ptR.set_w(xbar_toks[-1])
                for tk_ in xbar_toks[:-1]:
                    ptR.add_r(tk_)
                prev_xb = xbar_toks
            emit_fb_ctx(3, prev_xb)

            for dc in range(DC):
                sl_o = dc % 2
                towd = dma("scalar", owdc[:, sl_o, :, :], t["owc"][dc],
                           waits=owdcR[sl_o].wr_deps() + (ow_toks if tb == 0 else []))
                owdcR[sl_o].set_w(towd)
                owt = [towd]
                bk_i = dc % 2
                w0 = ([cxpR[e].w for e in range(EA)] + cxp_w_toks
                      + bankR[bk_i].wr_deps() + [gATbR.w] + owt + [tob])
                mt = None
                for e in range(EA):
                    mt = mm(bank(bk_i), owdc[:, sl_o, e, :], CXP[:, e, :],
                            start=(e == 0), stop=False, waits=w0 if e == 0 else [], tok=False)
                mb = mm(bank(bk_i), obs[0:8, dc * 128:(dc + 1) * 128],
                        gATb[:], start=False, stop=True)
                bankR[bk_i].set_w(mb)
                gATbR.add_r(mb)
                owdcR[sl_o].add_r(mb)
                for e in range(EA):
                    cxpR[e].add_r(mb)
                ts1 = vop("scalar_tensor_tensor", waits=[mb, x1bR[dc].w] + soutR[dc % 2].wr_deps(),
                          out=SOUT[:, dc % 2, :], in0=x1b[:, dc, :], scalar=2.0,
                          in1=bank(bk_i), op0=ALU.mult, op1=ALU.add)
                soutR[dc % 2].set_w(ts1)
                bankR[bk_i].add_r(ts1)
                x1bR[dc].add_r(ts1)
                td1 = dma("gpsimd", t["scache"][tb][:, dc * TB:(dc + 1) * TB], SOUT[:, dc % 2, :], waits=[ts1])
                soutR[dc % 2].add_r(td1)
                sstore_toks.append(td1)
                ts2 = vop("tensor_tensor", waits=[mb, x1bR[dc].w] + x2oR[dc % 2].wr_deps(),
                          out=X2O[:, dc % 2, :], in0=x1b[:, dc, :], in1=bank(bk_i), op=ALU.add)
                x2oR[dc % 2].set_w(ts2)
                bankR[bk_i].add_r(ts2)
                x1bR[dc].add_r(ts2)
                td2 = dma("gpsimd", t["x2cache"][tb][:, dc * TB:(dc + 1) * TB], X2O[:, dc % 2, :], waits=[ts2])
                x2oR[dc % 2].add_r(td2)
                sstore_toks.append(td2)

            if dbg and tb == 0:
                dbg_dump("XN", XN[:], [xnR[c].w for c in range(DC)], xnR)
                dbg_dump("QE", QE[:], [qeR[e].w for e in range(EA)], qeR)
                dbg_dump("CXP", CXP[:], [cxpR[e].w for e in range(EA)], cxpR)
        if dbg:
            dbg_dump("KFD", kF[:], [kFR[i].w for i in range(NTB)], kFR)
            dbg_dump("VKD", vK[:], [vKR[i].w for i in range(32)], vKR)
            dbg_dump("GTSD", GTS[:], [gtsR[i].w for i in range(32)], gtsR)
        final_toks += sstore_toks


    if do_c:
        if do_b:
            phase_barrier()
        sbuf = pool_sbuf
        _pool_reset()
        ln2w = sbuf("ln2w", (128, DC), F32)
        ln2b_f = sbuf("ln2b_f", (128, DC), F32)
        tkw2 = chunkload(ln2w, t["ln2_w"])
        tkb2 = chunkload(ln2b_f, t["ln2_b"])
        ones1x128f = sbuf("ones1x128f", (1, 128), F32)
        tof = k.op("gpsimd", lambda e: e.memset(ones1x128f[:], 1.0), sem=sG)
        CONSTF = CONST + [tof]

        fgws = sbuf("fgws", (128, DC, EF), BF16)
        b1c = sbuf("b1c", (128, EF * 4), F32)
        fb2s = sbuf("fb2s", (16, D), BF16)
        wstC = sbuf("wstC", (128, 2, 1024), F32)
        wstCR = [Res(), Res()]

        prepC_toks = []
        for ci in range(DC):
            sl_ = ci % 2
            tl = dma("scalar", wstC[:, sl_, 0:EF], t["ff_gate_w"][ci * 128:(ci + 1) * 128, :],
                     waits=wstCR[sl_].wr_deps())
            wstCR[sl_].set_w(tl)
            tc = vop("tensor_copy", eng="gpsimd", waits=[tl], out=fgws[:, ci, :],
                     in_=wstC[:, sl_, 0:EF])
            wstCR[sl_].add_r(tc)
            prepC_toks.append(tc)
        prep_b1 = [None] * EF
        for f in range(EF):
            tld = None
            for hb in range(4):
                tld = dma("sync", b1c[:, f * 4 + hb: f * 4 + hb + 1],
                          t["ff_b1"][f][hb * 128:(hb + 1) * 128])
            prep_b1[f] = tld
        tfl = dma("scalar", wstC[0:16, 0, :], t["ff_b2"][:, :], waits=wstCR[0].wr_deps())
        wstCR[0].set_w(tfl)
        tfb2 = vop("tensor_copy", waits=[tfl], out=fb2s[:], in_=wstC[0:16, 0, :])
        wstCR[0].add_r(tfb2)
        tfb2 = vfence(tfb2)
        prep_w1 = [None] * EF
        prep_w2 = [None] * EF

        X2B = sbuf("X2B", (128, DC, TB), BF16)
        x2bR = [Res() for _ in range(DC)]
        SB = sbuf("SB", (128, DC, TB), BF16)
        sbR = [Res() for _ in range(DC)]
        XN2 = sbuf("XN2", (128, DC, TB), BF16)
        xn2R = [Res() for _ in range(DC)]
        SQC = XN2
        sqcR = xn2R
        LGF2 = sbuf("LGF2", (16, TB), F32)
        lgf2R = Res()
        LGA2 = sbuf("LGA2", (128, 4, 16), F32)
        lga2R = [Res() for _ in range(4)]
        GTSF = sbuf("GTSF", (128, 4, 16), F32)
        gtsfR = [Res() for _ in range(4)]
        gATf = sbuf("gATf", (16, TB), BF16)
        gatfR = Res()
        gstg2 = sbuf("gstg2", (16, 128), BF16)
        gstg2R = Res()
        grow = sbuf("grow", (1, EF, TB), BF16)
        growR = Res()
        wm1 = [sbuf("wm10", (128, DC, FH), BF16), sbuf("wm11", (128, DC, FH), BF16)]
        wm1R = [Res(), Res()]
        wm2 = [sbuf("wm20", (128, 4, D), BF16), sbuf("wm21", (128, 4, D), BF16)]
        wm2R = [Res(), Res()]
        HST = sbuf("HST", (128, 2, TB), BF16)
        hstR = [Res(), Res()]
        HP = sbuf("HP", (128, 8, 4, TB), BF16)
        hpR = [[Res() for _ in range(4)] for _ in range(8)]
        OFF = sbuf("OFF", (128, DC, TB), F32)
        offR = [Res() for _ in range(DC)]
        OSTG = sbuf("OSTG", (128, 1, D), F32)
        ostgR = [Res()]

        def load_cache(name, dst, dstR, tb, extra=()):
            deps = list(extra)
            for dc in range(DC):
                deps += dstR[dc].wr_deps()
            tk_ = dma("gpsimd", dst[:], t[name][tb], waits=deps)
            for dc in range(DC):
                dstR[dc].set_w(tk_)
            return [tk_] * DC

        out_toks = []
        for tb in range(NTB):
            bwait = [final_toks[-1]] if (do_b and tb == 0) else []
            tx2 = load_cache("x2cache", X2B, x2bR, tb, extra=bwait)
            tsb = load_cache("scache", SB, sbR, tb, extra=bwait)
            for dc in range(DC):
                tsq = vop("tensor_tensor", eng="gpsimd", waits=[tx2[dc]] + sqcR[dc].wr_deps(),
                          out=SQC[:, dc, :], in0=X2B[:, dc, :], in1=X2B[:, dc, :], op=ALU.mult)
                sqcR[dc].set_w(tsq)
            m1 = m2 = None
            for dc in range(DC):
                m1 = mm(brow(4), onescol[:], X2B[:, dc, :], start=(dc == 0), stop=(dc == DC - 1),
                        waits=[tx2[dc]] + CONST + (bankR[4].wr_deps() if dc == 0 else []),
                        tok=(dc == DC - 1))
                m2 = mm(brow(5), onescol[:], SQC[:, dc, :], start=(dc == 0), stop=(dc == DC - 1),
                        waits=[sqcR[dc].w] + (bankR[5].wr_deps() if dc == 0 else []),
                        tok=(dc == DC - 1))
            bankR[4].set_w(m1)
            bankR[5].set_w(m2)
            for dc in range(DC):
                x2bR[dc].add_r(m1)
                sqcR[dc].add_r(m2)
            if dbg and tb == 0:
                dbg_dump("X2BE", X2B[:], [m1, m2], x2bR)
            trb, tmr_ = ln_rows(4, 5, [m1, m2], True)
            tbc = mm(bank(6), ones1x128[:], rowbf[:, 0:TB], start=True, stop=True,
                     waits=[trb] + CONST + bankR[6].wr_deps())
            bankR[6].set_w(tbc)
            rowbfR.add_r(tbc)
            tbc7 = mm(bank(7), ones1x128[:], rowbf[:, TB:2 * TB], start=True, stop=True,
                      waits=[tmr_] + CONST + bankR[7].wr_deps())
            bankR[7].set_w(tbc7)
            rowbfR.add_r(tbc7)
            xn2_toks = []
            for dc in range(DC):
                t1_ = vop("tensor_tensor", waits=[tbc, x2bR[dc].w, sqcR[dc].readers[-1] if sqcR[dc].readers else None, m2],
                          out=XN2[:, dc, :], in0=X2B[:, dc, :], in1=bank(6), op=ALU.mult)
                x2bR[dc].add_r(t1_)
                bankR[6].add_r(t1_)
                t2_ = vop("tensor_tensor", waits=[tbc7, t1_],
                          out=XN2[:, dc, :], in0=XN2[:, dc, :], in1=bank(7), op=ALU.subtract)
                bankR[7].add_r(t2_)
                t3_ = vop("tensor_scalar", eng="gpsimd", waits=[t2_] + tkw2 + tkb2,
                          out=XN2[:, dc, :], in0=XN2[:, dc, :],
                          scalar1=ln2w[:, dc:dc + 1], scalar2=ln2b_f[:, dc:dc + 1],
                          op0=ALU.mult, op1=ALU.add)
                xn2R[dc].set_w(t3_)
                xn2_toks.append(t3_)
            mt = None
            for ci in range(DC):
                mt = mm(PS[0:16, 1024:1024 + TB], fgws[:, ci, :], XN2[:, ci, :],
                        start=(ci == 0), stop=(ci == DC - 1),
                        waits=(xn2_toks + prepC_toks + bankR[2].wr_deps()) if ci == 0 else [],
                        tok=(ci == DC - 1))
            bankR[2].set_w(mt)
            for dc in range(DC):
                xn2R[dc].add_r(mt)
            tlg = vop("tensor_copy", waits=[mt] + lgf2R.wr_deps(), out=LGF2[:],
                      in_=PS[0:16, 1024:1024 + TB])
            bankR[2].add_r(tlg)
            lgf2R.set_w(tlg)
            tla_toks = []
            for tt in range(4):
                tpt = k.op("tensor",
                           lambda e, o=PS[:, 1536:1536 + 16], i=LGF2[0:16, tt * 128:(tt + 1) * 128]:
                           e.transpose(o, i, ident[0:16, 0:16]),
                           waits=[tlg, tid] + bankR[3].wr_deps(), sem=sT)
                bankR[3].set_w(tpt)
                lgf2R.add_r(tpt)
                tla = vop("tensor_copy", waits=[tpt] + lga2R[tt].wr_deps(),
                          out=LGA2[:, tt, :], in_=PS[:, 1536:1536 + 16])
                bankR[3].add_r(tla)
                lga2R[tt].set_w(tla)
                tla_toks.append(tla)
            tg = gates_chain(LGA2[:, :, :], GTSF[:, :, :], EF, 4, tla_toks,
                             [gtsfR[i] for i in range(4)])
            for tt in range(4):
                tgt = k.op("tensor",
                           lambda e, o=PS[0:16, 1536 + 16:1536 + 16 + 128], i=GTSF[:, tt, :]:
                           e.transpose(o, i, ident[:]),
                           waits=[tg, tid] + bankR[3].wr_deps(), sem=sT)
                bankR[3].set_w(tgt)
                gtsfR[tt].add_r(tgt)
                tge = vop("tensor_copy", waits=[tgt] + (gatfR.wr_deps() if tt == 0 else []),
                          out=gATf[0:16, tt * 128:(tt + 1) * 128],
                          in_=PS[0:16, 1536 + 16:1536 + 16 + 128])
                bankR[3].add_r(tge)
                if tt == 3:
                    gatfR.set_w(vfence(tge))
            tgd = dma("gpsimd", t["gatfd"], gATf[:], waits=[gatfR.w])
            gatfR.add_r(tgd)
            tgr16 = dma("sync", grow[:], t["gatfd"][:, :].unsqueeze(0),
                        waits=[tgd] + growR.wr_deps())
            growR.set_w(tgr16)
            if dbg and tb == 0:
                dbg_dump("XN2", XN2[:], [xn2R[c].w for c in range(DC)], xn2R)
                dbg_dump("GATF", gATf[:], [gatfR.w], [gatfR])
                dbg_dump("X2B", X2B[:], [x2bR[c].w for c in range(DC)], x2bR)
                dbg_dump("ROWC", rowbf[:], [rowbfR.w], [rowbfR])

            for ph in range(2):
                hp_toks = []
                for f8 in range(8):
                    f = ph * 8 + f8
                    slot = f % 2
                    deps = wm1R[slot].wr_deps()
                    if tb == 0:
                        w1toks = []
                        st_tok = None
                        for ci in range(DC):
                            sl_ = ci % 2
                            tl = dma("scalar", wstC[:, sl_, 0:FH],
                                     t["ff_w1"][f, ci * 128:(ci + 1) * 128, :],
                                     waits=wstCR[sl_].wr_deps())
                            wstCR[sl_].set_w(tl)
                            tc = vop("tensor_copy", eng="gpsimd",
                                     waits=[tl] + (deps if ci == 0 else []),
                                     out=wm1[slot][:, ci, :], in_=wstC[:, sl_, 0:FH])
                            wstCR[sl_].add_r(tc)
                            w1toks.append(tc)
                            st_tok = dma("sync", t["w1_c"][f][:, ci * FH:(ci + 1) * FH],
                                         wm1[slot][:, ci, :], waits=[tc])
                        wm1R[slot].set_w(w1toks[-1])
                        for tk_ in w1toks[:-1]:
                            wm1R[slot].add_r(tk_)
                        wm1R[slot].add_r(st_tok)
                        prep_w1[f] = st_tok
                    else:
                        tkw = dma("scalar", wm1[slot][:], t["w1_c"][f],
                                  waits=deps + ([prep_w1[f]] if tb == 1 else []))
                        wm1R[slot].set_w(tkw)
                        w1toks = [tkw]
                    tgb = mm(bank(5), ones1x128[:], grow[:, f, :], start=True, stop=True,
                             waits=[growR.w] + CONST + bankR[5].wr_deps())
                    bankR[5].set_w(tgb)
                    growR.add_r(tgb)
                    for hb in range(4):
                        bk_i = 2 + hb % 2
                        mt = None
                        for ci in range(DC):
                            mt = mm(bank(bk_i), wm1[slot][:, ci, hb * 128:(hb + 1) * 128],
                                    XN2[:, ci, :], start=(ci == 0), stop=(ci == DC - 1),
                                    waits=(w1toks + xn2_toks + bankR[bk_i].wr_deps()) if ci == 0 else [],
                                    tok=(ci == DC - 1))
                        bankR[bk_i].set_w(mt)
                        wm1R[slot].add_r(mt)
                        for dc in range(DC):
                            xn2R[dc].add_r(mt)
                        th = act(HST[:, hb % 2, :], bank(bk_i), AF.Relu,
                                 bias=b1c[:, f * 4 + hb:f * 4 + hb + 1],
                                 waits=[mt] + ([prep_b1[f]] if tb == 0 else []) + hstR[hb % 2].wr_deps())
                        bankR[bk_i].add_r(th)
                        hstR[hb % 2].set_w(th)
                        tp = vop("tensor_tensor", waits=[th, tgb] + hpR[f8][hb].wr_deps(),
                                 out=HP[:, f8, hb, :], in0=HST[:, hb % 2, :], in1=bank(5),
                                 op=ALU.mult)
                        hstR[hb % 2].add_r(tp)
                        bankR[5].add_r(tp)
                        hpR[f8][hb].set_w(tp)
                        hp_toks.append(tp)
                lastm = [None] * 8
                for f8 in range(8):
                    f = ph * 8 + f8
                    slot = f % 2
                    deps = wm2R[slot].wr_deps()
                    if tb == 0:
                        w2toks = []
                        st_tok = None
                        for hc in range(4):
                            sl_ = hc % 2
                            tl = dma("scalar", wstC[:, sl_, :],
                                     t["ff_w2"][f, hc * 128:(hc + 1) * 128, :],
                                     waits=wstCR[sl_].wr_deps())
                            wstCR[sl_].set_w(tl)
                            tc = vop("tensor_copy", eng="gpsimd",
                                     waits=[tl] + (deps if hc == 0 else []),
                                     out=wm2[slot][:, hc, :], in_=wstC[:, sl_, :])
                            wstCR[sl_].add_r(tc)
                            w2toks.append(tc)
                            st_tok = dma("sync", t["w2_c"][f][:, hc * D:(hc + 1) * D],
                                         wm2[slot][:, hc, :], waits=[tc])
                        wm2R[slot].set_w(w2toks[-1])
                        for tk_ in w2toks[:-1]:
                            wm2R[slot].add_r(tk_)
                        wm2R[slot].add_r(st_tok)
                        prep_w2[f] = st_tok
                    else:
                        tkw = dma("scalar", wm2[slot][:], t["w2_c"][f],
                                  waits=deps + ([prep_w2[f]] if tb == 1 else []))
                        wm2R[slot].set_w(tkw)
                        w2toks = [tkw]
                    for dc in range(DC):
                        first = (f8 == 0)
                        w0 = []
                        if first:
                            w0 = hp_toks + bankR[dc].wr_deps()
                        if dc == 0:
                            w0 = w0 + w2toks
                        mt = None
                        for hc in range(4):
                            last_mm = (f8 == 7 and hc == 3 and ph == 1 and False)
                            mt = mm(bank(dc), wm2[slot][:, hc, dc * 128:(dc + 1) * 128],
                                    HP[:, f8, hc, :],
                                    start=(f8 == 0 and hc == 0), stop=False,
                                    waits=w0 if hc == 0 else [], tok=(hc == 3))
                        lastm[dc] = mt
                        wm2R[slot].add_r(mt)
                        for hc in range(4):
                            hpR[f8][hc].add_r(mt)
                for dc in range(DC):
                    if ph == 1:
                        mfin = mm(bank(dc), fb2s[0:16, dc * 128:(dc + 1) * 128], gATf[:],
                                  start=False, stop=True, waits=[lastm[dc], gatfR.w, tfb2])
                        gatfR.add_r(mfin)
                    else:
                        mfin = lastm[dc]
                    bankR[dc].set_w(mfin)
                    tse = vop("tensor_tensor", waits=[mfin, sbR[dc].w, tsb[dc]],
                              out=SB[:, dc, :], in0=SB[:, dc, :], in1=bank(dc), op=ALU.add)
                    bankR[dc].add_r(tse)
                    sbR[dc].set_w(tse)

            for dc in range(DC):
                tsq = vop("tensor_tensor", eng="gpsimd", waits=[sbR[dc].w] + sqcR[dc].wr_deps(),
                          out=SQC[:, dc, :], in0=SB[:, dc, :], in1=SB[:, dc, :], op=ALU.mult)
                sqcR[dc].set_w(tsq)
                sbR[dc].add_r(tsq)
            m1 = m2 = None
            for dc in range(DC):
                m1 = mm(brow(4), onescol[:], SB[:, dc, :], start=(dc == 0), stop=(dc == DC - 1),
                        waits=[sbR[dc].w] + (bankR[4].wr_deps() if dc == 0 else []),
                        tok=(dc == DC - 1))
                m2 = mm(brow(5), onescol[:], SQC[:, dc, :], start=(dc == 0), stop=(dc == DC - 1),
                        waits=[sqcR[dc].w] + (bankR[5].wr_deps() if dc == 0 else []),
                        tok=(dc == DC - 1))
            bankR[4].set_w(m1)
            bankR[5].set_w(m2)
            for dc in range(DC):
                sbR[dc].add_r(m1)
                sqcR[dc].add_r(m2)
            trb, tmr_ = ln_rows(4, 5, [m1, m2], True)
            tbc = mm(bank(6), ones1x128f[:], rowsA[:, TB:2 * TB], start=True, stop=True,
                     waits=[trb, tmr_] + CONSTF + bankR[6].wr_deps())
            bankR[6].set_w(tbc)
            rowsR.add_r(tbc)
            tbc7 = mm(bank(7), ones1x128f[:], rowsA[:, 2 * TB:3 * TB], start=True, stop=True,
                      waits=[tmr_] + CONSTF + bankR[7].wr_deps())
            bankR[7].set_w(tbc7)
            rowsR.add_r(tbc7)
            off_toks = []
            for dc in range(DC):
                t1_ = vop("tensor_tensor", waits=[tbc, sbR[dc].w] + offR[dc].wr_deps(),
                          out=OFF[:, dc, :], in0=SB[:, dc, :], in1=bank(6), op=ALU.mult)
                sbR[dc].add_r(t1_)
                bankR[6].add_r(t1_)
                t2_ = vop("tensor_tensor", waits=[tbc7, t1_],
                          out=OFF[:, dc, :], in0=OFF[:, dc, :], in1=bank(7), op=ALU.subtract)
                bankR[7].add_r(t2_)
                offR[dc].set_w(t2_)
                off_toks.append(t2_)
            for tt in range(4):
                sl = 0
                ev_toks = []
                for dc in range(DC):
                    bnk = dc % 2
                    tptr = k.op("tensor",
                                lambda e, o=PS[:, bnk * 512:bnk * 512 + 128],
                                i=OFF[:, dc, tt * 128:(tt + 1) * 128]:
                                e.transpose(o, i, ident[:]),
                                waits=[off_toks[dc], tid] + bankR[bnk].wr_deps(), sem=sT)
                    bankR[bnk].set_w(tptr)
                    offR[dc].add_r(tptr)
                    tev = vop("tensor_copy", waits=[tptr] + (ostgR[sl].wr_deps() if dc == 0 else []),
                              out=OSTG[:, sl, dc * 128:(dc + 1) * 128],
                              in_=PS[:, bnk * 512:bnk * 512 + 128])
                    bankR[bnk].add_r(tev)
                    ev_toks.append(tev)
                ostgR[sl].set_w(ev_toks[-1])
                for tk_ in ev_toks[:-1]:
                    ostgR[sl].add_r(tk_)
                tdo = dma("sync", t["out"][(tb * 4 + tt) * 128:(tb * 4 + tt + 1) * 128, :],
                          OSTG[:, sl, :], waits=ev_toks)
                ostgR[sl].add_r(tdo)
                out_toks.append(tdo)
        final_toks += out_toks

    k.ops["gpsimd"].append((lambda e: None, [tk for tk in final_toks], None, 0))

    k.finalize()
    return nc, t


def _ins_a(inputs, core):
    return {
        "x": np.ascontiguousarray(inputs["L0_attn_x"][core]),
        "Wk": inputs["L0_attn_Wk"], "bk": inputs["L0_attn_bk"],
        "Wv": inputs["L0_attn_Wv"], "bv": inputs["L0_attn_bv"],
        "Wq": inputs["L0_attn_Wq"], "bq": inputs["L0_attn_bq"],
        "Wkp": inputs["L0_attn_Wkp"], "bkp": inputs["L0_attn_bkp"],
        "Wqp": inputs["L0_attn_Wqp"], "bqp": inputs["L0_attn_bqp"],
        "phase_scale": inputs["L0_attn_phase_scale"],
        "ln_g": inputs["L0_attn_ln_g"], "ln_b": inputs["L0_attn_ln_b"],
        "Wo": inputs["L0_attn_Wo"], "bo": inputs["L0_attn_bo"],
    }


def _ins_b(inputs):
    return {
        "attn_gate_w": inputs["L0_ffn_attn_gate_w"],
        "q_w": inputs["L0_ffn_q_w"], "q_b": inputs["L0_ffn_q_b"],
        "k_w": inputs["L0_ffn_k_w"], "k_b": inputs["L0_ffn_k_b"],
        "v_w": inputs["L0_ffn_v_w"], "v_b": inputs["L0_ffn_v_b"],
        "o_w": inputs["L0_ffn_o_w"], "o_b": inputs["L0_ffn_o_b"],
        "rel_table": inputs["L0_ffn_rel_table"],
        "ln1_w": inputs["L0_ffn_ln1_w"], "ln1_b": inputs["L0_ffn_ln1_b"],
    }


def _ins_c(inputs):
    return {
        "ff_gate_w": inputs["L0_ffn_ff_gate_w"],
        "ff_w1": inputs["L0_ffn_ff_w1"], "ff_b1": inputs["L0_ffn_ff_b1"],
        "ff_w2": inputs["L0_ffn_ff_w2"], "ff_b2": inputs["L0_ffn_ff_b2"],
        "ln2_w": inputs["L0_ffn_ln2_w"], "ln2_b": inputs["L0_ffn_ln2_b"],
    }


def kernel(**inputs):
    cores = list(range(8))
    nc, _ = build_program("ALL")
    im = []
    for c in cores:
        m = _ins_a(inputs, c)
        m.update(_ins_b(inputs))
        m.update(_ins_c(inputs))
        im.append(m)
    res = run_bass_kernel_spmd(nc, im, core_ids=cores)
    return np.stack([res.results[c]["out"] for c in cores])
